# version 02
# baseline (speedup 1.0000x reference)
import os
import contextlib
import numpy as np
import concourse.bass as bass
import concourse.mybir as mybir
from concourse.bass_utils import run_bass_kernel_spmd

F32 = mybir.dt.float32
BF16 = mybir.dt.bfloat16
I32 = mybir.dt.int32
AF = mybir.ActivationFunctionType
ALU = mybir.AluOpType
AX = mybir.AxisListType

TOK = 1024
DC = 16
D = 2048
HID = 5632
HC = 44
NH = 16
NCH = 8
QSCALE = float(1.0 / np.sqrt(2048.0))

DMA_SEMS = {"dg", "dy", "dw0", "dw1", "dw2", "dw3", "ky0", "ky1", "vy0", "vy1", "st"}
ENG_SEMS = {"t", "v", "s", "g", "cc"}

SEMS = {}


class Plan:
    ENGINES = ("tensor", "vector", "scalar", "gpsimd", "sync")

    def __init__(self):
        self.streams = {e: [] for e in self.ENGINES}
        self.semv = {}
        self.last_wait = {}

    def v(self, s):
        return self.semv.get(s, 0)

    def inc(self, s):
        by = 16 if s in DMA_SEMS else 1
        self.semv[s] = self.semv.get(s, 0) + by
        return self.semv[s]

    def emit(self, eng, fn):
        self.streams[eng].append(fn)

    def wait(self, eng, s, val=None):
        val = self.semv.get(s, 0) if val is None else val
        if val <= 0 or val <= self.last_wait.get((eng, s), 0):
            return
        self.last_wait[(eng, s)] = val
        self.emit(eng, lambda E, s=s, val=val: E.wait_ge(SEMS[s], val))

    def op(self, eng, build, inc=None):
        if inc is not None:
            self.inc(inc)
        self.emit(eng, lambda E, b=build, i=inc:
                  (b(E).then_inc(SEMS[i], 16 if i in DMA_SEMS else 1) if i else b(E)))


def build_program(upto=4):
    nc = bass.Bass("TRN2", target_bir_lowering=False,
                   detect_race_conditions=False)
    P = Plan()

    x_in = nc.dram_tensor("x_in", [D, TOK], F32, kind="ExternalInput").ap()
    x_out = nc.dram_tensor("x_out", [D, TOK], F32, kind="ExternalOutput").ap()
    Wq_a = nc.dram_tensor("Wq_a", [D, D], F32, kind="ExternalInput").ap()
    Wk_a = nc.dram_tensor("Wk_a", [D, D], F32, kind="ExternalInput").ap()
    Wv_a = nc.dram_tensor("Wv_a", [D, D], F32, kind="ExternalInput").ap()
    bq_a = nc.dram_tensor("bq_a", [D], F32, kind="ExternalInput").ap()
    bk_a = nc.dram_tensor("bk_a", [D], F32, kind="ExternalInput").ap()
    bv_a = nc.dram_tensor("bv_a", [D], F32, kind="ExternalInput").ap()
    Wal = nc.dram_tensor("Wal", [D, NH], F32, kind="ExternalInput").ap()
    Wbe = nc.dram_tensor("Wbe", [D, NH], F32, kind="ExternalInput").ap()
    bal = nc.dram_tensor("bal", [NH], F32, kind="ExternalInput").ap()
    bbe = nc.dram_tensor("bbe", [NH], F32, kind="ExternalInput").ap()
    Wq_b = nc.dram_tensor("Wq_b", [D, D], F32, kind="ExternalInput").ap()
    Wk_b = nc.dram_tensor("Wk_b", [D, D], F32, kind="ExternalInput").ap()
    Wv_b = nc.dram_tensor("Wv_b", [D, D], F32, kind="ExternalInput").ap()
    Wo_b = nc.dram_tensor("Wo_b", [D, D], F32, kind="ExternalInput").ap()
    WgT = nc.dram_tensor("WgT", [D, HID], I32, kind="ExternalInput").ap()
    WuT = nc.dram_tensor("WuT", [D, HID], I32, kind="ExternalInput").ap()
    WdT = nc.dram_tensor("WdT", [HID, D], I32, kind="ExternalInput").ap()
    sg_in = nc.dram_tensor("sg_in", [HID], F32, kind="ExternalInput").ap()
    su_in = nc.dram_tensor("su_in", [HID], F32, kind="ExternalInput").ap()
    sd_in = nc.dram_tensor("sd_in", [D], F32, kind="ExternalInput").ap()
    gsel_in = nc.dram_tensor("gsel_in", [NH, 4], F32, kind="ExternalInput").ap()

    ssend_d = nc.dram_tensor("ssend_d", [128, 2049], F32)
    sgath_d = nc.dram_tensor("sgath_d", [4, 128, 2049], F32)
    kown_d = nc.dram_tensor("kown_d", [D, TOK], BF16)
    vown_d = nc.dram_tensor("vown_d", [TOK, D], BF16)
    kfull_d = nc.dram_tensor("kfull_d", [4, D, TOK], BF16)
    vfull_d = nc.dram_tensor("vfull_d", [4, TOK, D], BF16)
    mid_d = nc.dram_tensor("mid_d", [HC, 128, TOK], BF16)
    tap_names = [t for t in os.environ.get("MK_TAP", "").split(",") if t]
    tap_d = {t: nc.dram_tensor(f"tap_{i}", [128, 16384], F32, kind="ExternalOutput")
             for i, t in enumerate(tap_names)}

    def tap(name, ap):
        if name not in tap_d:
            return
        n = min(ap.shape[-1] if ap.ndim == 2 else 0, 16384)
        part = ap.shape[0]
        P.wait("gpsimd", "v"); P.wait("gpsimd", "t"); P.wait("gpsimd", "s")
        P.op("gpsimd", lambda E, ap=ap, n=n, part=part: E.dma_start(
            out=tap_d[name].ap()[0:part, 0:n], in_=ap[:, 0:n]), inc="dg")
        P.wait("gpsimd", "dg")

    es = contextlib.ExitStack()
    E = es.enter_context

    x_sb = E(nc.sbuf_tensor("x_sb", [128, DC * TOK], F32))
    ident = E(nc.sbuf_tensor("ident", [128, 128], BF16))
    ident32 = E(nc.sbuf_tensor("ident32", [128, 128], F32))
    maskneg = E(nc.sbuf_tensor("maskneg", [128, 128], F32))
    EH = E(nc.sbuf_tensor("EH", [16, NH * 128], BF16))
    ONEH = E(nc.sbuf_tensor("ONEH", [128, NH * 16], BF16))
    ones_r = E(nc.sbuf_tensor("ones_r", [1, 128], BF16))
    ones_rf = E(nc.sbuf_tensor("ones_rf", [1, 128], F32))
    ones_c = E(nc.sbuf_tensor("ones_c", [128, 1], BF16))
    neg128_r = E(nc.sbuf_tensor("neg128_r", [1, 128], BF16))
    dummy = E(nc.sbuf_tensor("dumt", [1, 64], F32))
    bq_sb = E(nc.sbuf_tensor("bq_sb", [128, DC], F32))
    bk_sb = E(nc.sbuf_tensor("bk_sb", [128, DC], F32))
    ba_sb = E(nc.sbuf_tensor("ba_sb", [16, 1], F32))
    bb_sb = E(nc.sbuf_tensor("bb_sb", [16, 1], F32))
    sg_sb = E(nc.sbuf_tensor("sg_sb", [128, HC], F32))
    su_sb = E(nc.sbuf_tensor("su_sb", [128, HC], F32))
    sd_sb = E(nc.sbuf_tensor("sd_sb", [128, DC], F32))
    gsel_sb = E(nc.sbuf_tensor("gsel_sb", [16, 4], F32))
    c1em6 = E(nc.sbuf_tensor("c1em6", [128, 1], F32))
    nc.const_aps.register(F32, 1e-6, c1em6[:])

    for s in sorted(DMA_SEMS | ENG_SEMS):
        SEMS[s] = E(nc.semaphore(s))

    def xs(dc, t0, t1):
        return x_sb[:, dc * TOK + t0: dc * TOK + t1]

    def barrier():
        for s in sorted(DMA_SEMS):
            P.wait("gpsimd", s)
        P.op("tensor", lambda E: E.ldweights(ident[:, 0:1]), inc="t")
        P.op("vector", lambda E: E.memset(dummy[:, 0:16], 0.0), inc="v")
        P.op("scalar", lambda E: E.activation(dummy[:, 16:32], dummy[:, 0:16], AF.Copy),
             inc="s")
        P.op("gpsimd", lambda E: E.memset(dummy[:, 32:48], 0.0), inc="g")
        for eng in P.ENGINES:
            for s in ("t", "v", "s", "g"):
                P.wait(eng, s)

    def vdrain():
        P.wait("vector", "v")

    def rsqrt_refined(dst, src, tmp):
        P.wait("scalar", "v")
        P.op("scalar", lambda E: E.activation(tmp, src, AF.Sqrt), inc="s")
        P.wait("vector", "s")
        P.op("vector", lambda E: E.reciprocal(dst, tmp), inc="v")
        vdrain()
        P.op("vector", lambda E: E.tensor_mul(tmp, src, dst), inc="v")
        vdrain()
        P.op("vector", lambda E: E.tensor_mul(tmp, tmp, dst), inc="v")
        vdrain()
        P.op("vector", lambda E: E.tensor_scalar(tmp, tmp, -0.5, 1.5, ALU.mult,
                                                 ALU.add), inc="v")
        vdrain()
        P.op("vector", lambda E: E.tensor_mul(dst, dst, tmp), inc="v")
        vdrain()

    wsem_state = {}

    def load_wchunk(buf, sem, src_ap, nkc, col0, ncols):
        lastuse = wsem_state.get(id(buf), 0)
        if lastuse:
            P.wait("gpsimd", "t", lastuse)
        src = src_ap.rearrange("(kc p) m -> kc p m", p=128)
        P.op("gpsimd",
             lambda E, b=buf, s=src, c0=col0, ncc=ncols, k=nkc:
             E.dma_start(out=b[:, 0:k * ncc].rearrange("p (kc m) -> p kc m", kc=k),
                         in_=s[:, :, c0:c0 + ncc].transpose([1, 0, 2])),
             inc=sem)

    def wchunk_done(buf, tval):
        wsem_state[id(buf)] = tval

    def load_wchunk_rows(buf, sem, src_ap, nkc, row_kc0, dc):
        lastuse = wsem_state.get(id(buf), 0)
        if lastuse:
            P.wait("gpsimd", "t", lastuse)
        src = src_ap.rearrange("(kc p) m -> kc p m", p=128)
        P.op("gpsimd",
             lambda E, b=buf, s=src, k0=row_kc0, k=nkc, dc=dc:
             E.dma_start(out=b[:, 0:k * 128].rearrange("p (kc m) -> p kc m", kc=k),
                         in_=s[k0:k0 + k, :, dc * 128:(dc + 1) * 128]
                         .transpose([1, 0, 2])),
             inc=sem)

    def emit_ln_stats(pb, xb, xq, st):
        mean_t, rst_t, scr2a, scr2b = st
        prev_t = 0
        for dc in range(DC):
            for tb in range(2):
                if prev_t:
                    P.wait("scalar", "t", prev_t)
                a, b = tb * 512, (tb + 1) * 512
                P.op("scalar", lambda E, dc=dc, a=a, b=b: E.activation(
                    xb[:], xs(dc, a, b), AF.Copy))
                P.op("scalar", lambda E, dc=dc, a=a, b=b: E.activation(
                    xq[:], xs(dc, a, b), AF.Square), inc="s")
                P.wait("tensor", "s")
                P.op("tensor", lambda E, dc=dc, tb=tb: E.matmul(
                    pb[tb][0:1, :], ones_c[:], xb[:],
                    start=(dc == 0), stop=(dc == DC - 1)))
                P.op("tensor", lambda E, dc=dc, tb=tb: E.matmul(
                    pb[2 + tb][0:1, :], ones_c[:], xq[:],
                    start=(dc == 0), stop=(dc == DC - 1)), inc="t")
                prev_t = P.v("t")
        P.wait("vector", "t")
        P.op("vector", lambda E: E.tensor_scalar_mul(mean_t[0:1, 0:512], pb[0][0:1, :],
                                                     1.0 / 2048.0))
        P.op("vector", lambda E: E.tensor_scalar_mul(mean_t[0:1, 512:1024],
                                                     pb[1][0:1, :], 1.0 / 2048.0),
             inc="v")
        vdrain()
        for hb in range(2):
            sl = hb * 512
            P.op("vector", lambda E, hb=hb, sl=sl: E.tensor_scalar_mul(
                scr2a[:], pb[2 + hb][0:1, :], 1.0 / 2048.0), inc="v")
            P.op("vector", lambda E, sl=sl: E.tensor_mul(
                rst_t[0:1, sl:sl + 512], mean_t[0:1, sl:sl + 512],
                mean_t[0:1, sl:sl + 512]), inc="v")
            vdrain()
            P.op("vector", lambda E, sl=sl: E.tensor_sub(
                scr2a[:], scr2a[:], rst_t[0:1, sl:sl + 512]), inc="v")
            vdrain()
            P.op("vector", lambda E: E.tensor_scalar_add(
                scr2a[:], scr2a[:], 1e-5), inc="v")
            vdrain()
            rsqrt_refined(rst_t[0:1, sl:sl + 512], scr2a[:], scr2b[:])

    def emit_ln_norm(pb, xn_sb, xoff, t0, t1, st):
        mean_t, rst_t, scr2a, scr2b = st
        P.wait("tensor", "v")
        ntb = (t1 - t0) // 512
        for tb in range(ntb):
            a = t0 + tb * 512
            P.op("tensor", lambda E, tb=tb, a=a: E.matmul(
                pb[tb][:, :], ones_rf[:], mean_t[0:1, a:a + 512],
                start=True, stop=True))
            P.op("tensor", lambda E, tb=tb, a=a: E.matmul(
                pb[2 + tb][:, :], ones_rf[:], rst_t[0:1, a:a + 512],
                start=True, stop=True), inc=("t" if tb == ntb - 1 else None))
        P.wait("vector", "t")
        for dc in range(DC):
            for tb in range(ntb):
                a = t0 + tb * 512
                o = xoff(dc, a)
                dst = xn_sb[:, o:o + 512]
                P.op("vector", lambda E, a=a, dst=dst, tb=tb, dc=dc: E.tensor_sub(
                    dst, xs(dc, a, a + 512), pb[tb][:, :]))
                P.op("vector", lambda E, dst=dst, tb=tb: E.tensor_mul(
                    dst, dst, pb[2 + tb][:, :]), inc="v")

    def emit_proj_fm(xn_sb, xn_tok, w_src, wbufs, wsems, ec_list, t0, t1, pb2, evac):
        P.wait("tensor", "v")
        ntb = (t1 - t0) // 512
        single = wbufs[0] is wbufs[1]
        for i, ec in enumerate(ec_list):
            buf, sem = wbufs[i % 2], wsems[i % 2]
            if i == 0 or single:
                load_wchunk(buf, sem, w_src, DC, ec * 128, 128)
            P.wait("tensor", sem)
            if not single and i + 1 < len(ec_list):
                load_wchunk(wbufs[(i + 1) % 2], wsems[(i + 1) % 2], w_src,
                            DC, ec_list[i + 1] * 128, 128)
            for tb in range(ntb):
                bank = pb2[(i * ntb + tb) % 2]
                for kc in range(DC):
                    P.op("tensor", lambda E, b=bank, buf=buf, kc=kc, tb=tb:
                         E.matmul(b[:, :], buf[:, kc * 128:(kc + 1) * 128],
                                  xn_sb[:, kc * xn_tok + t0 + tb * 512:
                                        kc * xn_tok + t0 + (tb + 1) * 512],
                                  start=(kc == 0), stop=(kc == DC - 1)),
                         inc=("t" if kc == DC - 1 else None))
                evac(ec, tb, bank)
            wchunk_done(buf, P.v("t"))

    def emit_proj_tm(xn_sb, xn_tok, w_src, wvb, t0, tcs, pbl, bias_pair, evac):
        P.wait("tensor", "v")
        src = w_src.rearrange("(kc p) m -> kc p m", p=128)
        bias_use = 0
        for eb in range(4):
            if bias_pair is not None:
                bias_row, bias_dram = bias_pair
                if bias_use:
                    P.wait("gpsimd", "t", bias_use)
                P.op("gpsimd", lambda E, eb=eb: E.dma_start(
                    out=bias_row[:], in_=bias_dram[eb * 512:(eb + 1) * 512]
                    .unsqueeze(0)), inc="dg")
            for kc in range(DC):
                i = eb * DC + kc
                buf, sem = wvb[i % 2], ("dw2", "dw3")[i % 2]
                lastuse = wsem_state.get(id(buf), 0)
                if lastuse:
                    P.wait("gpsimd", "t", lastuse)
                P.op("gpsimd", lambda E, b=buf, s=src, kc=kc, eb=eb: E.dma_start(
                    out=b[:], in_=s[kc][:, eb * 512:(eb + 1) * 512]), inc=sem)
                P.wait("tensor", sem)
                for ti, tc in enumerate(tcs):
                    stop_k = (kc == DC - 1) and (bias_pair is None)
                    P.op("tensor", lambda E, b=pbl[ti], buf=buf, kc=kc, tc=tc,
                         stop_k=stop_k:
                         E.matmul(b[:, :],
                                  xn_sb[:, kc * xn_tok + t0 + tc * 128:
                                        kc * xn_tok + t0 + (tc + 1) * 128],
                                  buf[:], start=(kc == 0), stop=stop_k),
                         inc=("t" if ti == len(tcs) - 1 else None))
                wchunk_done(buf, P.v("t"))
            if bias_pair is not None:
                P.wait("tensor", "dg")
            for ti, tc in enumerate(tcs):
                if bias_pair is not None:
                    P.op("tensor", lambda E, b=pbl[ti]: E.matmul(
                        b[:, :], ones_r[:], bias_pair[0][0:1, :],
                        start=False, stop=True), inc="t")
                evac(tc, eb, pbl[ti])
            bias_use = P.v("t")

    P.op("sync", lambda E: E.dma_start(
        out=x_sb[:].rearrange("p (dc t) -> p dc t", dc=DC),
        in_=x_in.rearrange("(dc p) t -> p dc t", p=128)), inc="dy")
    P.op("gpsimd", lambda E: E.dma_start(
        out=bq_sb[:], in_=bq_a.rearrange("(dc p) -> p dc", p=128), allow_slow_non_contiguous=True), inc="dg")
    P.op("gpsimd", lambda E: E.dma_start(
        out=bk_sb[:], in_=bk_a.rearrange("(dc p) -> p dc", p=128), allow_slow_non_contiguous=True), inc="dg")
    P.op("gpsimd", lambda E: E.dma_start(out=ba_sb[:], in_=bal.unsqueeze(1)), inc="dg")
    P.op("gpsimd", lambda E: E.dma_start(out=bb_sb[:], in_=bbe.unsqueeze(1)), inc="dg")
    P.op("gpsimd", lambda E: E.dma_start(
        out=sg_sb[:], in_=sg_in.rearrange("(hc p) -> p hc", p=128), allow_slow_non_contiguous=True), inc="dg")
    P.op("gpsimd", lambda E: E.dma_start(
        out=su_sb[:], in_=su_in.rearrange("(hc p) -> p hc", p=128), allow_slow_non_contiguous=True), inc="dg")
    P.op("gpsimd", lambda E: E.dma_start(
        out=sd_sb[:], in_=sd_in.rearrange("(dc p) -> p dc", p=128), allow_slow_non_contiguous=True), inc="dg")
    P.op("gpsimd", lambda E: E.dma_start(out=gsel_sb[:], in_=gsel_in), inc="dg")
    P.op("gpsimd", lambda E: E.memset(ident[:], 0.0))
    P.op("gpsimd", lambda E: E.affine_select(
        out=ident[:], in_=ident[:], compare_op=ALU.not_equal, fill=1.0,
        base=0, pattern=[[-1, 128]], channel_multiplier=1))
    P.op("gpsimd", lambda E: E.memset(ident32[:], 0.0))
    P.op("gpsimd", lambda E: E.affine_select(
        out=ident32[:], in_=ident32[:], compare_op=ALU.not_equal, fill=1.0,
        base=0, pattern=[[-1, 128]], channel_multiplier=1))
    P.op("gpsimd", lambda E: E.memset(maskneg[:], 0.0))
    P.op("gpsimd", lambda E: E.affine_select(
        out=maskneg[:], in_=maskneg[:], compare_op=ALU.is_ge, fill=-1e30,
        base=0, pattern=[[1, 128]], channel_multiplier=-1))
    P.op("gpsimd", lambda E: E.memset(EH[:], 1.0))
    for h in range(NH):
        P.op("gpsimd", lambda E, h=h: E.affine_select(
            out=EH[:, h * 128:(h + 1) * 128], in_=EH[:, h * 128:(h + 1) * 128],
            compare_op=ALU.is_ge, fill=0.0, base=-256 * h, pattern=[[1, 128]],
            channel_multiplier=256))
        P.op("gpsimd", lambda E, h=h: E.affine_select(
            out=EH[:, h * 128:(h + 1) * 128], in_=EH[:, h * 128:(h + 1) * 128],
            compare_op=ALU.is_ge, fill=0.0, base=255 + 256 * h, pattern=[[-1, 128]],
            channel_multiplier=-256))
    P.op("gpsimd", lambda E: E.memset(ONEH[:], 1.0))
    for h in range(NH):
        P.op("gpsimd", lambda E, h=h: E.affine_select(
            out=ONEH[:, h * 16:(h + 1) * 16], in_=ONEH[:, h * 16:(h + 1) * 16],
            compare_op=ALU.is_ge, fill=0.0, base=-256 * h, pattern=[[256, 16]],
            channel_multiplier=1))
        P.op("gpsimd", lambda E, h=h: E.affine_select(
            out=ONEH[:, h * 16:(h + 1) * 16], in_=ONEH[:, h * 16:(h + 1) * 16],
            compare_op=ALU.is_ge, fill=0.0, base=255 + 256 * h, pattern=[[-256, 16]],
            channel_multiplier=-1))
    P.op("gpsimd", lambda E: E.memset(ones_r[:], 1.0))
    P.op("gpsimd", lambda E: E.memset(ones_rf[:], 1.0))
    P.op("gpsimd", lambda E: E.memset(ones_c[:], 1.0))
    P.op("gpsimd", lambda E: E.memset(c1em6[:], 1e-6))
    P.op("gpsimd", lambda E: E.memset(dummy[:], 0.0))
    P.op("gpsimd", lambda E: E.memset(neg128_r[:], -128.0), inc="g")
    for e in ("tensor", "vector", "scalar"):
        P.wait(e, "g"); P.wait(e, "dg"); P.wait(e, "dy")

    def sublayer_attn_a():
        with contextlib.ExitStack() as ss:
            SE = ss.enter_context
            cla_hh = SE(nc.sbuf_tensor("cla_hh", [16, 512], F32))
            lbmc_hh = SE(nc.sbuf_tensor("lbmc_hh", [16, 512], F32))
            rn_hh = SE(nc.sbuf_tensor("rn_hh", [16, 512], F32))
            scr_hh = SE(nc.sbuf_tensor("scr_hh", [16, 512], F32))
            ends_h = SE(nc.sbuf_tensor("ends_h", [16, 16], F32))
            ecb_t = SE(nc.sbuf_tensor("ecb_t", [16, TOK], F32))
            eef = SE(nc.sbuf_tensor("eef", [16, 8], F32))
            EH32 = SE(nc.sbuf_tensor("EH32", [16, NH * 128], F32))
            UTRI = SE(nc.sbuf_tensor("UTRI", [128, 128], F32))
            atm = SE(nc.sbuf_tensor("atm", [128, NCH * 16], F32))
            btm = SE(nc.sbuf_tensor("btm", [128, NCH * 16], F32))
            latm = SE(nc.sbuf_tensor("latm", [128, NCH * 16], F32))
            cla_tm = SE(nc.sbuf_tensor("cla_tm", [128, NCH * 16], F32))
            w_tm = SE(nc.sbuf_tensor("w_tm", [128, NCH * 16], F32))
            etd_sb = SE(nc.sbuf_tensor("etd_sb", [128, 128], F32))
            ba_row = SE(nc.sbuf_tensor("ba_row", [1, 2 * NH], F32))
            S_send = SE(nc.sbuf_tensor("S_send", [128, 2049], F32))
            snap_bf = SE(nc.sbuf_tensor("snap_bf", [128, 128], BF16))
            expd_a = SE(nc.sbuf_tensor("expd_a", [128, 128], F32))
            expd_b = SE(nc.sbuf_tensor("expd_b", [128, 128], F32))
            attnf = SE(nc.sbuf_tensor("attnf", [128, 128], BF16))
            kw_sb = SE(nc.sbuf_tensor("kw_sb", [128, 128], BF16))
            qcd = SE(nc.sbuf_tensor("qcd", [128, 128], BF16))
            q_sb = SE(nc.sbuf_tensor("q_sb", [128, NH * TOK], BF16))
            pb = [SE(nc.psum_tensor(f"paa{i}", [128, 512], F32)) for i in range(6)]
            ptr = SE(nc.psum_tensor("ptra", [128, 512], BF16))

            P.op("gpsimd", lambda E: E.memset(EH32[:], 1.0))
            for h in range(NH):
                P.op("gpsimd", lambda E, h=h: E.affine_select(
                    out=EH32[:, h * 128:(h + 1) * 128],
                    in_=EH32[:, h * 128:(h + 1) * 128],
                    compare_op=ALU.is_ge, fill=0.0, base=-256 * h,
                    pattern=[[1, 128]], channel_multiplier=256))
                P.op("gpsimd", lambda E, h=h: E.affine_select(
                    out=EH32[:, h * 128:(h + 1) * 128],
                    in_=EH32[:, h * 128:(h + 1) * 128],
                    compare_op=ALU.is_ge, fill=0.0, base=255 + 256 * h,
                    pattern=[[-1, 128]], channel_multiplier=-256))
            P.op("gpsimd", lambda E: E.memset(UTRI[:], 1.0))
            P.op("gpsimd", lambda E: E.affine_select(
                out=UTRI[:], in_=UTRI[:], compare_op=ALU.is_ge, fill=0.0,
                base=0, pattern=[[1, 128]], channel_multiplier=-1))
            P.op("gpsimd", lambda E: E.memset(S_send[:], 0.0), inc="g")
            P.op("gpsimd", lambda E: E.dma_start(
                out=ba_row[:, 0:NH], in_=bal.unsqueeze(0)), inc="dg")
            P.op("gpsimd", lambda E: E.dma_start(
                out=ba_row[:, NH:], in_=bbe.unsqueeze(0)), inc="dg")
            P.wait("vector", "g")
            P.wait("tensor", "g")
            P.wait("tensor", "dg")

            st_a = tuple(SE(nc.sbuf_tensor(f"sta{i}", [1, sz], F32))
                         for i, sz in enumerate((TOK, TOK, 512, 512)))
            inner = contextlib.ExitStack()
            IE = inner.enter_context
            bv_row = IE(nc.sbuf_tensor("bv_row", [1, 512], BF16))
            xb_s = IE(nc.sbuf_tensor("a_xb", [128, 512], BF16))
            xq_s = IE(nc.sbuf_tensor("a_xq", [128, 512], BF16))
            ksq = xq_s

            emit_ln_stats(pb[0:4], xb_s, xq_s, st_a)

            wb0 = IE(nc.sbuf_tensor("wba0", [128, DC * 128], BF16))
            wb = [wb0, wb0]
            wvb = [IE(nc.sbuf_tensor(f"wva{i}", [128, 512], BF16)) for i in range(2)]
            wab = IE(nc.sbuf_tensor("wab", [128, 2 * DC * 16], BF16))
            wsems = ["dw0", "dw0"]
            P.op("gpsimd", lambda E: E.dma_start(
                out=wab[:, 0:DC * 16].rearrange("p (dc m) -> p dc m", dc=DC),
                in_=Wal.rearrange("(dc p) m -> p dc m", p=128)), inc="dg")
            P.op("gpsimd", lambda E: E.dma_start(
                out=wab[:, DC * 16:].rearrange("p (dc m) -> p dc m", dc=DC),
                in_=Wbe.rearrange("(dc p) m -> p dc m", p=128)), inc="dg")

            xn = IE(nc.sbuf_tensor("xn", [128, DC * 512], BF16))
            k_sb = IE(nc.sbuf_tensor("k_sb", [128, NH * 512], BF16))
            v_sb = IE(nc.sbuf_tensor("v_sb", [128, 4 * D], BF16))

            for half in range(2):
                t0, t1 = half * 512, (half + 1) * 512
                emit_ln_norm(pb[0:4], xn, lambda dc, a, t0=t0: dc * 512 + (a - t0),
                             t0, t1, st_a)

                if half == 0:
                    tap("xn0", xn[:])
                def evac_q(ec, tb, bank, t0=t0):
                    P.wait("scalar", "t")
                    P.op("scalar", lambda E, ec=ec, b=bank, t0=t0: E.activation(
                        q_sb[:, ec * TOK + t0: ec * TOK + t0 + 512],
                        b[:, :], AF.Identity, bias=bq_sb[:, ec:ec + 1]), inc="s")
                    P.wait("tensor", "s")
                emit_proj_fm(xn, 512, Wq_a, wb, wsems, list(range(DC)), 0, 512,
                             pb[0:2], evac_q)

                if half == 0:
                    tap("q0", q_sb[:])
                P.wait("tensor", "dg")
                P.wait("tensor", "v")
                for q_i in range(2):
                    w0 = q_i * DC * 16
                    for nl in range(4):
                        n = half * 4 + nl
                        bank = pb[2 + (nl % 2)]
                        for kc in range(DC):
                            P.op("tensor", lambda E, b=bank, w0=w0, kc=kc, nl=nl:
                                 E.matmul(b[:, 0:16],
                                          xn[:, kc * 512 + nl * 128:
                                             kc * 512 + (nl + 1) * 128],
                                          wab[:, w0 + kc * 16: w0 + (kc + 1) * 16],
                                          start=(kc == 0), stop=False))
                        P.op("tensor", lambda E, b=bank, q_i=q_i: E.matmul(
                            b[:, 0:16], ones_rf[:],
                            ba_row[0:1, q_i * NH:(q_i + 1) * NH],
                            start=False, stop=True), inc="t")
                        P.wait("scalar", "t")
                        if q_i == 0:
                            P.op("scalar", lambda E, b=bank, n=n: E.activation(
                                atm[:, n * 16:(n + 1) * 16], b[:, 0:16], AF.Sigmoid),
                                inc="s")
                        else:
                            P.op("scalar", lambda E, b=bank, n=n: E.activation(
                                btm[:, n * 16:(n + 1) * 16], b[:, 0:16], AF.Identity),
                                inc="s")
                        P.wait("tensor", "s")
                sl_h = slice(half * 4 * 16, (half + 1) * 4 * 16)
                P.op("scalar", lambda E, sl=sl_h: E.activation(
                    latm[:, sl], btm[:, sl], AF.Abs))
                P.op("scalar", lambda E, sl=sl_h: E.activation(
                    latm[:, sl], latm[:, sl], AF.Exp, scale=-1.0))
                P.op("scalar", lambda E, sl=sl_h: E.activation(
                    latm[:, sl], latm[:, sl], AF.Ln, bias=1.0), inc="s")
                P.wait("vector", "s")
                P.op("vector", lambda E, sl=sl_h: E.tensor_scalar_max(
                    btm[:, sl], btm[:, sl], 0.0), inc="v")
                vdrain()
                P.op("vector", lambda E, sl=sl_h: E.tensor_add(
                    btm[:, sl], btm[:, sl], latm[:, sl]), inc="v")
                vdrain()
                P.wait("scalar", "v")
                P.op("scalar", lambda E, sl=sl_h: E.activation(
                    latm[:, sl], atm[:, sl], AF.Ln, bias=1e-6), inc="s")
                P.wait("tensor", "s")
                for nl in range(4):
                    n = half * 4 + nl
                    P.op("tensor", lambda E, n=n, nl=nl: E.matmul(
                        pb[4][:, nl * 16:(nl + 1) * 16], UTRI[:],
                        latm[:, n * 16:(n + 1) * 16], start=True, stop=True),
                        inc=("t" if nl == 3 else None))
                P.wait("vector", "t")
                for nl in range(4):
                    n = half * 4 + nl
                    P.op("vector", lambda E, n=n, nl=nl: E.tensor_copy(
                        cla_tm[:, n * 16:(n + 1) * 16],
                        pb[4][:, nl * 16:(nl + 1) * 16]),
                        inc=("v" if nl == 3 else None))
                P.wait("tensor", "v")
                for nl in range(4):
                    n = half * 4 + nl
                    P.op("tensor", lambda E, n=n, nl=nl: E.transpose(
                        pb[5][0:16, nl * 128:(nl + 1) * 128],
                        cla_tm[:, n * 16:(n + 1) * 16], ident32[:]),
                        inc=("t" if nl == 3 else None))
                P.wait("vector", "t")
                for nl in range(4):
                    P.op("vector", lambda E, nl=nl: E.tensor_copy(
                        cla_hh[:, nl * 128:(nl + 1) * 128],
                        pb[5][0:16, nl * 128:(nl + 1) * 128]),
                        inc=("v" if nl == 3 else None))
                vdrain()
                cends = cla_hh[:].rearrange("h (n c) -> h n c", c=128)[:, :, 127]
                P.op("vector", lambda E, half=half, cends=cends: E.tensor_copy(
                    ends_h[:, half * 4:(half + 1) * 4], cends), inc="v")
                vdrain()
                if half == 0:
                    tap("clh", cla_hh[:])
                    tap("ctm", cla_tm[:])
                P.wait("scalar", "v")
                P.op("scalar", lambda E, t0=t0, t1=t1: E.activation(
                    ecb_t[:, t0:t1], cla_hh[:], AF.Exp), inc="s")
                eendsh = ecb_t[:, 0:TOK].rearrange(
                    "h (n c) -> h n c", c=128)[:, half * 4:(half + 1) * 4, 127]
                P.wait("vector", "s")
                P.op("vector", lambda E, half=half, eendsh=eendsh: E.tensor_copy(
                    eef[:, half * 4:(half + 1) * 4], eendsh), inc="v")
                vdrain()
                P.wait("tensor", "s")
                P.wait("tensor", "v")
                for h in range(NH):
                    P.op("tensor", lambda E, h=h, half=half: E.matmul(
                        pb[4][:, h * 8 + half * 4: h * 8 + half * 4 + 4],
                        EH32[:, h * 128:(h + 1) * 128],
                        eef[:, half * 4:(half + 1) * 4],
                        start=(h == 0), stop=(h == NH - 1)),
                        inc=("t" if h == NH - 1 else None))
                P.wait("vector", "t")
                for h in range(NH):
                    P.op("vector", lambda E, h=h, half=half: E.tensor_copy(
                        etd_sb[:, h * 8 + half * 4: h * 8 + half * 4 + 4],
                        pb[4][:, h * 8 + half * 4: h * 8 + half * 4 + 4]),
                        inc=("v" if h == NH - 1 else None))
                vdrain()

                def evac_k(ec, tb, bank):
                    P.wait("scalar", "t")
                    P.op("scalar", lambda E, ec=ec, b=bank: E.activation(
                        k_sb[:, ec * 512:(ec + 1) * 512], b[:, :], AF.Identity,
                        bias=bk_sb[:, ec:ec + 1]), inc="s")
                    P.wait("tensor", "s")
                emit_proj_fm(xn, 512, Wk_a, wb, wsems, list(range(DC)), 0, 512,
                             pb[0:2], evac_k)

                def evac_v(tc, eb, bank):
                    P.wait("vector", "t")
                    P.op("vector", lambda E, b=bank, tc=tc, eb=eb: E.tensor_copy(
                        v_sb[:, tc * D + eb * 512: tc * D + (eb + 1) * 512], b[:, :]),
                        inc="v")
                    P.wait("tensor", "v")
                emit_proj_tm(xn, 512, Wv_a, wvb, 0, list(range(4)), pb[0:4],
                             (bv_row, bv_a), evac_v)

                prev_t = 0
                for h in range(NH):
                    if prev_t:
                        P.wait("scalar", "t", prev_t)
                    P.wait("scalar", "v")
                    P.op("scalar", lambda E, h=h: E.activation(
                        ksq[:], k_sb[:, h * 512:(h + 1) * 512], AF.Square), inc="s")
                    P.wait("tensor", "s")
                    P.op("tensor", lambda E, h=h: E.matmul(
                        pb[4][0:16, :], ONEH[:, h * 16:(h + 1) * 16], ksq[:],
                        start=(h == 0), stop=(h == NH - 1)), inc="t")
                    prev_t = P.v("t")
                P.wait("vector", "t")
                P.op("vector", lambda E: E.tensor_scalar_max(
                    scr_hh[:], pb[4][0:16, :], 1e-24), inc="v")
                rsqrt_refined(rn_hh[:], scr_hh[:], lbmc_hh[:])
                P.wait("tensor", "v")
                for nl in range(4):
                    n = half * 4 + nl
                    P.op("tensor", lambda E, n=n, nl=nl: E.transpose(
                        pb[5][0:16, nl * 128:(nl + 1) * 128],
                        btm[:, n * 16:(n + 1) * 16], ident32[:]),
                        inc=("t" if nl == 3 else None))
                P.wait("vector", "t")
                for nl in range(4):
                    P.op("vector", lambda E, nl=nl: E.tensor_copy(
                        scr_hh[:, nl * 128:(nl + 1) * 128],
                        pb[5][0:16, nl * 128:(nl + 1) * 128]),
                        inc=("v" if nl == 3 else None))
                vdrain()
                P.op("vector", lambda E: E.tensor_mul(scr_hh[:], scr_hh[:], rn_hh[:]),
                     inc="v")
                vdrain()
                P.wait("scalar", "v")
                P.op("scalar", lambda E: E.activation(lbmc_hh[:], scr_hh[:], AF.Ln),
                     inc="s")
                P.wait("vector", "s")
                P.op("vector", lambda E: E.tensor_sub(lbmc_hh[:], lbmc_hh[:],
                                                      cla_hh[:]), inc="v")
                vdrain()
                for nl in range(4):
                    ce = cla_hh[:, nl * 128 + 127: nl * 128 + 128]
                    P.op("vector", lambda E, nl=nl, ce=ce: E.tensor_add(
                        scr_hh[:, nl * 128:(nl + 1) * 128],
                        lbmc_hh[:, nl * 128:(nl + 1) * 128],
                        ce.broadcast_to((16, 128))),
                        inc=("v" if nl == 3 else None))
                vdrain()
                P.wait("scalar", "v")
                P.op("scalar", lambda E: E.activation(scr_hh[:], scr_hh[:], AF.Exp),
                     inc="s")
                P.wait("tensor", "s")
                for nl in range(4):
                    n = half * 4 + nl
                    P.op("tensor", lambda E, n=n, nl=nl: E.transpose(
                        pb[5][:, nl * 16:(nl + 1) * 16],
                        scr_hh[:, nl * 128:(nl + 1) * 128], ident32[0:16, 0:16]),
                        inc=("t" if nl == 3 else None))
                P.wait("vector", "t")
                for nl in range(4):
                    n = half * 4 + nl
                    P.op("vector", lambda E, n=n, nl=nl: E.tensor_copy(
                        w_tm[:, n * 16:(n + 1) * 16], pb[5][:, nl * 16:(nl + 1) * 16]),
                        inc=("v" if nl == 3 else None))
                vdrain()

                if half == 0:
                    tap("v0", v_sb[:])
                for nl in range(4):
                    n = half * 4 + nl
                    if half == 0 and nl == 1:
                        tap("s0", S_send[:, 0:2048])
                    for h in range(NH):
                        qt = q_sb[:, h * TOK + n * 128: h * TOK + (n + 1) * 128]
                        kt = k_sb[:, h * 512 + nl * 128: h * 512 + (nl + 1) * 128]
                        vt = v_sb[:, nl * D + h * 128: nl * D + (h + 1) * 128]
                        clat = cla_hh[:, nl * 128:(nl + 1) * 128]
                        ecbt = ecb_t[:, n * 128:(n + 1) * 128]
                        lbt = lbmc_hh[:, nl * 128:(nl + 1) * 128]
                        ehs = EH[:, h * 128:(h + 1) * 128]
                        ehs32 = EH32[:, h * 128:(h + 1) * 128]
                        P.op("vector", lambda E, h=h: E.tensor_copy(
                            snap_bf[:], S_send[:, h * 128:(h + 1) * 128]), inc="v")
                        P.wait("tensor", "v")
                        P.op("tensor", lambda E, kt=kt: E.transpose(
                            ptr[:, 0:128], kt, ident[:]))
                        P.op("tensor", lambda E, kt=kt, qt=qt: E.matmul(
                            pb[0][:, 0:128], kt, qt, start=True, stop=True))
                        P.op("tensor", lambda E, lbt=lbt, ehs32=ehs32: E.matmul(
                            pb[1][:, 0:128], lbt, ehs32, start=True, stop=False))
                        P.op("tensor", lambda E, clat=clat, ehs32=ehs32: E.matmul(
                            pb[1][:, 0:128], ehs32, clat, start=False, stop=True))
                        P.op("tensor", lambda E, ecbt=ecbt, ehs32=ehs32: E.matmul(
                            pb[4][:, 0:128], ehs32, ecbt, start=True, stop=True),
                            inc="t")
                        P.wait("vector", "t")
                        P.op("vector", lambda E: E.tensor_add(
                            expd_a[:], pb[1][:, 0:128], maskneg[:]), inc="v")
                        P.wait("scalar", "v")
                        P.op("scalar", lambda E: E.activation(
                            expd_b[:], expd_a[:], AF.Exp), inc="s")
                        P.wait("vector", "s")
                        P.op("vector", lambda E: E.tensor_mul(
                            attnf[:], pb[0][:, 0:128], expd_b[:]))
                        P.op("vector", lambda E, n=n, h=h: E.tensor_scalar_mul(
                            kw_sb[:], ptr[:, 0:128],
                            w_tm[:, n * 16 + h: n * 16 + h + 1]))
                        P.op("vector", lambda E, qt=qt: E.tensor_mul(
                            qcd[:], qt, pb[4][:, 0:128]), inc="v")
                        P.wait("tensor", "v")
                        P.op("tensor", lambda E, vt=vt: E.matmul(
                            pb[2][:, 0:128], kw_sb[:], vt, start=True, stop=True))
                        P.op("tensor", lambda E, vt=vt: E.matmul(
                            pb[3][:, 0:128], vt, attnf[:], start=True, stop=False))
                        P.op("tensor", lambda E: E.matmul(
                            pb[3][:, 0:128], snap_bf[:], qcd[:],
                            start=False, stop=True), inc="t")
                        P.wait("vector", "t")
                        P.op("vector", lambda E, h=h, n=n: E.tensor_add(
                            xs(h, n * 128, (n + 1) * 128),
                            xs(h, n * 128, (n + 1) * 128), pb[3][:, 0:128]))
                        P.op("vector", lambda E, h=h, n=n: E.scalar_tensor_tensor(
                            S_send[:, h * 128:(h + 1) * 128],
                            S_send[:, h * 128:(h + 1) * 128],
                            etd_sb[:, h * 8 + n: h * 8 + n + 1],
                            pb[2][:, 0:128], ALU.mult, ALU.add), inc="v")

            P.wait("gpsimd", "dw0"); P.wait("gpsimd", "dw2"); P.wait("gpsimd", "dw3")
            inner.close()

            gcb_t = SE(nc.sbuf_tensor("gcb_t", [16, TOK], F32))
            tap("end", ends_h[:])
            P.op("vector", lambda E: E.tensor_reduce(
                S_send[0:16, 2048:2049], ends_h[:, 0:8], AX.X, ALU.add), inc="v")
            vdrain()
            P.wait("gpsimd", "v")
            P.op("gpsimd", lambda E: E.memset(ends_h[:, 8:16], 0.0), inc="g")
            P.wait("vector", "g")
            P.op("vector", lambda E: E.tensor_tensor_scan(
                latm[0:16, 0:8], ends_h[:, 0:8], ends_h[:, 8:16], 0.0,
                ALU.add, ALU.add), inc="v")
            vdrain()
            P.op("vector", lambda E: E.tensor_copy(ends_h[:, 9:16], latm[0:16, 0:7]),
                 inc="v")
            vdrain()
            for half in range(2):
                P.wait("tensor", "v")
                for nl in range(4):
                    n = half * 4 + nl
                    P.op("tensor", lambda E, n=n, nl=nl: E.transpose(
                        pb[5][0:16, nl * 128:(nl + 1) * 128],
                        cla_tm[:, n * 16:(n + 1) * 16], ident32[:]),
                        inc=("t" if nl == 3 else None))
                P.wait("vector", "t")
                for nl in range(4):
                    n = half * 4 + nl
                    P.op("vector", lambda E, n=n, nl=nl: E.tensor_add(
                        cla_hh[:, nl * 128:(nl + 1) * 128],
                        pb[5][0:16, nl * 128:(nl + 1) * 128],
                        ends_h[:, 8 + n: 9 + n].broadcast_to((16, 128))),
                        inc=("v" if nl == 3 else None))
                P.wait("scalar", "v")
                P.op("scalar", lambda E, half=half: E.activation(
                    gcb_t[:, half * 512:(half + 1) * 512], cla_hh[:], AF.Exp),
                    inc="s")
                P.wait("tensor", "s")

            P.wait("gpsimd", "v")
            P.op("gpsimd", lambda E: E.dma_start(out=ssend_d.ap()[:, :], in_=S_send[:]),
                 inc="dg")
            P.wait("gpsimd", "dg")
            P.op("gpsimd", lambda E: E.collective_compute(
                "AllGather", ALU.bypass, replica_groups=[[0, 1, 2, 3], [4, 5, 6, 7]],
                ins=[ssend_d.ap().opt()], outs=[sgath_d.ap().opt()]), inc="cc")
            P.wait("gpsimd", "cc")
            P.wait("sync", "cc")

            S_in = SE(nc.sbuf_tensor("S_in", [128, 2048], F32))
            S_in_bf = SE(nc.sbuf_tensor("S_in_bf", [128, 2048], BF16))
            gla = SE(nc.sbuf_tensor("gla", [16, 16], F32))
            fc_sb = SE(nc.sbuf_tensor("fc_sb", [128, 64], F32))
            gbuf = [SE(nc.sbuf_tensor(f"gbuf{i}", [128, 2049], F32)) for i in range(2)]
            for p_ in range(4):
                P.op("sync", lambda E, p_=p_: E.dma_start(
                    out=gla[:, p_:p_ + 1], in_=sgath_d.ap()[p_, 0:16, 2048:2049],
                    allow_slow_non_contiguous=True), inc="dy")
            P.wait("vector", "dy")
            P.op("vector", lambda E: E.tensor_mul(gla[:, 4:8], gla[:, 0:4], gsel_sb[:]),
                 inc="v")
            vdrain()
            P.wait("gpsimd", "v")
            P.op("gpsimd", lambda E: E.memset(gla[:, 11:12], 0.0), inc="g")
            P.wait("vector", "g")
            P.op("vector", lambda E: E.tensor_copy(gla[:, 10:11], gla[:, 7:8]),
                 inc="v")
            vdrain()
            P.op("vector", lambda E: E.tensor_add(gla[:, 9:10], gla[:, 10:11],
                                                  gla[:, 6:7]), inc="v")
            vdrain()
            P.op("vector", lambda E: E.tensor_add(gla[:, 8:9], gla[:, 9:10],
                                                  gla[:, 5:6]), inc="v")
            vdrain()
            P.wait("scalar", "v")
            P.op("scalar", lambda E: E.activation(gla[:, 12:16], gla[:, 8:12],
                                                  AF.Exp), inc="s")
            P.wait("vector", "s")
            P.op("vector", lambda E: E.tensor_mul(gla[:, 12:16], gla[:, 12:16],
                                                  gsel_sb[:]), inc="v")
            vdrain()
            P.wait("tensor", "v")
            for h in range(NH):
                P.op("tensor", lambda E, h=h: E.matmul(
                    pb[4][:, h * 4:(h + 1) * 4], EH32[:, h * 128:(h + 1) * 128],
                    gla[:, 12:16], start=(h == 0), stop=(h == NH - 1)),
                    inc=("t" if h == NH - 1 else None))
            P.wait("vector", "t")
            P.op("vector", lambda E: E.tensor_copy(fc_sb[:], pb[4][:, 0:64]), inc="v")
            P.wait("gpsimd", "v")
            P.op("gpsimd", lambda E: E.memset(S_in[:], 0.0), inc="g")
            P.wait("vector", "g")
            gb_wait = [0, 0]
            for p_ in range(4):
                b = gbuf[p_ % 2]
                if gb_wait[p_ % 2]:
                    P.wait("sync", "v", gb_wait[p_ % 2])
                P.op("sync", lambda E, p_=p_, b=b: E.dma_start(
                    out=b[:], in_=sgath_d.ap()[p_]), inc="dy")
                P.wait("vector", "dy")
                for h in range(NH):
                    P.op("vector", lambda E, b=b, h=h, p_=p_: E.scalar_tensor_tensor(
                        S_in[:, h * 128:(h + 1) * 128], b[:, h * 128:(h + 1) * 128],
                        fc_sb[:, h * 4 + p_: h * 4 + p_ + 1],
                        S_in[:, h * 128:(h + 1) * 128], ALU.mult, ALU.add),
                        inc=("v" if h == NH - 1 else None))
                vdrain()
                gb_wait[p_ % 2] = P.v("v")
            vdrain()
            P.op("vector", lambda E: E.tensor_copy(S_in_bf[:], S_in[:]), inc="v")
            tap("sin", S_in[:])
            tap("gla", gla[:])
            tap("gsl", gsel_sb[:])
            tap("fc", fc_sb[:])
            tap("gcb", gcb_t[:])
            tap("ehh", ends_h[:])

            for n in range(NCH):
                for h in range(NH):
                    qt = q_sb[:, h * TOK + n * 128: h * TOK + (n + 1) * 128]
                    gcbt = gcb_t[:, n * 128:(n + 1) * 128]
                    ehs32 = EH32[:, h * 128:(h + 1) * 128]
                    P.wait("tensor", "v")
                    P.op("tensor", lambda E, gcbt=gcbt, ehs32=ehs32: E.matmul(
                        pb[0][:, 0:128], ehs32, gcbt, start=True, stop=True), inc="t")
                    P.wait("vector", "t")
                    P.op("vector", lambda E, qt=qt: E.tensor_mul(
                        qcd[:], qt, pb[0][:, 0:128]), inc="v")
                    P.wait("tensor", "v")
                    P.op("tensor", lambda E, h=h: E.matmul(
                        pb[1][:, 0:128], S_in_bf[:, h * 128:(h + 1) * 128], qcd[:],
                        start=True, stop=True), inc="t")
                    P.wait("vector", "t")
                    P.op("vector", lambda E, h=h, n=n: E.tensor_add(
                        xs(h, n * 128, (n + 1) * 128),
                        xs(h, n * 128, (n + 1) * 128), pb[1][:, 0:128]), inc="v")
        barrier()

    def sublayer_ffn(tag):
        with contextlib.ExitStack() as ss:
            SE = ss.enter_context
            csum = SE(nc.sbuf_tensor(f"csum{tag}", [1, TOK], BF16))
            msum = SE(nc.sbuf_tensor(f"msum{tag}", [1, TOK], BF16))
            t1sb = SE(nc.sbuf_tensor(f"t1sb{tag}", [128, 512], F32))
            midst = [SE(nc.sbuf_tensor(f"mst{tag}{i}", [128, TOK], BF16))
                     for i in range(2)]
            pb = [SE(nc.psum_tensor(f"pf{tag}{i}", [128, 512], F32)) for i in range(4)]
            with contextlib.ExitStack() as xs_scope:
                XL = xs_scope.enter_context
                xn = XL(nc.sbuf_tensor(f"xnf{tag}", [128, DC * TOK], BF16))
                xb_s = XL(nc.sbuf_tensor(f"fxb{tag}", [128, 512], BF16))
                xq_s = XL(nc.sbuf_tensor(f"fxq{tag}", [128, 512], BF16))
                st_f = tuple(XL(nc.sbuf_tensor(f"stf{tag}{i}", [1, sz], F32))
                             for i, sz in enumerate((TOK, TOK, 512, 512)))
                emit_ln_stats(pb[0:4], xb_s, xq_s, st_f)
                emit_ln_norm(pb[0:4], xn, lambda dc, a: dc * TOK + a, 0, TOK, st_f)
                P.wait("tensor", "v")
                for tb in range(2):
                    for kc in range(DC):
                        P.op("tensor", lambda E, tb=tb, kc=kc: E.matmul(
                            pb[2 + tb][0:1, :], ones_c[:],
                            xn[:, kc * TOK + tb * 512: kc * TOK + (tb + 1) * 512],
                            start=(kc == 0), stop=(kc == DC - 1)),
                            inc=("t" if kc == DC - 1 else None))
                P.wait("vector", "t")
                P.op("vector", lambda E: E.tensor_copy(csum[:, 0:512], pb[2][0:1, :]))
                P.op("vector", lambda E: E.tensor_copy(csum[:, 512:1024],
                                                       pb[3][0:1, :]), inc="v")
                with contextlib.ExitStack() as ws:
                    WL = ws.enter_context
                    wg = [WL(nc.sbuf_tensor(f"wg{tag}{i}", [128, DC * 128], BF16))
                          for i in range(2)]
                    wu = [WL(nc.sbuf_tensor(f"wu{tag}{i}", [128, DC * 128], BF16))
                          for i in range(2)]
                    P.wait("tensor", "v")
                    mid_dma = [0, 0]
                    for hc in range(HC):
                        bg, bu = wg[hc % 2], wu[hc % 2]
                        sg_, su_ = ("dw0", "dw1")[hc % 2], ("dw2", "dw3")[hc % 2]
                        if hc == 0:
                            load_wchunk(bg, sg_, WgT, DC, 0, 128)
                            load_wchunk(bu, su_, WuT, DC, 0, 128)
                        P.wait("tensor", sg_)
                        P.wait("tensor", su_)
                        if hc + 1 < HC:
                            load_wchunk(wg[(hc + 1) % 2], ("dw0", "dw1")[(hc + 1) % 2],
                                        WgT, DC, (hc + 1) * 128, 128)
                            load_wchunk(wu[(hc + 1) % 2], ("dw2", "dw3")[(hc + 1) % 2],
                                        WuT, DC, (hc + 1) * 128, 128)
                        mst = midst[hc % 2]
                        for tb in range(2):
                            for kc in range(DC):
                                P.op("tensor", lambda E, bg=bg, kc=kc, tb=tb: E.matmul(
                                    pb[0][:, :], bg[:, kc * 128:(kc + 1) * 128],
                                    xn[:, kc * TOK + tb * 512:
                                       kc * TOK + (tb + 1) * 512],
                                    start=(kc == 0), stop=False))
                            P.op("tensor", lambda E, tb=tb: E.matmul(
                                pb[0][:, :], neg128_r[:],
                                csum[0:1, tb * 512:(tb + 1) * 512],
                                start=False, stop=True), inc="t")
                            for kc in range(DC):
                                P.op("tensor", lambda E, bu=bu, kc=kc, tb=tb: E.matmul(
                                    pb[1][:, :], bu[:, kc * 128:(kc + 1) * 128],
                                    xn[:, kc * TOK + tb * 512:
                                       kc * TOK + (tb + 1) * 512],
                                    start=(kc == 0), stop=False))
                            P.op("tensor", lambda E, tb=tb: E.matmul(
                                pb[1][:, :], neg128_r[:],
                                csum[0:1, tb * 512:(tb + 1) * 512],
                                start=False, stop=True), inc="t")
                            P.wait("scalar", "t")
                            P.op("scalar", lambda E, hc=hc: E.activation(
                                t1sb[:], pb[0][:, :], AF.Silu,
                                scale=sg_sb[:, hc:hc + 1]), inc="s")
                            P.wait("vector", "s")
                            if mid_dma[hc % 2]:
                                P.wait("vector", ("vy0", "vy1")[hc % 2],
                                       mid_dma[hc % 2])
                            P.op("vector", lambda E, mst=mst, hc=hc, tb=tb:
                                 E.scalar_tensor_tensor(
                                     mst[:, tb * 512:(tb + 1) * 512],
                                     pb[1][:, :], su_sb[:, hc:hc + 1], t1sb[:],
                                     ALU.mult, ALU.mult), inc="v")
                            P.wait("tensor", "v")
                        for tb in range(2):
                            P.op("tensor", lambda E, mst=mst, tb=tb, hc=hc: E.matmul(
                                pb[2 + tb][0:1, :], ones_c[:],
                                mst[:, tb * 512:(tb + 1) * 512],
                                start=(hc == 0), stop=(hc == HC - 1)),
                                inc=("t" if tb == 1 else None))
                        P.wait("sync", "v")
                        P.op("sync", lambda E, mst=mst, hc=hc: E.dma_start(
                            out=mid_d.ap()[hc], in_=mst[:]),
                            inc=("vy0", "vy1")[hc % 2])
                        mid_dma[hc % 2] = P.v(("vy0", "vy1")[hc % 2])
                        wchunk_done(bg, P.v("t"))
                        wchunk_done(bu, P.v("t"))
            P.wait("vector", "t")
            P.op("vector", lambda E: E.tensor_copy(msum[:, 0:512], pb[2][0:1, :]))
            P.op("vector", lambda E: E.tensor_copy(msum[:, 512:1024], pb[3][0:1, :]),
                 inc="v")
            NG = 2
            GH = HC // NG
            with contextlib.ExitStack() as ws:
                WL = ws.enter_context
                P.wait("gpsimd", "t")
                P.wait("sync", "t")
                mgrp = WL(nc.sbuf_tensor(f"mg{tag}", [128, GH * TOK], BF16))
                wd = [WL(nc.sbuf_tensor(f"wd{tag}{i}", [128, GH * 128], BF16))
                      for i in range(2)]
                P.wait("tensor", "v")
                for hg in range(NG):
                    P.wait("sync", ("vy0", "vy1")[0])
                    P.wait("sync", ("vy0", "vy1")[1])
                    if hg > 0:
                        P.wait("sync", "t")
                    P.op("sync", lambda E, hg=hg: E.dma_start(
                        out=mgrp[:].rearrange("p (h t) -> p h t", h=GH),
                        in_=mid_d.ap()[hg * GH:(hg + 1) * GH].transpose([1, 0, 2])),
                        inc="ky0")
                    P.wait("tensor", "ky0")
                    for dc in range(DC):
                        bd = wd[dc % 2]
                        sd_ = ("dw0", "dw1")[dc % 2]
                        if dc == 0:
                            load_wchunk_rows(bd, sd_, WdT, GH, hg * GH, dc)
                        P.wait("tensor", sd_)
                        if dc + 1 < DC:
                            load_wchunk_rows(wd[(dc + 1) % 2],
                                             ("dw0", "dw1")[(dc + 1) % 2],
                                             WdT, GH, hg * GH, dc + 1)
                        for tb in range(2):
                            bank = pb[tb]
                            last = NG - 1
                            for hc in range(GH):
                                fin = (hg < last and hc == GH - 1)
                                P.op("tensor",
                                     lambda E, b=bank, bd=bd, hc=hc, tb=tb, fin=fin:
                                     E.matmul(b[:, :], bd[:, hc * 128:(hc + 1) * 128],
                                              mgrp[:, hc * TOK + tb * 512:
                                                   hc * TOK + (tb + 1) * 512],
                                              start=(hc == 0), stop=fin),
                                     inc=("t" if fin else None))
                            if hg == last:
                                P.op("tensor", lambda E, b=bank, tb=tb: E.matmul(
                                    b[:, :], neg128_r[:],
                                    msum[0:1, tb * 512:(tb + 1) * 512],
                                    start=False, stop=True), inc="t")
                            P.wait("vector", "t")
                            P.op("vector", lambda E, b=bank, dc=dc, tb=tb:
                                 E.scalar_tensor_tensor(
                                     xs(dc, tb * 512, (tb + 1) * 512), b[:, :],
                                     sd_sb[:, dc:dc + 1],
                                     xs(dc, tb * 512, (tb + 1) * 512),
                                     ALU.mult, ALU.add), inc="v")
                        wchunk_done(bd, P.v("t"))
                        P.wait("tensor", "v")
        barrier()

    def sublayer_attn_b():
        with contextlib.ExitStack() as ss:
            SE = ss.enter_context
            q_sb = SE(nc.sbuf_tensor("qb_sb", [128, DC * TOK], BF16))
            PT = SE(nc.sbuf_tensor("PT", [128, 32 * 512], BF16))
            with contextlib.ExitStack() as ps:
                PL = ps.enter_context
                stg = PL(nc.sbuf_tensor("stg", [128, 512], BF16))
                xn = PL(nc.sbuf_tensor("xnb", [128, DC * TOK], BF16))
                pbs = [PL(nc.psum_tensor(f"pbb{i}", [128, 512], F32)) for i in range(8)]
                xb_s = PL(nc.sbuf_tensor("bxb", [128, 512], BF16))
                xq_s = PL(nc.sbuf_tensor("bxq", [128, 512], BF16))
                st_b = tuple(PL(nc.sbuf_tensor(f"stb{i}", [1, sz], F32))
                             for i, sz in enumerate((TOK, TOK, 512, 512)))
                emit_ln_stats(pbs[0:4], xb_s, xq_s, st_b)
                emit_ln_norm(pbs[0:4], xn, lambda dc, a: dc * TOK + a, 0, TOK, st_b)
                wb = [PL(nc.sbuf_tensor(f"wbb{i}", [128, DC * 128], BF16))
                      for i in range(2)]
                wvb = [PL(nc.sbuf_tensor(f"wvbb{i}", [128, 512], BF16))
                       for i in range(2)]
                wsems = ["dw0", "dw1"]

                def evac_qb(ec, tb, bank):
                    P.wait("scalar", "t")
                    P.op("scalar", lambda E, ec=ec, tb=tb, b=bank: E.activation(
                        q_sb[:, ec * TOK + tb * 512: ec * TOK + (tb + 1) * 512],
                        b[:, :], AF.Identity, scale=QSCALE), inc="s")
                    P.wait("tensor", "s")
                emit_proj_fm(xn, TOK, Wq_b, wb, wsems, list(range(DC)), 0, TOK,
                             pbs[0:2], evac_qb)

                def evac_kb(ec, tb, bank):
                    P.wait("vector", "t")
                    P.wait("vector", "dg")
                    P.op("vector", lambda E, b=bank: E.tensor_copy(stg[:], b[:, :]),
                         inc="v")
                    P.wait("gpsimd", "v")
                    P.op("gpsimd", lambda E, ec=ec, tb=tb: E.dma_start(
                        out=kown_d.ap()[ec * 128:(ec + 1) * 128,
                                        tb * 512:(tb + 1) * 512],
                        in_=stg[:]), inc="dg")
                    P.wait("tensor", "v")
                emit_proj_fm(xn, TOK, Wk_b, wb, wsems, list(range(DC)), 0, TOK,
                             pbs[0:2], evac_kb)

                def evac_vb(tc, eb, bank):
                    P.wait("vector", "t")
                    P.wait("vector", "dg")
                    P.op("vector", lambda E, b=bank: E.tensor_copy(stg[:], b[:, :]),
                         inc="v")
                    P.wait("gpsimd", "v")
                    P.op("gpsimd", lambda E, tc=tc, eb=eb: E.dma_start(
                        out=vown_d.ap()[tc * 128:(tc + 1) * 128,
                                        eb * 512:(eb + 1) * 512],
                        in_=stg[:]), inc="dg")
                    P.wait("tensor", "v")
                emit_proj_tm(xn, TOK, Wv_b, wvb, 0, list(range(8)), pbs[0:8], None,
                             evac_vb)
            P.wait("gpsimd", "dg")
            P.op("gpsimd", lambda E: E.collective_compute(
                "AllGather", ALU.bypass, replica_groups=[[0, 1, 2, 3], [4, 5, 6, 7]],
                ins=[kown_d.ap().opt()], outs=[kfull_d.ap().opt()]), inc="cc")
            P.wait("gpsimd", "cc")
            P.op("gpsimd", lambda E: E.collective_compute(
                "AllGather", ALU.bypass, replica_groups=[[0, 1, 2, 3], [4, 5, 6, 7]],
                ins=[vown_d.ap().opt()], outs=[vfull_d.ap().opt()]), inc="cc")
            P.wait("gpsimd", "cc")
            P.wait("sync", "cc")

            bphase = int(os.environ.get("MK_DEV_BPH", "4"))
            for half in range(2 if bphase >= 4 else 1):
                if bphase < 2:
                    break
                with contextlib.ExitStack() as sp:
                    SL = sp.enter_context
                    scsb = [SL(nc.sbuf_tensor(f"sc{half}{i}", [128, 4096], F32))
                            for i in range(2)]
                    psb = SL(nc.sbuf_tensor(f"ps{half}", [128, 4096], BF16))
                    rowst = SL(nc.sbuf_tensor(f"row{half}", [128, 8], F32))
                    kblk = [SL(nc.sbuf_tensor(f"kb{half}{i}", [128, DC * 512], BF16))
                            for i in range(2)]
                    pb2 = [SL(nc.psum_tensor(f"pbs{half}{i}", [128, 512], F32))
                           for i in range(2)]
                    ptrb2 = [SL(nc.psum_tensor(f"ptrb{half}{i}", [128, 512], BF16))
                             for i in range(2)]
                    kb_wait = [0, 0]
                    ptr_wait = [0] * 8
                    P.wait("tensor", "v")
                    P.wait("sync", "t")
                    P.wait("sync", "v")
                    P.wait("sync", "s")
                    for pr in range(2):
                        qcs = [half * 4 + pr * 2, half * 4 + pr * 2 + 1]
                        for kb in range(8):
                            bi = kb % 2
                            if kb_wait[bi]:
                                P.wait("sync", "t", kb_wait[bi])
                            g, tl = kb // 2, (kb % 2) * 512
                            P.op("sync", lambda E, bi=bi, g=g, tl=tl: E.dma_start(
                                out=kblk[bi][:].rearrange("p (kc m) -> p kc m", kc=DC),
                                in_=kfull_d.ap()[g]
                                .rearrange("(kc p) t -> kc p t", p=128)
                                [:, :, tl:tl + 512].transpose([1, 0, 2])),
                                inc=("ky0", "ky1")[bi])
                            P.wait("tensor", ("ky0", "ky1")[bi])
                            for qi, qc in enumerate(qcs):
                                for kc in range(DC):
                                    P.op("tensor",
                                         lambda E, qi=qi, qc=qc, kc=kc, bi=bi:
                                         E.matmul(pb2[qi][:, :],
                                                  q_sb[:, kc * TOK + qc * 128:
                                                       kc * TOK + (qc + 1) * 128],
                                                  kblk[bi][:,
                                                           kc * 512:(kc + 1) * 512],
                                                  start=(kc == 0),
                                                  stop=(kc == DC - 1)),
                                         inc=("t" if kc == DC - 1 else None))
                            kb_wait[bi] = P.v("t")
                            P.wait("vector", "t")
                            for qi in range(2):
                                P.op("vector", lambda E, qi=qi, kb=kb: E.tensor_copy(
                                    scsb[qi][:, kb * 512:(kb + 1) * 512],
                                    pb2[qi][:, :]), inc="v")
                            P.wait("tensor", "v")
                        vdrain()
                        for qi, qc in enumerate(qcs):
                            qloc = qc % 4
                            P.op("vector", lambda E, qi=qi, qloc=qloc: E.reduce_max(
                                rowst[:, qloc:qloc + 1], scsb[qi][:], axis=AX.X,
                                negate=True), inc="v")
                            vdrain()
                            P.wait("scalar", "v")
                            P.wait("scalar", "t")
                            if os.environ.get("MK_NOSM"):
                                P.op("scalar", lambda E, qi=qi: E.activation(
                                    psb[:], scsb[qi][:], AF.Identity), inc="s")
                            else:
                                P.op("scalar", lambda E, qi=qi, qloc=qloc: E.activation(
                                    psb[:], scsb[qi][:], AF.Exp,
                                    bias=rowst[:, qloc:qloc + 1],
                                    accum_out=rowst[:, 4 + qloc:5 + qloc]), inc="s")
                            P.wait("vector", "s")
                            P.op("vector", lambda E, qloc=qloc: E.reciprocal(
                                rowst[:, 4 + qloc:5 + qloc],
                                rowst[:, 4 + qloc:5 + qloc]), inc="v")
                            vdrain()
                            P.op("vector", lambda E, qloc=qloc: E.tensor_scalar_mul(
                                psb[:], psb[:], rowst[:, 4 + qloc:5 + qloc]),
                                inc="v")
                            vdrain()
                            P.wait("tensor", "v")
                            evacv = ptr_wait
                            for kc in range(0 if os.environ.get("MK_NOTR") else 32):
                                bk = kc % 2
                                sl = (kc // 2) % 4
                                if evacv[(kc - 2) % 8] and kc >= 2:
                                    P.wait("tensor", "v", evacv[(kc - 2) % 8])
                                elif evacv[(kc - 2) % 8]:
                                    P.wait("tensor", "v", evacv[(kc - 2) % 8])
                                P.op("tensor", lambda E, kc=kc, sl=sl, bk=bk:
                                     E.transpose(
                                         ptrb2[bk][:, sl * 128:(sl + 1) * 128],
                                         psb[:, kc * 128:(kc + 1) * 128], ident[:]),
                                     inc="t")
                                P.wait("vector", "t")
                                P.op("vector", lambda E, kc=kc, sl=sl, bk=bk,
                                     qloc=qloc: E.tensor_copy(
                                         PT[:, kc * 512 + qloc * 128:
                                            kc * 512 + (qloc + 1) * 128],
                                         ptrb2[bk][:, sl * 128:(sl + 1) * 128]),
                                     inc="v")
                                evacv[kc % 8] = P.v("v")
                if bphase < 3:
                    continue
                with contextlib.ExitStack() as ap_:
                    AL_ = ap_.enter_context
                    av = AL_(nc.sbuf_tensor(f"av{half}", [128, DC * 512], BF16))
                    vblk = [AL_(nc.sbuf_tensor(f"vb{half}{i}", [128, D], BF16))
                            for i in range(2)]
                    pb8 = [AL_(nc.psum_tensor(f"pav{half}{i}", [128, 512], F32))
                           for i in range(8)]
                    vb_wait = [0, 0]
                    P.wait("tensor", "v")
                    P.wait("sync", "t")
                    P.wait("sync", "s")
                    P.wait("sync", "v")
                    P.wait("gpsimd", "t")
                    for p8 in range(2):
                        for kc in range(32):
                            bi = kc % 2
                            if vb_wait[bi]:
                                P.wait("sync", "t", vb_wait[bi])
                            g, r = kc // 8, (kc % 8) * 128
                            P.op("sync", lambda E, bi=bi, g=g, r=r: E.dma_start(
                                out=vblk[bi][:], in_=vfull_d.ap()[g, r:r + 128, :]),
                                inc=("vy0", "vy1")[bi])
                            P.wait("tensor", ("vy0", "vy1")[bi])
                            for ei in range(8):
                                ec = p8 * 8 + ei
                                P.op("tensor", lambda E, bi=bi, ec=ec, ei=ei, kc=kc:
                                     E.matmul(pb8[ei][:, :],
                                              vblk[bi][:, ec * 128:(ec + 1) * 128],
                                              PT[:, kc * 512:(kc + 1) * 512],
                                              start=(kc == 0), stop=(kc == 31)),
                                     inc=("t" if ei == 7 else None))
                            vb_wait[bi] = P.v("t")
                        P.wait("vector", "t")
                        for ei in range(8):
                            ec = p8 * 8 + ei
                            P.op("vector", lambda E, ec=ec, ei=ei: E.tensor_copy(
                                av[:, ec * 512:(ec + 1) * 512], pb8[ei][:, :]),
                                inc=("v" if ei == 7 else None))
                        P.wait("tensor", "v")
                    with contextlib.ExitStack() as ows:
                        OL = ows.enter_context
                        wo = [OL(nc.sbuf_tensor(f"wo{half}{i}", [128, DC * 128], BF16))
                              for i in range(2)]
                        for i in range(DC):
                            buf, sem = wo[i % 2], ("dw0", "dw1")[i % 2]
                            if i == 0:
                                load_wchunk(buf, sem, Wo_b, DC, 0, 128)
                            P.wait("tensor", sem)
                            if i + 1 < DC:
                                load_wchunk(wo[(i + 1) % 2],
                                            ("dw0", "dw1")[(i + 1) % 2],
                                            Wo_b, DC, (i + 1) * 128, 128)
                            for kc in range(DC):
                                P.op("tensor", lambda E, buf=buf, kc=kc: E.matmul(
                                    pb8[0][:, :], buf[:, kc * 128:(kc + 1) * 128],
                                    av[:, kc * 512:(kc + 1) * 512],
                                    start=(kc == 0), stop=(kc == DC - 1)),
                                    inc=("t" if kc == DC - 1 else None))
                            P.wait("vector", "t")
                            P.op("vector", lambda E, dc=i, half=half: E.tensor_add(
                                xs(dc, half * 512, (half + 1) * 512),
                                xs(dc, half * 512, (half + 1) * 512),
                                pb8[0][:, :]), inc="v")
                            P.wait("tensor", "v")
                            wchunk_done(buf, P.v("t"))
        barrier()

    sublayer_attn_a()
    if upto >= 2:
        sublayer_ffn("f1")
    if upto >= 3:
        sublayer_attn_b()
    if upto >= 4:
        sublayer_ffn("f2")

    P.wait("sync", "v")
    P.wait("sync", "t")
    P.op("sync", lambda E: E.dma_start(
        out=x_out.rearrange("(dc p) t -> p dc t", p=128),
        in_=x_sb[:].rearrange("p (dc t) -> p dc t", dc=DC)), inc="st")
    P.wait("sync", "st")
    P.wait("gpsimd", "st")

    with nc.Block() as block:
        @block.sync
        def _(e):
            for fn in P.streams["sync"]:
                fn(e)

        @block.gpsimd
        def _(e):
            for fn in P.streams["gpsimd"]:
                fn(e)

        @block.tensor
        def _(e):
            for fn in P.streams["tensor"]:
                fn(e)

        @block.vector
        def _(e):
            for fn in P.streams["vector"]:
                fn(e)

        @block.scalar
        def _(e):
            for fn in P.streams["scalar"]:
                fn(e)

    es.close()
    return nc


def kernel(**inp):
    x = np.asarray(inp["L0_attn_a_x"])
    WgT_h = np.ascontiguousarray(np.asarray(inp["L0_ffn_wq_gate"]).T)
    WuT_h = np.ascontiguousarray(np.asarray(inp["L0_ffn_wq_up"]).T)
    WdT_h = np.ascontiguousarray(np.asarray(inp["L0_ffn_wq_down"]).T)
    shared = {
        "Wq_a": np.asarray(inp["L0_attn_a_Wq"]), "bq_a": np.asarray(inp["L0_attn_a_bq"]),
        "Wk_a": np.asarray(inp["L0_attn_a_Wk"]), "bk_a": np.asarray(inp["L0_attn_a_bk"]),
        "Wv_a": np.asarray(inp["L0_attn_a_Wv"]), "bv_a": np.asarray(inp["L0_attn_a_bv"]),
        "Wal": np.asarray(inp["L0_attn_a_Walpha"]),
        "bal": np.asarray(inp["L0_attn_a_balpha"]),
        "Wbe": np.asarray(inp["L0_attn_a_Wbeta"]),
        "bbe": np.asarray(inp["L0_attn_a_bbeta"]),
        "Wq_b": np.asarray(inp["L0_attn_b_W_q"]), "Wk_b": np.asarray(inp["L0_attn_b_W_k"]),
        "Wv_b": np.asarray(inp["L0_attn_b_W_v"]), "Wo_b": np.asarray(inp["L0_attn_b_W_o"]),
        "WgT": WgT_h, "WuT": WuT_h, "WdT": WdT_h,
        "sg_in": np.asarray(inp["L0_ffn_s_gate"]),
        "su_in": np.asarray(inp["L0_ffn_s_up"]),
        "sd_in": np.asarray(inp["L0_ffn_s_down"]),
    }
    per_core = []
    for c in range(8):
        b, s = divmod(c, 4)
        d = dict(shared)
        d["x_in"] = np.ascontiguousarray(x[b, s * TOK:(s + 1) * TOK, :].T)
        gsel = np.zeros((NH, 4), dtype=np.float32)
        gsel[:, :s] = 1.0
        d["gsel_in"] = gsel
        per_core.append(d)

    upto = int(os.environ.get("MK_DEV_UPTO", "4"))
    nc = build_program(upto)
    res = run_bass_kernel_spmd(nc, per_core, core_ids=list(range(8)))
    out = np.empty((2, 4096, 2048), dtype=np.float32)
    for c in range(8):
        b, s = divmod(c, 4)
        out[b, s * TOK:(s + 1) * TOK, :] = np.asarray(res.results[c]["x_out"]).T
    return out


# version 03
# speedup vs baseline: 1.1547x; 1.1547x over previous
import os
import contextlib
import numpy as np
import concourse.bass as bass
import concourse.mybir as mybir
from concourse.bass_utils import run_bass_kernel_spmd

F32 = mybir.dt.float32
BF16 = mybir.dt.bfloat16
I32 = mybir.dt.int32
AF = mybir.ActivationFunctionType
ALU = mybir.AluOpType
AX = mybir.AxisListType

TOK = 1024
DC = 16
D = 2048
HID = 5632
HC = 44
NH = 16
NCH = 8
QSCALE = float(1.0 / np.sqrt(2048.0))

DMA_SEMS = {"dg", "dy", "dw0", "dw1", "dw2", "dw3", "ky0", "ky1", "vy0", "vy1", "st"}
ENG_SEMS = {"t", "v", "s", "g", "cc"}

SEMS = {}


class Plan:
    ENGINES = ("tensor", "vector", "scalar", "gpsimd", "sync")

    def __init__(self):
        self.streams = {e: [] for e in self.ENGINES}
        self.semv = {}
        self.last_wait = {}

    def v(self, s):
        return self.semv.get(s, 0)

    def inc(self, s):
        by = 16 if s in DMA_SEMS else 1
        self.semv[s] = self.semv.get(s, 0) + by
        return self.semv[s]

    def emit(self, eng, fn):
        self.streams[eng].append(fn)

    def wait(self, eng, s, val=None):
        val = self.semv.get(s, 0) if val is None else val
        if val <= 0 or val <= self.last_wait.get((eng, s), 0):
            return
        self.last_wait[(eng, s)] = val
        self.emit(eng, lambda E, s=s, val=val: E.wait_ge(SEMS[s], val))

    def op(self, eng, build, inc=None):
        if inc is not None:
            self.inc(inc)
        self.emit(eng, lambda E, b=build, i=inc:
                  (b(E).then_inc(SEMS[i], 16 if i in DMA_SEMS else 1) if i else b(E)))


def build_program(upto=4):
    nc = bass.Bass("TRN2", target_bir_lowering=False,
                   detect_race_conditions=False)
    P = Plan()

    x_in = nc.dram_tensor("x_in", [D, TOK], F32, kind="ExternalInput").ap()
    x_out = nc.dram_tensor("x_out", [D, TOK], F32, kind="ExternalOutput").ap()
    Wq_a = nc.dram_tensor("Wq_a", [D, D], F32, kind="ExternalInput").ap()
    Wk_a = nc.dram_tensor("Wk_a", [D, D], F32, kind="ExternalInput").ap()
    Wv_a = nc.dram_tensor("Wv_a", [D, D], F32, kind="ExternalInput").ap()
    bq_a = nc.dram_tensor("bq_a", [D], F32, kind="ExternalInput").ap()
    bk_a = nc.dram_tensor("bk_a", [D], F32, kind="ExternalInput").ap()
    bv_a = nc.dram_tensor("bv_a", [D], F32, kind="ExternalInput").ap()
    Wal = nc.dram_tensor("Wal", [D, NH], F32, kind="ExternalInput").ap()
    Wbe = nc.dram_tensor("Wbe", [D, NH], F32, kind="ExternalInput").ap()
    bal = nc.dram_tensor("bal", [NH], F32, kind="ExternalInput").ap()
    bbe = nc.dram_tensor("bbe", [NH], F32, kind="ExternalInput").ap()
    Wq_b = nc.dram_tensor("Wq_b", [D, D], F32, kind="ExternalInput").ap()
    Wk_b = nc.dram_tensor("Wk_b", [D, D], F32, kind="ExternalInput").ap()
    Wv_b = nc.dram_tensor("Wv_b", [D, D], F32, kind="ExternalInput").ap()
    Wo_b = nc.dram_tensor("Wo_b", [D, D], F32, kind="ExternalInput").ap()
    WgT = nc.dram_tensor("WgT", [D, HID], I32, kind="ExternalInput").ap()
    WuT = nc.dram_tensor("WuT", [D, HID], I32, kind="ExternalInput").ap()
    WdT = nc.dram_tensor("WdT", [HID, D], I32, kind="ExternalInput").ap()
    sg_in = nc.dram_tensor("sg_in", [HID], F32, kind="ExternalInput").ap()
    su_in = nc.dram_tensor("su_in", [HID], F32, kind="ExternalInput").ap()
    sd_in = nc.dram_tensor("sd_in", [D], F32, kind="ExternalInput").ap()
    gsel_in = nc.dram_tensor("gsel_in", [NH, 4], F32, kind="ExternalInput").ap()

    ssend_d = nc.dram_tensor("ssend_d", [128, 2049], F32)
    sgath_d = nc.dram_tensor("sgath_d", [4, 128, 2049], F32)
    kown_d = nc.dram_tensor("kown_d", [D, TOK], BF16)
    vown_d = nc.dram_tensor("vown_d", [TOK, D], BF16)
    kfull_d = nc.dram_tensor("kfull_d", [4, D, TOK], BF16)
    vfull_d = nc.dram_tensor("vfull_d", [4, TOK, D], BF16)
    mid_d = nc.dram_tensor("mid_d", [HC, 128, TOK], BF16)
    tap_names = [t for t in os.environ.get("MK_TAP", "").split(",") if t]
    tap_d = {t: nc.dram_tensor(f"tap_{i}", [128, 16384], F32, kind="ExternalOutput")
             for i, t in enumerate(tap_names)}

    def tap(name, ap):
        if name not in tap_d:
            return
        n = min(ap.shape[-1] if ap.ndim == 2 else 0, 16384)
        part = ap.shape[0]
        P.wait("gpsimd", "v"); P.wait("gpsimd", "t"); P.wait("gpsimd", "s")
        P.op("gpsimd", lambda E, ap=ap, n=n, part=part: E.dma_start(
            out=tap_d[name].ap()[0:part, 0:n], in_=ap[:, 0:n]), inc="dg")
        P.wait("gpsimd", "dg")

    es = contextlib.ExitStack()
    E = es.enter_context

    x_sb = E(nc.sbuf_tensor("x_sb", [128, DC * TOK], F32))
    ident = E(nc.sbuf_tensor("ident", [128, 128], BF16))
    ident32 = E(nc.sbuf_tensor("ident32", [128, 128], F32))
    maskneg = E(nc.sbuf_tensor("maskneg", [128, 128], F32))
    EH = E(nc.sbuf_tensor("EH", [16, NH * 128], BF16))
    ONEH = E(nc.sbuf_tensor("ONEH", [128, NH * 16], BF16))
    ones_r = E(nc.sbuf_tensor("ones_r", [1, 128], BF16))
    ones_rf = E(nc.sbuf_tensor("ones_rf", [1, 128], F32))
    ones_c = E(nc.sbuf_tensor("ones_c", [128, 1], BF16))
    neg128_r = E(nc.sbuf_tensor("neg128_r", [1, 128], BF16))
    dummy = E(nc.sbuf_tensor("dumt", [1, 64], F32))
    bq_sb = E(nc.sbuf_tensor("bq_sb", [128, DC], F32))
    bk_sb = E(nc.sbuf_tensor("bk_sb", [128, DC], F32))
    ba_sb = E(nc.sbuf_tensor("ba_sb", [16, 1], F32))
    bb_sb = E(nc.sbuf_tensor("bb_sb", [16, 1], F32))
    sg_sb = E(nc.sbuf_tensor("sg_sb", [128, HC], F32))
    su_sb = E(nc.sbuf_tensor("su_sb", [128, HC], F32))
    sd_sb = E(nc.sbuf_tensor("sd_sb", [128, DC], F32))
    gsel_sb = E(nc.sbuf_tensor("gsel_sb", [16, 4], F32))
    c1em6 = E(nc.sbuf_tensor("c1em6", [128, 1], F32))
    nc.const_aps.register(F32, 1e-6, c1em6[:])

    for s in sorted(DMA_SEMS | ENG_SEMS):
        SEMS[s] = E(nc.semaphore(s))

    def xs(dc, t0, t1):
        return x_sb[:, dc * TOK + t0: dc * TOK + t1]

    def barrier():
        for s in sorted(DMA_SEMS):
            P.wait("gpsimd", s)
        P.wait("gpsimd", "cc")
        P.op("tensor", lambda E: E.ldweights(ident[:, 0:1]), inc="t")
        P.op("vector", lambda E: E.memset(dummy[:, 0:16], 0.0), inc="v")
        P.op("scalar", lambda E: E.activation(dummy[:, 16:32], dummy[:, 0:16], AF.Copy),
             inc="s")
        P.op("gpsimd", lambda E: E.memset(dummy[:, 32:48], 0.0), inc="g")
        for eng in P.ENGINES:
            for s in ("t", "v", "s", "g"):
                P.wait(eng, s)

    def vdrain():
        P.wait("vector", "v")

    def rsqrt_refined(dst, src, tmp):
        P.wait("scalar", "v")
        P.op("scalar", lambda E: E.activation(tmp, src, AF.Sqrt), inc="s")
        P.wait("vector", "s")
        P.op("vector", lambda E: E.reciprocal(dst, tmp), inc="v")
        vdrain()
        P.op("vector", lambda E: E.tensor_mul(tmp, src, dst), inc="v")
        vdrain()
        P.op("vector", lambda E: E.tensor_mul(tmp, tmp, dst), inc="v")
        vdrain()
        P.op("vector", lambda E: E.tensor_scalar(tmp, tmp, -0.5, 1.5, ALU.mult,
                                                 ALU.add), inc="v")
        vdrain()
        P.op("vector", lambda E: E.tensor_mul(dst, dst, tmp), inc="v")
        vdrain()

    wsem_state = {}
    fm_halves = {}

    def load_wchunk(buf, sem, src_ap, nkc, col0, ncols):
        lastuse = wsem_state.get(id(buf), 0)
        if lastuse:
            P.wait("gpsimd", "t", lastuse)
        src = src_ap.rearrange("(kc p) m -> kc p m", p=128)
        P.op("gpsimd",
             lambda E, b=buf, s=src, c0=col0, ncc=ncols, k=nkc:
             E.dma_start(out=b[:, 0:k * ncc].rearrange("p (kc m) -> p kc m", kc=k),
                         in_=s[:, :, c0:c0 + ncc].transpose([1, 0, 2])),
             inc=sem)

    def wchunk_done(buf, tval):
        wsem_state[id(buf)] = tval

    def load_wchunk_rows(buf, sem, src_ap, nkc, row_kc0, dc):
        lastuse = wsem_state.get(id(buf), 0)
        if lastuse:
            P.wait("gpsimd", "t", lastuse)
        src = src_ap.rearrange("(kc p) m -> kc p m", p=128)
        P.op("gpsimd",
             lambda E, b=buf, s=src, k0=row_kc0, k=nkc, dc=dc:
             E.dma_start(out=b[:, 0:k * 128].rearrange("p (kc m) -> p kc m", kc=k),
                         in_=s[k0:k0 + k, :, dc * 128:(dc + 1) * 128]
                         .transpose([1, 0, 2])),
             inc=sem)

    def emit_ln_stats(pb, xb, xq, st):
        mean_t, rst_t, scr2a, scr2b = st
        prev_t = 0
        for dc in range(DC):
            for tb in range(2):
                if prev_t:
                    P.wait("scalar", "t", prev_t)
                a, b = tb * 512, (tb + 1) * 512
                P.op("scalar", lambda E, dc=dc, a=a, b=b: E.activation(
                    xb[:], xs(dc, a, b), AF.Copy))
                P.op("scalar", lambda E, dc=dc, a=a, b=b: E.activation(
                    xq[:], xs(dc, a, b), AF.Square), inc="s")
                P.wait("tensor", "s")
                P.op("tensor", lambda E, dc=dc, tb=tb: E.matmul(
                    pb[tb][0:1, :], ones_c[:], xb[:],
                    start=(dc == 0), stop=(dc == DC - 1)))
                P.op("tensor", lambda E, dc=dc, tb=tb: E.matmul(
                    pb[2 + tb][0:1, :], ones_c[:], xq[:],
                    start=(dc == 0), stop=(dc == DC - 1)), inc="t")
                prev_t = P.v("t")
        P.wait("vector", "t")
        P.op("vector", lambda E: E.tensor_scalar_mul(mean_t[0:1, 0:512], pb[0][0:1, :],
                                                     1.0 / 2048.0))
        P.op("vector", lambda E: E.tensor_scalar_mul(mean_t[0:1, 512:1024],
                                                     pb[1][0:1, :], 1.0 / 2048.0),
             inc="v")
        vdrain()
        for hb in range(2):
            sl = hb * 512
            P.op("vector", lambda E, hb=hb, sl=sl: E.tensor_scalar_mul(
                scr2a[:], pb[2 + hb][0:1, :], 1.0 / 2048.0), inc="v")
            P.op("vector", lambda E, sl=sl: E.tensor_mul(
                rst_t[0:1, sl:sl + 512], mean_t[0:1, sl:sl + 512],
                mean_t[0:1, sl:sl + 512]), inc="v")
            vdrain()
            P.op("vector", lambda E, sl=sl: E.tensor_sub(
                scr2a[:], scr2a[:], rst_t[0:1, sl:sl + 512]), inc="v")
            vdrain()
            P.op("vector", lambda E: E.tensor_scalar_add(
                scr2a[:], scr2a[:], 1e-5), inc="v")
            vdrain()
            rsqrt_refined(rst_t[0:1, sl:sl + 512], scr2a[:], scr2b[:])

    def emit_ln_norm(pb, xn_sb, xoff, t0, t1, st):
        mean_t, rst_t, scr2a, scr2b = st
        P.wait("tensor", "v")
        ntb = (t1 - t0) // 512
        for tb in range(ntb):
            a = t0 + tb * 512
            P.op("tensor", lambda E, tb=tb, a=a: E.matmul(
                pb[tb][:, :], ones_rf[:], mean_t[0:1, a:a + 512],
                start=True, stop=True))
            P.op("tensor", lambda E, tb=tb, a=a: E.matmul(
                pb[2 + tb][:, :], ones_rf[:], rst_t[0:1, a:a + 512],
                start=True, stop=True), inc=("t" if tb == ntb - 1 else None))
        P.wait("vector", "t")
        for dc in range(DC):
            for tb in range(ntb):
                a = t0 + tb * 512
                o = xoff(dc, a)
                dst = xn_sb[:, o:o + 512]
                P.op("vector", lambda E, a=a, dst=dst, tb=tb, dc=dc: E.tensor_sub(
                    dst, xs(dc, a, a + 512), pb[tb][:, :]))
                P.op("vector", lambda E, dst=dst, tb=tb: E.tensor_mul(
                    dst, dst, pb[2 + tb][:, :]), inc="v")

    def emit_proj_fm(xn_sb, xn_tok, w_src, wbufs, wsems, ec_list, t0, t1, pb2, evac):
        P.wait("tensor", "v")
        ntb = (t1 - t0) // 512
        buf = wbufs[0]
        nb = len(pb2)
        guards = [None] * nb
        src2 = w_src.rearrange("(kc p) m -> kc p m", p=128)
        hlast = fm_halves.setdefault(id(buf), [0, 0])

        def load_half(ec, hb):
            w = max(hlast[hb], wsem_state.get(id(buf), 0))
            if w:
                P.wait("gpsimd", "t", w)
            P.op("gpsimd", lambda E, ec=ec, hb=hb: E.dma_start(
                out=buf[:, hb * 8 * 128:(hb + 1) * 8 * 128]
                .rearrange("p (kc m) -> p kc m", kc=8),
                in_=src2[hb * 8:(hb + 1) * 8, :, ec * 128:(ec + 1) * 128]
                .transpose([1, 0, 2])), inc=wsems[hb])

        load_half(ec_list[0], 0)
        for i, ec in enumerate(ec_list):
            for tb in range(ntb):
                bi = (i * ntb + tb) % nb
                bank = pb2[bi]
                if guards[bi]:
                    P.wait("tensor", guards[bi][0], guards[bi][1])
                for hb in range(2):
                    P.wait("tensor", wsems[hb])
                    if tb == 0 and hb == 0:
                        load_half(ec, 1)
                    for kcl in range(8):
                        kc = hb * 8 + kcl
                        P.op("tensor", lambda E, b=bank, kc=kc, tb=tb:
                             E.matmul(b[:, :], buf[:, kc * 128:(kc + 1) * 128],
                                      xn_sb[:, kc * xn_tok + t0 + tb * 512:
                                            kc * xn_tok + t0 + (tb + 1) * 512],
                                      start=(kc == 0), stop=(kc == DC - 1)),
                             inc=("t" if (kcl == 7) else None))
                    hlast[hb] = P.v("t")
                    if tb == ntb - 1 and hb == 1 and i + 1 < len(ec_list):
                        load_half(ec_list[i + 1], 0)
                guards[bi] = evac(ec, tb, bank)
        wchunk_done(buf, max(hlast))
        for g in guards:
            if g:
                P.wait("tensor", g[0], g[1])

    def emit_proj_tm(xn_sb, xn_tok, w_src, wvb, t0, tcs, pbl, bias_pair, evac):
        P.wait("tensor", "v")
        src = w_src.rearrange("(kc p) m -> kc p m", p=128)
        bias_use = 0
        guards = [None] * len(pbl)
        for eb in range(4):
            if bias_pair is not None:
                bias_row, bias_dram = bias_pair
                if bias_use:
                    P.wait("gpsimd", "t", bias_use)
                P.op("gpsimd", lambda E, eb=eb: E.dma_start(
                    out=bias_row[:], in_=bias_dram[eb * 512:(eb + 1) * 512]
                    .unsqueeze(0)), inc="dg")
            for kc2 in range(DC // 2):
                i = eb * (DC // 2) + kc2
                buf, sem = wvb[i % 2], ("dw2", "dw3")[i % 2]
                lastuse = wsem_state.get(id(buf), 0)
                if lastuse:
                    P.wait("gpsimd", "t", lastuse)
                P.op("gpsimd", lambda E, b=buf, s=src, kc2=kc2, eb=eb: E.dma_start(
                    out=b[:].rearrange("p (u m) -> p u m", u=2),
                    in_=s[2 * kc2:2 * kc2 + 2, :, eb * 512:(eb + 1) * 512]
                    .transpose([1, 0, 2])), inc=sem)
                P.wait("tensor", sem)
                for u in range(2):
                    kc = 2 * kc2 + u
                    if kc == 0:
                        for ti in range(len(tcs)):
                            if guards[ti]:
                                P.wait("tensor", guards[ti][0], guards[ti][1])
                    for ti, tc in enumerate(tcs):
                        stop_k = (kc == DC - 1) and (bias_pair is None)
                        P.op("tensor", lambda E, b=pbl[ti], buf=buf, kc=kc, tc=tc,
                             u=u, stop_k=stop_k:
                             E.matmul(b[:, :],
                                      xn_sb[:, kc * xn_tok + t0 + tc * 128:
                                            kc * xn_tok + t0 + (tc + 1) * 128],
                                      buf[:, u * 512:(u + 1) * 512],
                                      start=(kc == 0), stop=stop_k),
                             inc=("t" if ti == len(tcs) - 1 else None))
                wchunk_done(buf, P.v("t"))
            if bias_pair is not None:
                P.wait("tensor", "dg")
            for ti, tc in enumerate(tcs):
                if bias_pair is not None:
                    P.op("tensor", lambda E, b=pbl[ti]: E.matmul(
                        b[:, :], ones_r[:], bias_pair[0][0:1, :],
                        start=False, stop=True), inc="t")
                guards[ti] = evac(tc, eb, pbl[ti])
            bias_use = P.v("t")
        for g in guards:
            if g:
                P.wait("tensor", g[0], g[1])

    P.op("sync", lambda E: E.dma_start(
        out=x_sb[:].rearrange("p (dc t) -> p dc t", dc=DC),
        in_=x_in.rearrange("(dc p) t -> p dc t", p=128)), inc="dy")
    P.op("gpsimd", lambda E: E.dma_start(
        out=bq_sb[:], in_=bq_a.rearrange("(dc p) -> p dc", p=128), allow_slow_non_contiguous=True), inc="dg")
    P.op("gpsimd", lambda E: E.dma_start(
        out=bk_sb[:], in_=bk_a.rearrange("(dc p) -> p dc", p=128), allow_slow_non_contiguous=True), inc="dg")
    P.op("gpsimd", lambda E: E.dma_start(out=ba_sb[:], in_=bal.unsqueeze(1)), inc="dg")
    P.op("gpsimd", lambda E: E.dma_start(out=bb_sb[:], in_=bbe.unsqueeze(1)), inc="dg")
    P.op("gpsimd", lambda E: E.dma_start(
        out=sg_sb[:], in_=sg_in.rearrange("(hc p) -> p hc", p=128), allow_slow_non_contiguous=True), inc="dg")
    P.op("gpsimd", lambda E: E.dma_start(
        out=su_sb[:], in_=su_in.rearrange("(hc p) -> p hc", p=128), allow_slow_non_contiguous=True), inc="dg")
    P.op("gpsimd", lambda E: E.dma_start(
        out=sd_sb[:], in_=sd_in.rearrange("(dc p) -> p dc", p=128), allow_slow_non_contiguous=True), inc="dg")
    P.op("gpsimd", lambda E: E.dma_start(out=gsel_sb[:], in_=gsel_in), inc="dg")
    P.op("gpsimd", lambda E: E.memset(ident[:], 0.0))
    P.op("gpsimd", lambda E: E.affine_select(
        out=ident[:], in_=ident[:], compare_op=ALU.not_equal, fill=1.0,
        base=0, pattern=[[-1, 128]], channel_multiplier=1))
    P.op("gpsimd", lambda E: E.memset(ident32[:], 0.0))
    P.op("gpsimd", lambda E: E.affine_select(
        out=ident32[:], in_=ident32[:], compare_op=ALU.not_equal, fill=1.0,
        base=0, pattern=[[-1, 128]], channel_multiplier=1))
    P.op("gpsimd", lambda E: E.memset(maskneg[:], 0.0))
    P.op("gpsimd", lambda E: E.affine_select(
        out=maskneg[:], in_=maskneg[:], compare_op=ALU.is_ge, fill=-1e30,
        base=0, pattern=[[1, 128]], channel_multiplier=-1))
    P.op("gpsimd", lambda E: E.memset(EH[:], 1.0))
    for h in range(NH):
        P.op("gpsimd", lambda E, h=h: E.affine_select(
            out=EH[:, h * 128:(h + 1) * 128], in_=EH[:, h * 128:(h + 1) * 128],
            compare_op=ALU.is_ge, fill=0.0, base=-256 * h, pattern=[[1, 128]],
            channel_multiplier=256))
        P.op("gpsimd", lambda E, h=h: E.affine_select(
            out=EH[:, h * 128:(h + 1) * 128], in_=EH[:, h * 128:(h + 1) * 128],
            compare_op=ALU.is_ge, fill=0.0, base=255 + 256 * h, pattern=[[-1, 128]],
            channel_multiplier=-256))
    P.op("gpsimd", lambda E: E.memset(ONEH[:], 1.0))
    for h in range(NH):
        P.op("gpsimd", lambda E, h=h: E.affine_select(
            out=ONEH[:, h * 16:(h + 1) * 16], in_=ONEH[:, h * 16:(h + 1) * 16],
            compare_op=ALU.is_ge, fill=0.0, base=-256 * h, pattern=[[256, 16]],
            channel_multiplier=1))
        P.op("gpsimd", lambda E, h=h: E.affine_select(
            out=ONEH[:, h * 16:(h + 1) * 16], in_=ONEH[:, h * 16:(h + 1) * 16],
            compare_op=ALU.is_ge, fill=0.0, base=255 + 256 * h, pattern=[[-256, 16]],
            channel_multiplier=-1))
    P.op("gpsimd", lambda E: E.memset(ones_r[:], 1.0))
    P.op("gpsimd", lambda E: E.memset(ones_rf[:], 1.0))
    P.op("gpsimd", lambda E: E.memset(ones_c[:], 1.0))
    P.op("gpsimd", lambda E: E.memset(c1em6[:], 1e-6))
    P.op("gpsimd", lambda E: E.memset(dummy[:], 0.0))
    P.op("gpsimd", lambda E: E.memset(neg128_r[:], -128.0), inc="g")
    for e in ("tensor", "vector", "scalar"):
        P.wait(e, "g"); P.wait(e, "dg"); P.wait(e, "dy")

    def sublayer_attn_a():
        with contextlib.ExitStack() as ss:
            SE = ss.enter_context
            cla_hh = SE(nc.sbuf_tensor("cla_hh", [16, 512], F32))
            lbmc_hh = SE(nc.sbuf_tensor("lbmc_hh", [16, 512], F32))
            rn_hh = SE(nc.sbuf_tensor("rn_hh", [16, 512], F32))
            scr_hh = SE(nc.sbuf_tensor("scr_hh", [16, 512], F32))
            ends_h = SE(nc.sbuf_tensor("ends_h", [16, 16], F32))
            ecb_t = SE(nc.sbuf_tensor("ecb_t", [16, TOK], BF16))
            eef = SE(nc.sbuf_tensor("eef", [16, 8], F32))
            EH32 = SE(nc.sbuf_tensor("EH32", [16, NH * 128], F32))
            UTRI = SE(nc.sbuf_tensor("UTRI", [128, 128], F32))
            atm = SE(nc.sbuf_tensor("atm", [128, NCH * 16], F32))
            btm = SE(nc.sbuf_tensor("btm", [128, NCH * 16], F32))
            latm = SE(nc.sbuf_tensor("latm", [128, NCH * 16], F32))
            cla_tm = SE(nc.sbuf_tensor("cla_tm", [128, NCH * 16], F32))
            w_tm = SE(nc.sbuf_tensor("w_tm", [128, NCH * 16], F32))
            etd_sb = SE(nc.sbuf_tensor("etd_sb", [128, 128], F32))
            ba_row = SE(nc.sbuf_tensor("ba_row", [1, 2 * NH], F32))
            S_send = SE(nc.sbuf_tensor("S_send", [128, 2049], F32))
            snap_bf = SE(nc.sbuf_tensor("snap_bf", [128, 128], BF16))
            expd_a = SE(nc.sbuf_tensor("expd_a", [128, 128], F32))
            expd_b = SE(nc.sbuf_tensor("expd_b", [128, 128], F32))
            attnf = SE(nc.sbuf_tensor("attnf", [128, 128], BF16))
            kw_sb = SE(nc.sbuf_tensor("kw_sb", [128, 128], BF16))
            qcd = SE(nc.sbuf_tensor("qcd", [128, 128], BF16))
            q_sb = SE(nc.sbuf_tensor("q_sb", [128, NH * TOK], BF16))
            pb = [SE(nc.psum_tensor(f"paa{i}", [128, 512], F32)) for i in range(6)]
            ptr = SE(nc.psum_tensor("ptra", [128, 512], BF16))

            P.op("gpsimd", lambda E: E.memset(EH32[:], 1.0))
            for h in range(NH):
                P.op("gpsimd", lambda E, h=h: E.affine_select(
                    out=EH32[:, h * 128:(h + 1) * 128],
                    in_=EH32[:, h * 128:(h + 1) * 128],
                    compare_op=ALU.is_ge, fill=0.0, base=-256 * h,
                    pattern=[[1, 128]], channel_multiplier=256))
                P.op("gpsimd", lambda E, h=h: E.affine_select(
                    out=EH32[:, h * 128:(h + 1) * 128],
                    in_=EH32[:, h * 128:(h + 1) * 128],
                    compare_op=ALU.is_ge, fill=0.0, base=255 + 256 * h,
                    pattern=[[-1, 128]], channel_multiplier=-256))
            P.op("gpsimd", lambda E: E.memset(UTRI[:], 1.0))
            P.op("gpsimd", lambda E: E.affine_select(
                out=UTRI[:], in_=UTRI[:], compare_op=ALU.is_ge, fill=0.0,
                base=0, pattern=[[1, 128]], channel_multiplier=-1))
            P.op("gpsimd", lambda E: E.memset(S_send[:], 0.0), inc="g")
            P.op("gpsimd", lambda E: E.dma_start(
                out=ba_row[:, 0:NH], in_=bal.unsqueeze(0)), inc="dg")
            P.op("gpsimd", lambda E: E.dma_start(
                out=ba_row[:, NH:], in_=bbe.unsqueeze(0)), inc="dg")
            P.wait("vector", "g")
            P.wait("tensor", "g")
            P.wait("tensor", "dg")

            st_a = tuple(SE(nc.sbuf_tensor(f"sta{i}", [1, sz], F32))
                         for i, sz in enumerate((TOK, TOK, 512, 512)))
            inner = contextlib.ExitStack()
            IE = inner.enter_context
            bv_row = IE(nc.sbuf_tensor("bv_row", [1, 512], BF16))
            xb_s = IE(nc.sbuf_tensor("a_xb", [128, 512], BF16))
            xq_s = IE(nc.sbuf_tensor("a_xq", [128, 512], BF16))
            ksq = xq_s

            emit_ln_stats(pb[0:4], xb_s, xq_s, st_a)

            wb0 = IE(nc.sbuf_tensor("wba0", [128, DC * 128], BF16))
            wb = [wb0, wb0]
            wvb = [IE(nc.sbuf_tensor(f"wva{i}", [128, 1024], BF16)) for i in range(2)]
            wab = IE(nc.sbuf_tensor("wab", [128, 2 * DC * 16], BF16))
            wsems = ["dw0", "dw1"]
            P.op("gpsimd", lambda E: E.dma_start(
                out=wab[:, 0:DC * 16].rearrange("p (dc m) -> p dc m", dc=DC),
                in_=Wal.rearrange("(dc p) m -> p dc m", p=128)), inc="dg")
            P.op("gpsimd", lambda E: E.dma_start(
                out=wab[:, DC * 16:].rearrange("p (dc m) -> p dc m", dc=DC),
                in_=Wbe.rearrange("(dc p) m -> p dc m", p=128)), inc="dg")

            xn = IE(nc.sbuf_tensor("xn", [128, DC * 512], BF16))
            k_sb = IE(nc.sbuf_tensor("k_sb", [128, NH * 512], BF16))
            v_sb = IE(nc.sbuf_tensor("v_sb", [128, 4 * D], BF16))

            for half in range(2):
                t0, t1 = half * 512, (half + 1) * 512
                emit_ln_norm(pb[0:4], xn, lambda dc, a, t0=t0: dc * 512 + (a - t0),
                             t0, t1, st_a)

                if half == 0:
                    tap("xn0", xn[:])
                def evac_q(ec, tb, bank, t0=t0):
                    P.wait("scalar", "t", P.v("t"))
                    P.op("scalar", lambda E, ec=ec, b=bank, t0=t0: E.activation(
                        q_sb[:, ec * TOK + t0: ec * TOK + t0 + 512],
                        b[:, :], AF.Identity, bias=bq_sb[:, ec:ec + 1]), inc="s")
                    return ("s", P.v("s"))
                emit_proj_fm(xn, 512, Wq_a, wb, wsems, list(range(DC)), 0, 512,
                             pb[0:2], evac_q)

                if half == 0:
                    tap("q0", q_sb[:])
                P.wait("tensor", "dg")
                P.wait("tensor", "v")
                for q_i in range(2):
                    w0 = q_i * DC * 16
                    for nl in range(4):
                        n = half * 4 + nl
                        bank = pb[2 + (nl % 2)]
                        for kc in range(DC):
                            P.op("tensor", lambda E, b=bank, w0=w0, kc=kc, nl=nl:
                                 E.matmul(b[:, 0:16],
                                          xn[:, kc * 512 + nl * 128:
                                             kc * 512 + (nl + 1) * 128],
                                          wab[:, w0 + kc * 16: w0 + (kc + 1) * 16],
                                          start=(kc == 0), stop=False))
                        P.op("tensor", lambda E, b=bank, q_i=q_i: E.matmul(
                            b[:, 0:16], ones_rf[:],
                            ba_row[0:1, q_i * NH:(q_i + 1) * NH],
                            start=False, stop=True), inc="t")
                        P.wait("scalar", "t")
                        if q_i == 0:
                            P.op("scalar", lambda E, b=bank, n=n: E.activation(
                                atm[:, n * 16:(n + 1) * 16], b[:, 0:16], AF.Sigmoid),
                                inc="s")
                        else:
                            P.op("scalar", lambda E, b=bank, n=n: E.activation(
                                btm[:, n * 16:(n + 1) * 16], b[:, 0:16], AF.Identity),
                                inc="s")
                        P.wait("tensor", "s")
                sl_h = slice(half * 4 * 16, (half + 1) * 4 * 16)
                P.op("scalar", lambda E, sl=sl_h: E.activation(
                    latm[:, sl], btm[:, sl], AF.Abs))
                P.op("scalar", lambda E, sl=sl_h: E.activation(
                    latm[:, sl], latm[:, sl], AF.Exp, scale=-1.0))
                P.op("scalar", lambda E, sl=sl_h: E.activation(
                    latm[:, sl], latm[:, sl], AF.Ln, bias=1.0), inc="s")
                P.wait("vector", "s")
                P.op("vector", lambda E, sl=sl_h: E.tensor_scalar_max(
                    btm[:, sl], btm[:, sl], 0.0), inc="v")
                vdrain()
                P.op("vector", lambda E, sl=sl_h: E.tensor_add(
                    btm[:, sl], btm[:, sl], latm[:, sl]), inc="v")
                vdrain()
                P.wait("scalar", "v")
                P.op("scalar", lambda E, sl=sl_h: E.activation(
                    latm[:, sl], atm[:, sl], AF.Ln, bias=1e-6), inc="s")
                P.wait("tensor", "s")
                for nl in range(4):
                    n = half * 4 + nl
                    P.op("tensor", lambda E, n=n, nl=nl: E.matmul(
                        pb[4][:, nl * 16:(nl + 1) * 16], UTRI[:],
                        latm[:, n * 16:(n + 1) * 16], start=True, stop=True),
                        inc=("t" if nl == 3 else None))
                P.wait("vector", "t")
                for nl in range(4):
                    n = half * 4 + nl
                    P.op("vector", lambda E, n=n, nl=nl: E.tensor_copy(
                        cla_tm[:, n * 16:(n + 1) * 16],
                        pb[4][:, nl * 16:(nl + 1) * 16]),
                        inc=("v" if nl == 3 else None))
                P.wait("tensor", "v")
                for nl in range(4):
                    n = half * 4 + nl
                    P.op("tensor", lambda E, n=n, nl=nl: E.transpose(
                        pb[5][0:16, nl * 128:(nl + 1) * 128],
                        cla_tm[:, n * 16:(n + 1) * 16], ident32[:]),
                        inc=("t" if nl == 3 else None))
                P.wait("vector", "t")
                for nl in range(4):
                    P.op("vector", lambda E, nl=nl: E.tensor_copy(
                        cla_hh[:, nl * 128:(nl + 1) * 128],
                        pb[5][0:16, nl * 128:(nl + 1) * 128]),
                        inc=("v" if nl == 3 else None))
                vdrain()
                cends = cla_hh[:].rearrange("h (n c) -> h n c", c=128)[:, :, 127]
                P.op("vector", lambda E, half=half, cends=cends: E.tensor_copy(
                    ends_h[:, half * 4:(half + 1) * 4], cends), inc="v")
                vdrain()
                if half == 0:
                    tap("clh", cla_hh[:])
                    tap("ctm", cla_tm[:])
                P.wait("scalar", "v")
                P.op("scalar", lambda E, t0=t0, t1=t1: E.activation(
                    ecb_t[:, t0:t1], cla_hh[:], AF.Exp), inc="s")
                eendsh = ecb_t[:, 0:TOK].rearrange(
                    "h (n c) -> h n c", c=128)[:, half * 4:(half + 1) * 4, 127]
                P.wait("vector", "s")
                P.op("vector", lambda E, half=half, eendsh=eendsh: E.tensor_copy(
                    eef[:, half * 4:(half + 1) * 4], eendsh), inc="v")
                vdrain()
                P.wait("tensor", "s")
                P.wait("tensor", "v")
                for h in range(NH):
                    P.op("tensor", lambda E, h=h, half=half: E.matmul(
                        pb[4][:, h * 8 + half * 4: h * 8 + half * 4 + 4],
                        EH32[:, h * 128:(h + 1) * 128],
                        eef[:, half * 4:(half + 1) * 4],
                        start=(h == 0), stop=(h == NH - 1)),
                        inc=("t" if h == NH - 1 else None))
                P.wait("vector", "t")
                for h in range(NH):
                    P.op("vector", lambda E, h=h, half=half: E.tensor_copy(
                        etd_sb[:, h * 8 + half * 4: h * 8 + half * 4 + 4],
                        pb[4][:, h * 8 + half * 4: h * 8 + half * 4 + 4]),
                        inc=("v" if h == NH - 1 else None))
                vdrain()

                def evac_k(ec, tb, bank):
                    P.wait("scalar", "t", P.v("t"))
                    P.op("scalar", lambda E, ec=ec, b=bank: E.activation(
                        k_sb[:, ec * 512:(ec + 1) * 512], b[:, :], AF.Identity,
                        bias=bk_sb[:, ec:ec + 1]), inc="s")
                    return ("s", P.v("s"))
                emit_proj_fm(xn, 512, Wk_a, wb, wsems, list(range(DC)), 0, 512,
                             pb[0:2], evac_k)

                def evac_v(tc, eb, bank):
                    P.wait("vector", "t", P.v("t"))
                    P.op("vector", lambda E, b=bank, tc=tc, eb=eb: E.tensor_copy(
                        v_sb[:, tc * D + eb * 512: tc * D + (eb + 1) * 512], b[:, :]),
                        inc="v")
                    return ("v", P.v("v"))
                emit_proj_tm(xn, 512, Wv_a, wvb, 0, list(range(4)), pb[0:4],
                             (bv_row, bv_a), evac_v)

                prev_t = 0
                for h in range(NH):
                    if prev_t:
                        P.wait("scalar", "t", prev_t)
                    P.wait("scalar", "v")
                    P.op("scalar", lambda E, h=h: E.activation(
                        ksq[:], k_sb[:, h * 512:(h + 1) * 512], AF.Square), inc="s")
                    P.wait("tensor", "s")
                    P.op("tensor", lambda E, h=h: E.matmul(
                        pb[4][0:16, :], ONEH[:, h * 16:(h + 1) * 16], ksq[:],
                        start=(h == 0), stop=(h == NH - 1)), inc="t")
                    prev_t = P.v("t")
                P.wait("vector", "t")
                P.op("vector", lambda E: E.tensor_scalar_max(
                    scr_hh[:], pb[4][0:16, :], 1e-24), inc="v")
                rsqrt_refined(rn_hh[:], scr_hh[:], lbmc_hh[:])
                P.wait("tensor", "v")
                for nl in range(4):
                    n = half * 4 + nl
                    P.op("tensor", lambda E, n=n, nl=nl: E.transpose(
                        pb[5][0:16, nl * 128:(nl + 1) * 128],
                        btm[:, n * 16:(n + 1) * 16], ident32[:]),
                        inc=("t" if nl == 3 else None))
                P.wait("vector", "t")
                for nl in range(4):
                    P.op("vector", lambda E, nl=nl: E.tensor_copy(
                        scr_hh[:, nl * 128:(nl + 1) * 128],
                        pb[5][0:16, nl * 128:(nl + 1) * 128]),
                        inc=("v" if nl == 3 else None))
                vdrain()
                P.op("vector", lambda E: E.tensor_mul(scr_hh[:], scr_hh[:], rn_hh[:]),
                     inc="v")
                vdrain()
                P.wait("scalar", "v")
                P.op("scalar", lambda E: E.activation(lbmc_hh[:], scr_hh[:], AF.Ln),
                     inc="s")
                P.wait("vector", "s")
                P.op("vector", lambda E: E.tensor_sub(lbmc_hh[:], lbmc_hh[:],
                                                      cla_hh[:]), inc="v")
                vdrain()
                for nl in range(4):
                    ce = cla_hh[:, nl * 128 + 127: nl * 128 + 128]
                    P.op("vector", lambda E, nl=nl, ce=ce: E.tensor_add(
                        scr_hh[:, nl * 128:(nl + 1) * 128],
                        lbmc_hh[:, nl * 128:(nl + 1) * 128],
                        ce.broadcast_to((16, 128))),
                        inc=("v" if nl == 3 else None))
                vdrain()
                P.wait("scalar", "v")
                P.op("scalar", lambda E: E.activation(scr_hh[:], scr_hh[:], AF.Exp),
                     inc="s")
                P.wait("tensor", "s")
                for nl in range(4):
                    n = half * 4 + nl
                    P.op("tensor", lambda E, n=n, nl=nl: E.transpose(
                        pb[5][:, nl * 16:(nl + 1) * 16],
                        scr_hh[:, nl * 128:(nl + 1) * 128], ident32[0:16, 0:16]),
                        inc=("t" if nl == 3 else None))
                P.wait("vector", "t")
                for nl in range(4):
                    n = half * 4 + nl
                    P.op("vector", lambda E, n=n, nl=nl: E.tensor_copy(
                        w_tm[:, n * 16:(n + 1) * 16], pb[5][:, nl * 16:(nl + 1) * 16]),
                        inc=("v" if nl == 3 else None))
                vdrain()

                if half == 0:
                    tap("v0", v_sb[:])
                for nl in range(4):
                    n = half * 4 + nl
                    if half == 0 and nl == 1:
                        tap("s0", S_send[:, 0:2048])
                    for h in range(NH):
                        qt = q_sb[:, h * TOK + n * 128: h * TOK + (n + 1) * 128]
                        kt = k_sb[:, h * 512 + nl * 128: h * 512 + (nl + 1) * 128]
                        vt = v_sb[:, nl * D + h * 128: nl * D + (h + 1) * 128]
                        clat = cla_hh[:, nl * 128:(nl + 1) * 128]
                        ecbt = ecb_t[:, n * 128:(n + 1) * 128]
                        lbt = lbmc_hh[:, nl * 128:(nl + 1) * 128]
                        ehs = EH[:, h * 128:(h + 1) * 128]
                        ehs32 = EH32[:, h * 128:(h + 1) * 128]
                        P.op("vector", lambda E, h=h: E.tensor_copy(
                            snap_bf[:], S_send[:, h * 128:(h + 1) * 128]), inc="v")
                        P.wait("tensor", "v")
                        P.op("tensor", lambda E, kt=kt: E.transpose(
                            ptr[:, 0:128], kt, ident[:]))
                        P.op("tensor", lambda E, kt=kt, qt=qt: E.matmul(
                            pb[0][:, 0:128], kt, qt, start=True, stop=True))
                        P.op("tensor", lambda E, lbt=lbt, ehs32=ehs32: E.matmul(
                            pb[1][:, 0:128], lbt, ehs32, start=True, stop=False))
                        P.op("tensor", lambda E, clat=clat, ehs32=ehs32: E.matmul(
                            pb[1][:, 0:128], ehs32, clat, start=False, stop=True))
                        P.op("tensor", lambda E, ecbt=ecbt, ehs=ehs: E.matmul(
                            pb[4][:, 0:128], ehs, ecbt, start=True, stop=True),
                            inc="t")
                        P.wait("vector", "t")
                        P.op("vector", lambda E: E.tensor_add(
                            expd_a[:], pb[1][:, 0:128], maskneg[:]), inc="v")
                        P.wait("scalar", "v")
                        P.op("scalar", lambda E: E.activation(
                            expd_b[:], expd_a[:], AF.Exp), inc="s")
                        P.wait("vector", "s")
                        P.op("vector", lambda E: E.tensor_mul(
                            attnf[:], pb[0][:, 0:128], expd_b[:]))
                        P.op("vector", lambda E, n=n, h=h: E.tensor_scalar_mul(
                            kw_sb[:], ptr[:, 0:128],
                            w_tm[:, n * 16 + h: n * 16 + h + 1]))
                        P.op("vector", lambda E, qt=qt: E.tensor_mul(
                            qcd[:], qt, pb[4][:, 0:128]), inc="v")
                        P.wait("tensor", "v")
                        P.op("tensor", lambda E, vt=vt: E.matmul(
                            pb[2][:, 0:128], kw_sb[:], vt, start=True, stop=True))
                        P.op("tensor", lambda E, vt=vt: E.matmul(
                            pb[3][:, 0:128], vt, attnf[:], start=True, stop=False))
                        P.op("tensor", lambda E: E.matmul(
                            pb[3][:, 0:128], snap_bf[:], qcd[:],
                            start=False, stop=True), inc="t")
                        P.wait("vector", "t")
                        P.op("vector", lambda E, h=h, n=n: E.tensor_add(
                            xs(h, n * 128, (n + 1) * 128),
                            xs(h, n * 128, (n + 1) * 128), pb[3][:, 0:128]))
                        P.op("vector", lambda E, h=h, n=n: E.scalar_tensor_tensor(
                            S_send[:, h * 128:(h + 1) * 128],
                            S_send[:, h * 128:(h + 1) * 128],
                            etd_sb[:, h * 8 + n: h * 8 + n + 1],
                            pb[2][:, 0:128], ALU.mult, ALU.add), inc="v")

            P.wait("gpsimd", "dw0"); P.wait("gpsimd", "dw2"); P.wait("gpsimd", "dw3")
            inner.close()

            gcb_t = SE(nc.sbuf_tensor("gcb_t", [16, TOK], F32))
            tap("end", ends_h[:])
            P.op("vector", lambda E: E.tensor_reduce(
                S_send[0:16, 2048:2049], ends_h[:, 0:8], AX.X, ALU.add), inc="v")
            vdrain()
            P.wait("gpsimd", "v")
            P.op("gpsimd", lambda E: E.memset(ends_h[:, 8:16], 0.0), inc="g")
            P.wait("vector", "g")
            P.op("vector", lambda E: E.tensor_tensor_scan(
                latm[0:16, 0:8], ends_h[:, 0:8], ends_h[:, 8:16], 0.0,
                ALU.add, ALU.add), inc="v")
            vdrain()
            P.op("vector", lambda E: E.tensor_copy(ends_h[:, 9:16], latm[0:16, 0:7]),
                 inc="v")
            vdrain()
            for half in range(2):
                P.wait("tensor", "v")
                for nl in range(4):
                    n = half * 4 + nl
                    P.op("tensor", lambda E, n=n, nl=nl: E.transpose(
                        pb[5][0:16, nl * 128:(nl + 1) * 128],
                        cla_tm[:, n * 16:(n + 1) * 16], ident32[:]),
                        inc=("t" if nl == 3 else None))
                P.wait("vector", "t")
                for nl in range(4):
                    n = half * 4 + nl
                    P.op("vector", lambda E, n=n, nl=nl: E.tensor_add(
                        cla_hh[:, nl * 128:(nl + 1) * 128],
                        pb[5][0:16, nl * 128:(nl + 1) * 128],
                        ends_h[:, 8 + n: 9 + n].broadcast_to((16, 128))),
                        inc=("v" if nl == 3 else None))
                P.wait("scalar", "v")
                P.op("scalar", lambda E, half=half: E.activation(
                    gcb_t[:, half * 512:(half + 1) * 512], cla_hh[:], AF.Exp),
                    inc="s")
                P.wait("tensor", "s")

            P.wait("gpsimd", "v")
            P.op("gpsimd", lambda E: E.dma_start(out=ssend_d.ap()[:, :], in_=S_send[:]),
                 inc="dg")
            P.wait("gpsimd", "dg")
            P.op("gpsimd", lambda E: E.collective_compute(
                "AllGather", ALU.bypass, replica_groups=[[0, 1, 2, 3], [4, 5, 6, 7]],
                ins=[ssend_d.ap().opt()], outs=[sgath_d.ap().opt()]), inc="cc")
            P.wait("gpsimd", "cc")
            P.wait("sync", "cc")

            S_in = SE(nc.sbuf_tensor("S_in", [128, 2048], F32))
            S_in_bf = SE(nc.sbuf_tensor("S_in_bf", [128, 2048], BF16))
            gla = SE(nc.sbuf_tensor("gla", [16, 16], F32))
            fc_sb = SE(nc.sbuf_tensor("fc_sb", [128, 64], F32))
            gbuf = [SE(nc.sbuf_tensor(f"gbuf{i}", [128, 2049], F32)) for i in range(2)]
            for p_ in range(4):
                P.op("sync", lambda E, p_=p_: E.dma_start(
                    out=gla[:, p_:p_ + 1], in_=sgath_d.ap()[p_, 0:16, 2048:2049],
                    allow_slow_non_contiguous=True), inc="dy")
            P.wait("vector", "dy")
            P.op("vector", lambda E: E.tensor_mul(gla[:, 4:8], gla[:, 0:4], gsel_sb[:]),
                 inc="v")
            vdrain()
            P.wait("gpsimd", "v")
            P.op("gpsimd", lambda E: E.memset(gla[:, 11:12], 0.0), inc="g")
            P.wait("vector", "g")
            P.op("vector", lambda E: E.tensor_copy(gla[:, 10:11], gla[:, 7:8]),
                 inc="v")
            vdrain()
            P.op("vector", lambda E: E.tensor_add(gla[:, 9:10], gla[:, 10:11],
                                                  gla[:, 6:7]), inc="v")
            vdrain()
            P.op("vector", lambda E: E.tensor_add(gla[:, 8:9], gla[:, 9:10],
                                                  gla[:, 5:6]), inc="v")
            vdrain()
            P.wait("scalar", "v")
            P.op("scalar", lambda E: E.activation(gla[:, 12:16], gla[:, 8:12],
                                                  AF.Exp), inc="s")
            P.wait("vector", "s")
            P.op("vector", lambda E: E.tensor_mul(gla[:, 12:16], gla[:, 12:16],
                                                  gsel_sb[:]), inc="v")
            vdrain()
            P.wait("tensor", "v")
            for h in range(NH):
                P.op("tensor", lambda E, h=h: E.matmul(
                    pb[4][:, h * 4:(h + 1) * 4], EH32[:, h * 128:(h + 1) * 128],
                    gla[:, 12:16], start=(h == 0), stop=(h == NH - 1)),
                    inc=("t" if h == NH - 1 else None))
            P.wait("vector", "t")
            P.op("vector", lambda E: E.tensor_copy(fc_sb[:], pb[4][:, 0:64]), inc="v")
            P.wait("gpsimd", "v")
            P.op("gpsimd", lambda E: E.memset(S_in[:], 0.0), inc="g")
            P.wait("vector", "g")
            gb_wait = [0, 0]
            for p_ in range(4):
                b = gbuf[p_ % 2]
                if gb_wait[p_ % 2]:
                    P.wait("sync", "v", gb_wait[p_ % 2])
                P.op("sync", lambda E, p_=p_, b=b: E.dma_start(
                    out=b[:], in_=sgath_d.ap()[p_]), inc="dy")
                P.wait("vector", "dy")
                for h in range(NH):
                    P.op("vector", lambda E, b=b, h=h, p_=p_: E.scalar_tensor_tensor(
                        S_in[:, h * 128:(h + 1) * 128], b[:, h * 128:(h + 1) * 128],
                        fc_sb[:, h * 4 + p_: h * 4 + p_ + 1],
                        S_in[:, h * 128:(h + 1) * 128], ALU.mult, ALU.add),
                        inc=("v" if h == NH - 1 else None))
                vdrain()
                gb_wait[p_ % 2] = P.v("v")
            vdrain()
            P.op("vector", lambda E: E.tensor_copy(S_in_bf[:], S_in[:]), inc="v")
            tap("sin", S_in[:])
            tap("gla", gla[:])
            tap("gsl", gsel_sb[:])
            tap("fc", fc_sb[:])
            tap("gcb", gcb_t[:])
            tap("ehh", ends_h[:])

            for n in range(NCH):
                for h in range(NH):
                    qt = q_sb[:, h * TOK + n * 128: h * TOK + (n + 1) * 128]
                    gcbt = gcb_t[:, n * 128:(n + 1) * 128]
                    ehs32 = EH32[:, h * 128:(h + 1) * 128]
                    P.wait("tensor", "v")
                    P.op("tensor", lambda E, gcbt=gcbt, ehs32=ehs32: E.matmul(
                        pb[0][:, 0:128], ehs32, gcbt, start=True, stop=True), inc="t")
                    P.wait("vector", "t")
                    P.op("vector", lambda E, qt=qt: E.tensor_mul(
                        qcd[:], qt, pb[0][:, 0:128]), inc="v")
                    P.wait("tensor", "v")
                    P.op("tensor", lambda E, h=h: E.matmul(
                        pb[1][:, 0:128], S_in_bf[:, h * 128:(h + 1) * 128], qcd[:],
                        start=True, stop=True), inc="t")
                    P.wait("vector", "t")
                    P.op("vector", lambda E, h=h, n=n: E.tensor_add(
                        xs(h, n * 128, (n + 1) * 128),
                        xs(h, n * 128, (n + 1) * 128), pb[1][:, 0:128]), inc="v")
        barrier()

    def sublayer_ffn(tag):
        with contextlib.ExitStack() as ss:
            SE = ss.enter_context
            csum = SE(nc.sbuf_tensor(f"csum{tag}", [1, TOK], BF16))
            msum = SE(nc.sbuf_tensor(f"msum{tag}", [1, TOK], BF16))
            t1sb = [SE(nc.sbuf_tensor(f"t1sb{tag}{i}", [128, 512], F32))
                    for i in range(2)]
            midst = [SE(nc.sbuf_tensor(f"mst{tag}{i}", [128, TOK], BF16))
                     for i in range(2)]
            pb = [SE(nc.psum_tensor(f"pf{tag}{i}", [128, 512], F32)) for i in range(6)]
            with contextlib.ExitStack() as xs_scope:
                XL = xs_scope.enter_context
                xn = XL(nc.sbuf_tensor(f"xnf{tag}", [128, DC * TOK], BF16))
                xb_s = XL(nc.sbuf_tensor(f"fxb{tag}", [128, 512], BF16))
                xq_s = XL(nc.sbuf_tensor(f"fxq{tag}", [128, 512], BF16))
                st_f = tuple(XL(nc.sbuf_tensor(f"stf{tag}{i}", [1, sz], F32))
                             for i, sz in enumerate((TOK, TOK, 512, 512)))
                emit_ln_stats(pb[0:4], xb_s, xq_s, st_f)
                emit_ln_norm(pb[0:4], xn, lambda dc, a: dc * TOK + a, 0, TOK, st_f)
                P.wait("tensor", "v")
                for tb in range(2):
                    for kc in range(DC):
                        P.op("tensor", lambda E, tb=tb, kc=kc: E.matmul(
                            pb[4 + tb][0:1, :], ones_c[:],
                            xn[:, kc * TOK + tb * 512: kc * TOK + (tb + 1) * 512],
                            start=(kc == 0), stop=(kc == DC - 1)),
                            inc=("t" if kc == DC - 1 else None))
                P.wait("vector", "t")
                P.op("vector", lambda E: E.tensor_copy(csum[:, 0:512], pb[4][0:1, :]))
                P.op("vector", lambda E: E.tensor_copy(csum[:, 512:1024],
                                                       pb[5][0:1, :]), inc="v")
                with contextlib.ExitStack() as ws:
                    WL = ws.enter_context
                    wg = [WL(nc.sbuf_tensor(f"wg{tag}{i}", [128, DC * 128], BF16))
                          for i in range(2)]
                    wu = [WL(nc.sbuf_tensor(f"wu{tag}{i}", [128, DC * 128], BF16))
                          for i in range(2)]
                    P.wait("tensor", "v")
                    mid_dma = [0, 0]
                    bank_guard = [0, 0]
                    for hc in range(HC):
                        bg, bu = wg[hc % 2], wu[hc % 2]
                        sg_, su_ = ("dw0", "dw1")[hc % 2], ("dw2", "dw3")[hc % 2]
                        if hc == 0:
                            load_wchunk(bg, sg_, WgT, DC, 0, 128)
                            load_wchunk(bu, su_, WuT, DC, 0, 128)
                        P.wait("tensor", sg_)
                        P.wait("tensor", su_)
                        if hc + 1 < HC:
                            load_wchunk(wg[(hc + 1) % 2], ("dw0", "dw1")[(hc + 1) % 2],
                                        WgT, DC, (hc + 1) * 128, 128)
                            load_wchunk(wu[(hc + 1) % 2], ("dw2", "dw3")[(hc + 1) % 2],
                                        WuT, DC, (hc + 1) * 128, 128)
                        mst = midst[hc % 2]
                        for tb in range(2):
                            gbank, ubank = pb[tb * 2], pb[tb * 2 + 1]
                            if bank_guard[tb]:
                                P.wait("tensor", "v", bank_guard[tb])
                            for kc in range(DC):
                                P.op("tensor", lambda E, bg=bg, kc=kc, tb=tb,
                                     b=gbank: E.matmul(
                                         b[:, :], bg[:, kc * 128:(kc + 1) * 128],
                                         xn[:, kc * TOK + tb * 512:
                                            kc * TOK + (tb + 1) * 512],
                                         start=(kc == 0), stop=False))
                            P.op("tensor", lambda E, tb=tb, b=gbank: E.matmul(
                                b[:, :], neg128_r[:],
                                csum[0:1, tb * 512:(tb + 1) * 512],
                                start=False, stop=True), inc="t")
                            for kc in range(DC):
                                P.op("tensor", lambda E, bu=bu, kc=kc, tb=tb,
                                     b=ubank: E.matmul(
                                         b[:, :], bu[:, kc * 128:(kc + 1) * 128],
                                         xn[:, kc * TOK + tb * 512:
                                            kc * TOK + (tb + 1) * 512],
                                         start=(kc == 0), stop=False))
                            P.op("tensor", lambda E, tb=tb, b=ubank: E.matmul(
                                b[:, :], neg128_r[:],
                                csum[0:1, tb * 512:(tb + 1) * 512],
                                start=False, stop=True), inc="t")
                            tu = P.v("t")
                            P.wait("scalar", "t", tu)
                            P.op("scalar", lambda E, hc=hc, tb=tb, b=gbank: E.activation(
                                t1sb[tb][:], b[:, :], AF.Silu,
                                scale=sg_sb[:, hc:hc + 1]), inc="s")
                            P.wait("vector", "s", P.v("s"))
                            if mid_dma[hc % 2]:
                                P.wait("vector", ("vy0", "vy1")[hc % 2],
                                       mid_dma[hc % 2])
                            P.op("vector", lambda E, mst=mst, hc=hc, tb=tb, b=ubank:
                                 E.scalar_tensor_tensor(
                                     mst[:, tb * 512:(tb + 1) * 512],
                                     b[:, :], su_sb[:, hc:hc + 1], t1sb[tb][:],
                                     ALU.mult, ALU.mult), inc="v")
                            bank_guard[tb] = P.v("v")
                        P.wait("sync", "v", P.v("v"))
                        P.op("sync", lambda E, mst=mst, hc=hc: E.dma_start(
                            out=mid_d.ap()[hc], in_=mst[:]),
                            inc=("vy0", "vy1")[hc % 2])
                        mid_dma[hc % 2] = P.v(("vy0", "vy1")[hc % 2])
                        wchunk_done(bg, P.v("t"))
                        wchunk_done(bu, P.v("t"))
            NG = 2
            GH = HC // NG
            with contextlib.ExitStack() as ws:
                WL = ws.enter_context
                P.wait("gpsimd", "t")
                P.wait("sync", "t")
                mgrp = WL(nc.sbuf_tensor(f"mg{tag}", [128, GH * TOK], BF16))
                wd = [WL(nc.sbuf_tensor(f"wd{tag}{i}", [128, GH * 128], BF16))
                      for i in range(2)]
                down_guard = [0, 0]
                P.wait("tensor", "v")
                P.wait("tensor", "s")
                for hg in range(NG):
                    P.wait("sync", ("vy0", "vy1")[0])
                    P.wait("sync", ("vy0", "vy1")[1])
                    if hg > 0:
                        P.wait("sync", "t")
                    P.op("sync", lambda E, hg=hg: E.dma_start(
                        out=mgrp[:].rearrange("p (h t) -> p h t", h=GH),
                        in_=mid_d.ap()[hg * GH:(hg + 1) * GH].transpose([1, 0, 2])),
                        inc="ky0")
                    P.wait("tensor", "ky0")
                    for tb in range(2):
                        for hc in range(GH):
                            P.op("tensor", lambda E, tb=tb, hc=hc, hg=hg: E.matmul(
                                pb[4 + tb][0:1, :], ones_c[:],
                                mgrp[:, hc * TOK + tb * 512: hc * TOK + (tb + 1) * 512],
                                start=(hg == 0 and hc == 0),
                                stop=(hg == NG - 1 and hc == GH - 1)),
                                inc=("t" if hc == GH - 1 else None))
                    if hg == NG - 1:
                        P.wait("vector", "t")
                        P.op("vector", lambda E: E.tensor_copy(msum[:, 0:512],
                                                               pb[4][0:1, :]))
                        P.op("vector", lambda E: E.tensor_copy(msum[:, 512:1024],
                                                               pb[5][0:1, :]), inc="v")
                        P.wait("tensor", "v")
                    for dc in range(DC):
                        bd = wd[dc % 2]
                        sd_ = ("dw0", "dw1")[dc % 2]
                        if dc == 0:
                            load_wchunk_rows(bd, sd_, WdT, GH, hg * GH, dc)
                        P.wait("tensor", sd_)
                        if dc + 1 < DC:
                            load_wchunk_rows(wd[(dc + 1) % 2],
                                             ("dw0", "dw1")[(dc + 1) % 2],
                                             WdT, GH, hg * GH, dc + 1)
                        for tb in range(2):
                            bank = pb[tb]
                            if down_guard[tb]:
                                P.wait("tensor", "v", down_guard[tb])
                            last = NG - 1
                            for hc in range(GH):
                                fin = (hg < last and hc == GH - 1)
                                P.op("tensor",
                                     lambda E, b=bank, bd=bd, hc=hc, tb=tb, fin=fin:
                                     E.matmul(b[:, :], bd[:, hc * 128:(hc + 1) * 128],
                                              mgrp[:, hc * TOK + tb * 512:
                                                   hc * TOK + (tb + 1) * 512],
                                              start=(hc == 0), stop=fin),
                                     inc=("t" if fin else None))
                            if hg == last:
                                P.op("tensor", lambda E, b=bank, tb=tb: E.matmul(
                                    b[:, :], neg128_r[:],
                                    msum[0:1, tb * 512:(tb + 1) * 512],
                                    start=False, stop=True), inc="t")
                            P.wait("vector", "t", P.v("t"))
                            P.op("vector", lambda E, b=bank, dc=dc, tb=tb:
                                 E.scalar_tensor_tensor(
                                     xs(dc, tb * 512, (tb + 1) * 512), b[:, :],
                                     sd_sb[:, dc:dc + 1],
                                     xs(dc, tb * 512, (tb + 1) * 512),
                                     ALU.mult, ALU.add), inc="v")
                            down_guard[tb] = P.v("v")
                        wchunk_done(bd, P.v("t"))
        barrier()

    def sublayer_attn_b():
        with contextlib.ExitStack() as ss:
            SE = ss.enter_context
            q_sb = SE(nc.sbuf_tensor("qb_sb", [128, DC * TOK], BF16))
            PT = SE(nc.sbuf_tensor("PT", [128, 32 * 512], BF16))
            with contextlib.ExitStack() as ps:
                PL = ps.enter_context
                stg = PL(nc.sbuf_tensor("stg", [128, 512], BF16))
                xn = PL(nc.sbuf_tensor("xnb", [128, DC * TOK], BF16))
                pbs = [PL(nc.psum_tensor(f"pbb{i}", [128, 512], F32)) for i in range(8)]
                xb_s = PL(nc.sbuf_tensor("bxb", [128, 512], BF16))
                xq_s = PL(nc.sbuf_tensor("bxq", [128, 512], BF16))
                st_b = tuple(PL(nc.sbuf_tensor(f"stb{i}", [1, sz], F32))
                             for i, sz in enumerate((TOK, TOK, 512, 512)))
                emit_ln_stats(pbs[0:4], xb_s, xq_s, st_b)
                emit_ln_norm(pbs[0:4], xn, lambda dc, a: dc * TOK + a, 0, TOK, st_b)
                wb = [PL(nc.sbuf_tensor(f"wbb{i}", [128, DC * 128], BF16))
                      for i in range(2)]
                wvb = [PL(nc.sbuf_tensor(f"wvbb{i}", [128, 1024], BF16))
                       for i in range(2)]
                wsems = ["dw0", "dw1"]


                def evac_kb(ec, tb, bank):
                    P.wait("vector", "t", P.v("t"))
                    P.wait("vector", "dg")
                    P.op("vector", lambda E, b=bank: E.tensor_copy(stg[:], b[:, :]),
                         inc="v")
                    P.wait("gpsimd", "v", P.v("v"))
                    P.op("gpsimd", lambda E, ec=ec, tb=tb: E.dma_start(
                        out=kown_d.ap()[ec * 128:(ec + 1) * 128,
                                        tb * 512:(tb + 1) * 512],
                        in_=stg[:]), inc="dg")
                    return ("v", P.v("v"))
                emit_proj_fm(xn, TOK, Wk_b, wb, wsems, list(range(DC)), 0, TOK,
                             pbs[0:2], evac_kb)

                def evac_vb(tc, eb, bank):
                    P.wait("vector", "t", P.v("t"))
                    P.wait("vector", "dg")
                    P.op("vector", lambda E, b=bank: E.tensor_copy(stg[:], b[:, :]),
                         inc="v")
                    P.wait("gpsimd", "v", P.v("v"))
                    P.op("gpsimd", lambda E, tc=tc, eb=eb: E.dma_start(
                        out=vown_d.ap()[tc * 128:(tc + 1) * 128,
                                        eb * 512:(eb + 1) * 512],
                        in_=stg[:]), inc="dg")
                    return ("v", P.v("v"))
                emit_proj_tm(xn, TOK, Wv_b, wvb, 0, list(range(8)), pbs[0:8], None,
                             evac_vb)
                P.wait("gpsimd", "dg")
                P.op("gpsimd", lambda E: E.collective_compute(
                    "AllGather", ALU.bypass,
                    replica_groups=[[0, 1, 2, 3], [4, 5, 6, 7]],
                    ins=[kown_d.ap().opt()], outs=[kfull_d.ap().opt()]), inc="cc")
                ccK = P.v("cc")
                P.op("gpsimd", lambda E: E.collective_compute(
                    "AllGather", ALU.bypass,
                    replica_groups=[[0, 1, 2, 3], [4, 5, 6, 7]],
                    ins=[vown_d.ap().opt()], outs=[vfull_d.ap().opt()]), inc="cc")
                ccV = P.v("cc")

                def evac_qb(ec, tb, bank):
                    P.wait("scalar", "t", P.v("t"))
                    P.op("scalar", lambda E, ec=ec, tb=tb, b=bank: E.activation(
                        q_sb[:, ec * TOK + tb * 512: ec * TOK + (tb + 1) * 512],
                        b[:, :], AF.Identity, scale=QSCALE), inc="s")
                    return ("s", P.v("s"))
                emit_proj_fm(xn, TOK, Wq_b, wb, wsems, list(range(DC)), 0, TOK,
                             pbs[0:2], evac_qb)

            bphase = int(os.environ.get("MK_DEV_BPH", "4"))
            for half in range(2 if bphase >= 4 else 1):
                if bphase < 2:
                    break
                with contextlib.ExitStack() as sp:
                    SL = sp.enter_context
                    scsb = [SL(nc.sbuf_tensor(f"sc{half}{i}", [128, 4096], F32))
                            for i in range(2)]
                    psb = SL(nc.sbuf_tensor(f"ps{half}", [128, 4096], BF16))
                    rowst = SL(nc.sbuf_tensor(f"row{half}", [128, 8], F32))
                    kblk = [SL(nc.sbuf_tensor(f"kb{half}{i}", [128, DC * 512], BF16))
                            for i in range(2)]
                    pb2 = [SL(nc.psum_tensor(f"pbs{half}{i}", [128, 512], F32))
                           for i in range(2)]
                    ptrb2 = [SL(nc.psum_tensor(f"ptrb{half}{i}", [128, 512], BF16))
                             for i in range(2)]
                    kb_wait = [0, 0]
                    ptr_wait = [0] * 8
                    P.wait("tensor", "v")
                    P.wait("sync", "cc", ccK)
                    P.wait("sync", "t")
                    P.wait("sync", "v")
                    P.wait("sync", "s")
                    for pr in range(2):
                        qcs = [half * 4 + pr * 2, half * 4 + pr * 2 + 1]
                        for kb in range(8):
                            bi = kb % 2
                            if kb_wait[bi]:
                                P.wait("sync", "t", kb_wait[bi])
                            g, tl = kb // 2, (kb % 2) * 512
                            P.op("sync", lambda E, bi=bi, g=g, tl=tl: E.dma_start(
                                out=kblk[bi][:].rearrange("p (kc m) -> p kc m", kc=DC),
                                in_=kfull_d.ap()[g]
                                .rearrange("(kc p) t -> kc p t", p=128)
                                [:, :, tl:tl + 512].transpose([1, 0, 2])),
                                inc=("ky0", "ky1")[bi])
                            P.wait("tensor", ("ky0", "ky1")[bi])
                            for qi, qc in enumerate(qcs):
                                for kc in range(DC):
                                    P.op("tensor",
                                         lambda E, qi=qi, qc=qc, kc=kc, bi=bi:
                                         E.matmul(pb2[qi][:, :],
                                                  q_sb[:, kc * TOK + qc * 128:
                                                       kc * TOK + (qc + 1) * 128],
                                                  kblk[bi][:,
                                                           kc * 512:(kc + 1) * 512],
                                                  start=(kc == 0),
                                                  stop=(kc == DC - 1)),
                                         inc=("t" if kc == DC - 1 else None))
                            kb_wait[bi] = P.v("t")
                            P.wait("vector", "t")
                            for qi in range(2):
                                P.op("vector", lambda E, qi=qi, kb=kb: E.tensor_copy(
                                    scsb[qi][:, kb * 512:(kb + 1) * 512],
                                    pb2[qi][:, :]), inc="v")
                            P.wait("tensor", "v")
                        vdrain()
                        for qi, qc in enumerate(qcs):
                            qloc = qc % 4
                            P.op("vector", lambda E, qi=qi, qloc=qloc: E.reduce_max(
                                rowst[:, qloc:qloc + 1], scsb[qi][:], axis=AX.X,
                                negate=True), inc="v")
                            vdrain()
                            P.wait("scalar", "v")
                            P.wait("scalar", "t")
                            if os.environ.get("MK_NOSM"):
                                P.op("scalar", lambda E, qi=qi: E.activation(
                                    psb[:], scsb[qi][:], AF.Identity), inc="s")
                            else:
                                P.op("scalar", lambda E, qi=qi, qloc=qloc: E.activation(
                                    psb[:], scsb[qi][:], AF.Exp,
                                    bias=rowst[:, qloc:qloc + 1],
                                    accum_out=rowst[:, 4 + qloc:5 + qloc]), inc="s")
                            P.wait("vector", "s")
                            P.op("vector", lambda E, qloc=qloc: E.reciprocal(
                                rowst[:, 4 + qloc:5 + qloc],
                                rowst[:, 4 + qloc:5 + qloc]), inc="v")
                            vdrain()
                            P.op("vector", lambda E, qloc=qloc: E.tensor_scalar_mul(
                                psb[:], psb[:], rowst[:, 4 + qloc:5 + qloc]),
                                inc="v")
                            vdrain()
                            P.wait("tensor", "v")
                            evacv = ptr_wait
                            for kc in range(0 if os.environ.get("MK_NOTR") else 32):
                                bk = kc % 2
                                sl = (kc // 2) % 4
                                if evacv[(kc - 2) % 8] and kc >= 2:
                                    P.wait("tensor", "v", evacv[(kc - 2) % 8])
                                elif evacv[(kc - 2) % 8]:
                                    P.wait("tensor", "v", evacv[(kc - 2) % 8])
                                P.op("tensor", lambda E, kc=kc, sl=sl, bk=bk:
                                     E.transpose(
                                         ptrb2[bk][:, sl * 128:(sl + 1) * 128],
                                         psb[:, kc * 128:(kc + 1) * 128], ident[:]),
                                     inc="t")
                                P.wait("vector", "t")
                                P.op("vector", lambda E, kc=kc, sl=sl, bk=bk,
                                     qloc=qloc: E.tensor_copy(
                                         PT[:, kc * 512 + qloc * 128:
                                            kc * 512 + (qloc + 1) * 128],
                                         ptrb2[bk][:, sl * 128:(sl + 1) * 128]),
                                     inc="v")
                                evacv[kc % 8] = P.v("v")
                if bphase < 3:
                    continue
                with contextlib.ExitStack() as ap_:
                    AL_ = ap_.enter_context
                    av = AL_(nc.sbuf_tensor(f"av{half}", [128, DC * 512], BF16))
                    vblk = [AL_(nc.sbuf_tensor(f"vb{half}{i}", [128, D], BF16))
                            for i in range(2)]
                    pb8 = [AL_(nc.psum_tensor(f"pav{half}{i}", [128, 512], F32))
                           for i in range(8)]
                    vb_wait = [0, 0]
                    P.wait("tensor", "v")
                    P.wait("sync", "cc", ccV)
                    P.wait("sync", "t")
                    P.wait("sync", "s")
                    P.wait("sync", "v")
                    P.wait("gpsimd", "t")
                    for p8 in range(2):
                        for kc in range(32):
                            bi = kc % 2
                            if vb_wait[bi]:
                                P.wait("sync", "t", vb_wait[bi])
                            g, r = kc // 8, (kc % 8) * 128
                            P.op("sync", lambda E, bi=bi, g=g, r=r: E.dma_start(
                                out=vblk[bi][:], in_=vfull_d.ap()[g, r:r + 128, :]),
                                inc=("vy0", "vy1")[bi])
                            P.wait("tensor", ("vy0", "vy1")[bi])
                            for ei in range(8):
                                ec = p8 * 8 + ei
                                P.op("tensor", lambda E, bi=bi, ec=ec, ei=ei, kc=kc:
                                     E.matmul(pb8[ei][:, :],
                                              vblk[bi][:, ec * 128:(ec + 1) * 128],
                                              PT[:, kc * 512:(kc + 1) * 512],
                                              start=(kc == 0), stop=(kc == 31)),
                                     inc=("t" if ei == 7 else None))
                            vb_wait[bi] = P.v("t")
                        P.wait("vector", "t")
                        for ei in range(8):
                            ec = p8 * 8 + ei
                            P.op("vector", lambda E, ec=ec, ei=ei: E.tensor_copy(
                                av[:, ec * 512:(ec + 1) * 512], pb8[ei][:, :]),
                                inc=("v" if ei == 7 else None))
                        P.wait("tensor", "v")
                    with contextlib.ExitStack() as ows:
                        OL = ows.enter_context
                        wo = [OL(nc.sbuf_tensor(f"wo{half}{i}", [128, DC * 128], BF16))
                              for i in range(2)]
                        for i in range(DC):
                            buf, sem = wo[i % 2], ("dw0", "dw1")[i % 2]
                            if i == 0:
                                load_wchunk(buf, sem, Wo_b, DC, 0, 128)
                            P.wait("tensor", sem)
                            if i + 1 < DC:
                                load_wchunk(wo[(i + 1) % 2],
                                            ("dw0", "dw1")[(i + 1) % 2],
                                            Wo_b, DC, (i + 1) * 128, 128)
                            for kc in range(DC):
                                P.op("tensor", lambda E, buf=buf, kc=kc: E.matmul(
                                    pb8[0][:, :], buf[:, kc * 128:(kc + 1) * 128],
                                    av[:, kc * 512:(kc + 1) * 512],
                                    start=(kc == 0), stop=(kc == DC - 1)),
                                    inc=("t" if kc == DC - 1 else None))
                            P.wait("vector", "t")
                            P.op("vector", lambda E, dc=i, half=half: E.tensor_add(
                                xs(dc, half * 512, (half + 1) * 512),
                                xs(dc, half * 512, (half + 1) * 512),
                                pb8[0][:, :]), inc="v")
                            P.wait("tensor", "v")
                            wchunk_done(buf, P.v("t"))
        barrier()

    sublayer_attn_a()
    if upto >= 2:
        sublayer_ffn("f1")
    if upto >= 3:
        sublayer_attn_b()
    if upto >= 4:
        sublayer_ffn("f2")

    P.wait("sync", "v")
    P.wait("sync", "t")
    P.op("sync", lambda E: E.dma_start(
        out=x_out.rearrange("(dc p) t -> p dc t", p=128),
        in_=x_sb[:].rearrange("p (dc t) -> p dc t", dc=DC)), inc="st")
    P.wait("sync", "st")
    P.wait("gpsimd", "st")

    with nc.Block() as block:
        @block.sync
        def _(e):
            for fn in P.streams["sync"]:
                fn(e)

        @block.gpsimd
        def _(e):
            for fn in P.streams["gpsimd"]:
                fn(e)

        @block.tensor
        def _(e):
            for fn in P.streams["tensor"]:
                fn(e)

        @block.vector
        def _(e):
            for fn in P.streams["vector"]:
                fn(e)

        @block.scalar
        def _(e):
            for fn in P.streams["scalar"]:
                fn(e)

    es.close()
    return nc


def kernel(**inp):
    x = np.asarray(inp["L0_attn_a_x"])
    WgT_h = np.ascontiguousarray(np.asarray(inp["L0_ffn_wq_gate"]).T)
    WuT_h = np.ascontiguousarray(np.asarray(inp["L0_ffn_wq_up"]).T)
    WdT_h = np.ascontiguousarray(np.asarray(inp["L0_ffn_wq_down"]).T)
    shared = {
        "Wq_a": np.asarray(inp["L0_attn_a_Wq"]), "bq_a": np.asarray(inp["L0_attn_a_bq"]),
        "Wk_a": np.asarray(inp["L0_attn_a_Wk"]), "bk_a": np.asarray(inp["L0_attn_a_bk"]),
        "Wv_a": np.asarray(inp["L0_attn_a_Wv"]), "bv_a": np.asarray(inp["L0_attn_a_bv"]),
        "Wal": np.asarray(inp["L0_attn_a_Walpha"]),
        "bal": np.asarray(inp["L0_attn_a_balpha"]),
        "Wbe": np.asarray(inp["L0_attn_a_Wbeta"]),
        "bbe": np.asarray(inp["L0_attn_a_bbeta"]),
        "Wq_b": np.asarray(inp["L0_attn_b_W_q"]), "Wk_b": np.asarray(inp["L0_attn_b_W_k"]),
        "Wv_b": np.asarray(inp["L0_attn_b_W_v"]), "Wo_b": np.asarray(inp["L0_attn_b_W_o"]),
        "WgT": WgT_h, "WuT": WuT_h, "WdT": WdT_h,
        "sg_in": np.asarray(inp["L0_ffn_s_gate"]),
        "su_in": np.asarray(inp["L0_ffn_s_up"]),
        "sd_in": np.asarray(inp["L0_ffn_s_down"]),
    }
    per_core = []
    for c in range(8):
        b, s = divmod(c, 4)
        d = dict(shared)
        d["x_in"] = np.ascontiguousarray(x[b, s * TOK:(s + 1) * TOK, :].T)
        gsel = np.zeros((NH, 4), dtype=np.float32)
        gsel[:, :s] = 1.0
        d["gsel_in"] = gsel
        per_core.append(d)

    upto = int(os.environ.get("MK_DEV_UPTO", "4"))
    nc = build_program(upto)
    res = run_bass_kernel_spmd(nc, per_core, core_ids=list(range(8)))
    out = np.empty((2, 4096, 2048), dtype=np.float32)
    for c in range(8):
        b, s = divmod(c, 4)
        out[b, s * TOK:(s + 1) * TOK, :] = np.asarray(res.results[c]["x_out"]).T
    return out


# version 04
# speedup vs baseline: 1.1626x; 1.0068x over previous
import os
import contextlib
import numpy as np
import concourse.bass as bass
import concourse.mybir as mybir
from concourse.bass_utils import run_bass_kernel_spmd

F32 = mybir.dt.float32
BF16 = mybir.dt.bfloat16
I32 = mybir.dt.int32
AF = mybir.ActivationFunctionType
ALU = mybir.AluOpType
AX = mybir.AxisListType

TOK = 1024
DC = 16
D = 2048
HID = 5632
HC = 44
NH = 16
NCH = 8
QSCALE = float(1.0 / np.sqrt(2048.0))

DMA_SEMS = {"dg", "dy", "dw0", "dw1", "dw2", "dw3", "ky0", "ky1", "vy0", "vy1", "st"}
ENG_SEMS = {"t", "v", "s", "g", "cc"}

SEMS = {}


class Plan:
    ENGINES = ("tensor", "vector", "scalar", "gpsimd", "sync")

    def __init__(self):
        self.streams = {e: [] for e in self.ENGINES}
        self.semv = {}
        self.last_wait = {}

    def v(self, s):
        return self.semv.get(s, 0)

    def inc(self, s):
        by = 16 if s in DMA_SEMS else 1
        self.semv[s] = self.semv.get(s, 0) + by
        return self.semv[s]

    def emit(self, eng, fn):
        self.streams[eng].append(fn)

    def wait(self, eng, s, val=None):
        val = self.semv.get(s, 0) if val is None else val
        if val <= 0 or val <= self.last_wait.get((eng, s), 0):
            return
        self.last_wait[(eng, s)] = val
        self.emit(eng, lambda E, s=s, val=val: E.wait_ge(SEMS[s], val))

    def op(self, eng, build, inc=None):
        if inc is not None:
            self.inc(inc)
        self.emit(eng, lambda E, b=build, i=inc:
                  (b(E).then_inc(SEMS[i], 16 if i in DMA_SEMS else 1) if i else b(E)))


def build_program(upto=4):
    nc = bass.Bass("TRN2", target_bir_lowering=False,
                   detect_race_conditions=False)
    P = Plan()

    x_in = nc.dram_tensor("x_in", [D, TOK], F32, kind="ExternalInput").ap()
    x_out = nc.dram_tensor("x_out", [D, TOK], F32, kind="ExternalOutput").ap()
    Wq_a = nc.dram_tensor("Wq_a", [D, D], F32, kind="ExternalInput").ap()
    Wk_a = nc.dram_tensor("Wk_a", [D, D], F32, kind="ExternalInput").ap()
    Wv_a = nc.dram_tensor("Wv_a", [D, D], F32, kind="ExternalInput").ap()
    bq_a = nc.dram_tensor("bq_a", [D], F32, kind="ExternalInput").ap()
    bk_a = nc.dram_tensor("bk_a", [D], F32, kind="ExternalInput").ap()
    bv_a = nc.dram_tensor("bv_a", [D], F32, kind="ExternalInput").ap()
    Wal = nc.dram_tensor("Wal", [D, NH], F32, kind="ExternalInput").ap()
    Wbe = nc.dram_tensor("Wbe", [D, NH], F32, kind="ExternalInput").ap()
    bal = nc.dram_tensor("bal", [NH], F32, kind="ExternalInput").ap()
    bbe = nc.dram_tensor("bbe", [NH], F32, kind="ExternalInput").ap()
    Wq_b = nc.dram_tensor("Wq_b", [D, D], F32, kind="ExternalInput").ap()
    Wk_b = nc.dram_tensor("Wk_b", [D, D], F32, kind="ExternalInput").ap()
    Wv_b = nc.dram_tensor("Wv_b", [D, D], F32, kind="ExternalInput").ap()
    Wo_b = nc.dram_tensor("Wo_b", [D, D], F32, kind="ExternalInput").ap()
    WgT = nc.dram_tensor("WgT", [D, HID], I32, kind="ExternalInput").ap()
    WuT = nc.dram_tensor("WuT", [D, HID], I32, kind="ExternalInput").ap()
    WdT = nc.dram_tensor("WdT", [HID, D], I32, kind="ExternalInput").ap()
    sg_in = nc.dram_tensor("sg_in", [HID], F32, kind="ExternalInput").ap()
    su_in = nc.dram_tensor("su_in", [HID], F32, kind="ExternalInput").ap()
    sd_in = nc.dram_tensor("sd_in", [D], F32, kind="ExternalInput").ap()
    gsel_in = nc.dram_tensor("gsel_in", [NH, 4], F32, kind="ExternalInput").ap()

    ssend_d = nc.dram_tensor("ssend_d", [128, 2049], F32)
    sgath_d = nc.dram_tensor("sgath_d", [4, 128, 2049], F32)
    kown_d = nc.dram_tensor("kown_d", [D, TOK], BF16)
    vown_d = nc.dram_tensor("vown_d", [TOK, D], BF16)
    kfull_d = nc.dram_tensor("kfull_d", [4, D, TOK], BF16)
    vfull_d = nc.dram_tensor("vfull_d", [4, TOK, D], BF16)
    mid_d = nc.dram_tensor("mid_d", [HC, 128, TOK], BF16)
    tap_names = [t for t in os.environ.get("MK_TAP", "").split(",") if t]
    tap_d = {t: nc.dram_tensor(f"tap_{i}", [128, 16384], F32, kind="ExternalOutput")
             for i, t in enumerate(tap_names)}

    def tap(name, ap):
        if name not in tap_d:
            return
        n = min(ap.shape[-1] if ap.ndim == 2 else 0, 16384)
        part = ap.shape[0]
        P.wait("gpsimd", "v"); P.wait("gpsimd", "t"); P.wait("gpsimd", "s")
        P.op("gpsimd", lambda E, ap=ap, n=n, part=part: E.dma_start(
            out=tap_d[name].ap()[0:part, 0:n], in_=ap[:, 0:n]), inc="dg")
        P.wait("gpsimd", "dg")

    es = contextlib.ExitStack()
    E = es.enter_context

    x_sb = E(nc.sbuf_tensor("x_sb", [128, DC * TOK], F32))
    ident = E(nc.sbuf_tensor("ident", [128, 128], BF16))
    ident32 = E(nc.sbuf_tensor("ident32", [128, 128], F32))
    maskneg = E(nc.sbuf_tensor("maskneg", [128, 128], F32))
    EH = E(nc.sbuf_tensor("EH", [16, NH * 128], BF16))
    ONEH = E(nc.sbuf_tensor("ONEH", [128, NH * 16], BF16))
    ones_r = E(nc.sbuf_tensor("ones_r", [1, 128], BF16))
    ones_rf = E(nc.sbuf_tensor("ones_rf", [1, 128], F32))
    ones_c = E(nc.sbuf_tensor("ones_c", [128, 1], BF16))
    neg128_r = E(nc.sbuf_tensor("neg128_r", [1, 128], BF16))
    dummy = E(nc.sbuf_tensor("dumt", [1, 64], F32))
    bq_sb = E(nc.sbuf_tensor("bq_sb", [128, DC], F32))
    bk_sb = E(nc.sbuf_tensor("bk_sb", [128, DC], F32))
    ba_sb = E(nc.sbuf_tensor("ba_sb", [16, 1], F32))
    bb_sb = E(nc.sbuf_tensor("bb_sb", [16, 1], F32))
    sg_sb = E(nc.sbuf_tensor("sg_sb", [128, HC], F32))
    su_sb = E(nc.sbuf_tensor("su_sb", [128, HC], F32))
    sd_sb = E(nc.sbuf_tensor("sd_sb", [128, DC], F32))
    gsel_sb = E(nc.sbuf_tensor("gsel_sb", [16, 4], F32))
    c1em6 = E(nc.sbuf_tensor("c1em6", [128, 1], F32))
    nc.const_aps.register(F32, 1e-6, c1em6[:])

    for s in sorted(DMA_SEMS | ENG_SEMS):
        SEMS[s] = E(nc.semaphore(s))

    def xs(dc, t0, t1):
        return x_sb[:, dc * TOK + t0: dc * TOK + t1]

    def barrier():
        for s in sorted(DMA_SEMS):
            P.wait("gpsimd", s)
        P.wait("gpsimd", "cc")
        P.op("tensor", lambda E: E.ldweights(ident[:, 0:1]), inc="t")
        P.op("vector", lambda E: E.memset(dummy[:, 0:16], 0.0), inc="v")
        P.op("scalar", lambda E: E.activation(dummy[:, 16:32], dummy[:, 0:16], AF.Copy),
             inc="s")
        P.op("gpsimd", lambda E: E.memset(dummy[:, 32:48], 0.0), inc="g")
        for eng in P.ENGINES:
            for s in ("t", "v", "s", "g"):
                P.wait(eng, s)

    def vdrain():
        P.wait("vector", "v")

    def rsqrt_refined(dst, src, tmp):
        P.wait("scalar", "v")
        P.op("scalar", lambda E: E.activation(tmp, src, AF.Sqrt), inc="s")
        P.wait("vector", "s")
        P.op("vector", lambda E: E.reciprocal(dst, tmp), inc="v")
        vdrain()
        P.op("vector", lambda E: E.tensor_mul(tmp, src, dst), inc="v")
        vdrain()
        P.op("vector", lambda E: E.tensor_mul(tmp, tmp, dst), inc="v")
        vdrain()
        P.op("vector", lambda E: E.tensor_scalar(tmp, tmp, -0.5, 1.5, ALU.mult,
                                                 ALU.add), inc="v")
        vdrain()
        P.op("vector", lambda E: E.tensor_mul(dst, dst, tmp), inc="v")
        vdrain()

    wsem_state = {}
    fm_halves = {}

    def load_wchunk(buf, sem, src_ap, nkc, col0, ncols):
        lastuse = wsem_state.get(id(buf), 0)
        if lastuse:
            P.wait("gpsimd", "t", lastuse)
        src = src_ap.rearrange("(kc p) m -> kc p m", p=128)
        P.op("gpsimd",
             lambda E, b=buf, s=src, c0=col0, ncc=ncols, k=nkc:
             E.dma_start(out=b[:, 0:k * ncc].rearrange("p (kc m) -> p kc m", kc=k),
                         in_=s[:, :, c0:c0 + ncc].transpose([1, 0, 2])),
             inc=sem)

    def wchunk_done(buf, tval):
        wsem_state[id(buf)] = tval

    def load_wchunk_rows(buf, sem, src_ap, nkc, row_kc0, dc):
        lastuse = wsem_state.get(id(buf), 0)
        if lastuse:
            P.wait("gpsimd", "t", lastuse)
        src = src_ap.rearrange("(kc p) m -> kc p m", p=128)
        P.op("gpsimd",
             lambda E, b=buf, s=src, k0=row_kc0, k=nkc, dc=dc:
             E.dma_start(out=b[:, 0:k * 128].rearrange("p (kc m) -> p kc m", kc=k),
                         in_=s[k0:k0 + k, :, dc * 128:(dc + 1) * 128]
                         .transpose([1, 0, 2])),
             inc=sem)

    def emit_ln_stats(pb, xb, xq, st):
        mean_t, rst_t, scr2a, scr2b = st
        prev_t = 0
        for dc in range(DC):
            for tb in range(2):
                if prev_t:
                    P.wait("scalar", "t", prev_t)
                a, b = tb * 512, (tb + 1) * 512
                P.op("scalar", lambda E, dc=dc, a=a, b=b: E.activation(
                    xb[:], xs(dc, a, b), AF.Copy))
                P.op("scalar", lambda E, dc=dc, a=a, b=b: E.activation(
                    xq[:], xs(dc, a, b), AF.Square), inc="s")
                P.wait("tensor", "s")
                P.op("tensor", lambda E, dc=dc, tb=tb: E.matmul(
                    pb[tb][0:1, :], ones_c[:], xb[:],
                    start=(dc == 0), stop=(dc == DC - 1)))
                P.op("tensor", lambda E, dc=dc, tb=tb: E.matmul(
                    pb[2 + tb][0:1, :], ones_c[:], xq[:],
                    start=(dc == 0), stop=(dc == DC - 1)), inc="t")
                prev_t = P.v("t")
        P.wait("vector", "t")
        P.op("vector", lambda E: E.tensor_scalar_mul(mean_t[0:1, 0:512], pb[0][0:1, :],
                                                     1.0 / 2048.0))
        P.op("vector", lambda E: E.tensor_scalar_mul(mean_t[0:1, 512:1024],
                                                     pb[1][0:1, :], 1.0 / 2048.0),
             inc="v")
        vdrain()
        for hb in range(2):
            sl = hb * 512
            P.op("vector", lambda E, hb=hb, sl=sl: E.tensor_scalar_mul(
                scr2a[:], pb[2 + hb][0:1, :], 1.0 / 2048.0), inc="v")
            P.op("vector", lambda E, sl=sl: E.tensor_mul(
                rst_t[0:1, sl:sl + 512], mean_t[0:1, sl:sl + 512],
                mean_t[0:1, sl:sl + 512]), inc="v")
            vdrain()
            P.op("vector", lambda E, sl=sl: E.tensor_sub(
                scr2a[:], scr2a[:], rst_t[0:1, sl:sl + 512]), inc="v")
            vdrain()
            P.op("vector", lambda E: E.tensor_scalar_add(
                scr2a[:], scr2a[:], 1e-5), inc="v")
            vdrain()
            rsqrt_refined(rst_t[0:1, sl:sl + 512], scr2a[:], scr2b[:])

    def emit_ln_norm(pb, xn_sb, xoff, t0, t1, st):
        mean_t, rst_t, scr2a, scr2b = st
        P.wait("tensor", "v")
        ntb = (t1 - t0) // 512
        for tb in range(ntb):
            a = t0 + tb * 512
            P.op("tensor", lambda E, tb=tb, a=a: E.matmul(
                pb[tb][:, :], ones_rf[:], mean_t[0:1, a:a + 512],
                start=True, stop=True))
            P.op("tensor", lambda E, tb=tb, a=a: E.matmul(
                pb[2 + tb][:, :], ones_rf[:], rst_t[0:1, a:a + 512],
                start=True, stop=True), inc=("t" if tb == ntb - 1 else None))
        P.wait("vector", "t")
        for dc in range(DC):
            for tb in range(ntb):
                a = t0 + tb * 512
                o = xoff(dc, a)
                dst = xn_sb[:, o:o + 512]
                P.op("vector", lambda E, a=a, dst=dst, tb=tb, dc=dc: E.tensor_sub(
                    dst, xs(dc, a, a + 512), pb[tb][:, :]))
                P.op("vector", lambda E, dst=dst, tb=tb: E.tensor_mul(
                    dst, dst, pb[2 + tb][:, :]), inc="v")

    def emit_proj_fm(xn_sb, xn_tok, w_src, wbufs, wsems, ec_list, t0, t1, pb2, evac):
        P.wait("tensor", "v")
        ntb = (t1 - t0) // 512
        buf = wbufs[0]
        nb = len(pb2)
        guards = [None] * nb
        src2 = w_src.rearrange("(kc p) m -> kc p m", p=128)
        hlast = fm_halves.setdefault(id(buf), [0, 0])

        def load_half(ec, hb):
            w = max(hlast[hb], wsem_state.get(id(buf), 0))
            if w:
                P.wait("gpsimd", "t", w)
            P.op("gpsimd", lambda E, ec=ec, hb=hb: E.dma_start(
                out=buf[:, hb * 8 * 128:(hb + 1) * 8 * 128]
                .rearrange("p (kc m) -> p kc m", kc=8),
                in_=src2[hb * 8:(hb + 1) * 8, :, ec * 128:(ec + 1) * 128]
                .transpose([1, 0, 2])), inc=wsems[hb])

        load_half(ec_list[0], 0)
        for i, ec in enumerate(ec_list):
            for tb in range(ntb):
                bi = (i * ntb + tb) % nb
                bank = pb2[bi]
                if guards[bi]:
                    P.wait("tensor", guards[bi][0], guards[bi][1])
                for hb in range(2):
                    P.wait("tensor", wsems[hb])
                    if tb == 0 and hb == 0:
                        load_half(ec, 1)
                    for kcl in range(8):
                        kc = hb * 8 + kcl
                        P.op("tensor", lambda E, b=bank, kc=kc, tb=tb:
                             E.matmul(b[:, :], buf[:, kc * 128:(kc + 1) * 128],
                                      xn_sb[:, kc * xn_tok + t0 + tb * 512:
                                            kc * xn_tok + t0 + (tb + 1) * 512],
                                      start=(kc == 0), stop=(kc == DC - 1)),
                             inc=("t" if (kcl == 7) else None))
                    hlast[hb] = P.v("t")
                    if tb == ntb - 1 and hb == 1 and i + 1 < len(ec_list):
                        load_half(ec_list[i + 1], 0)
                guards[bi] = evac(ec, tb, bank)
        wchunk_done(buf, max(hlast))
        for g in guards:
            if g:
                P.wait("tensor", g[0], g[1])

    def emit_proj_tm(xn_sb, xn_tok, w_src, wvb, t0, tcs, pbl, bias_pair, evac):
        P.wait("tensor", "v")
        src = w_src.rearrange("(kc p) m -> kc p m", p=128)
        bias_use = 0
        guards = [None] * len(pbl)
        for eb in range(4):
            if bias_pair is not None:
                bias_row, bias_dram = bias_pair
                if bias_use:
                    P.wait("gpsimd", "t", bias_use)
                P.op("gpsimd", lambda E, eb=eb: E.dma_start(
                    out=bias_row[:], in_=bias_dram[eb * 512:(eb + 1) * 512]
                    .unsqueeze(0)), inc="dg")
            for kc2 in range(DC // 2):
                i = eb * (DC // 2) + kc2
                buf, sem = wvb[i % 2], ("dw2", "dw3")[i % 2]
                lastuse = wsem_state.get(id(buf), 0)
                if lastuse:
                    P.wait("gpsimd", "t", lastuse)
                P.op("gpsimd", lambda E, b=buf, s=src, kc2=kc2, eb=eb: E.dma_start(
                    out=b[:].rearrange("p (u m) -> p u m", u=2),
                    in_=s[2 * kc2:2 * kc2 + 2, :, eb * 512:(eb + 1) * 512]
                    .transpose([1, 0, 2])), inc=sem)
                P.wait("tensor", sem)
                for u in range(2):
                    kc = 2 * kc2 + u
                    if kc == 0:
                        for ti in range(len(tcs)):
                            if guards[ti]:
                                P.wait("tensor", guards[ti][0], guards[ti][1])
                    for ti, tc in enumerate(tcs):
                        stop_k = (kc == DC - 1) and (bias_pair is None)
                        P.op("tensor", lambda E, b=pbl[ti], buf=buf, kc=kc, tc=tc,
                             u=u, stop_k=stop_k:
                             E.matmul(b[:, :],
                                      xn_sb[:, kc * xn_tok + t0 + tc * 128:
                                            kc * xn_tok + t0 + (tc + 1) * 128],
                                      buf[:, u * 512:(u + 1) * 512],
                                      start=(kc == 0), stop=stop_k),
                             inc=("t" if ti == len(tcs) - 1 else None))
                wchunk_done(buf, P.v("t"))
            if bias_pair is not None:
                P.wait("tensor", "dg")
            for ti, tc in enumerate(tcs):
                if bias_pair is not None:
                    P.op("tensor", lambda E, b=pbl[ti]: E.matmul(
                        b[:, :], ones_r[:], bias_pair[0][0:1, :],
                        start=False, stop=True), inc="t")
                guards[ti] = evac(tc, eb, pbl[ti])
            bias_use = P.v("t")
        for g in guards:
            if g:
                P.wait("tensor", g[0], g[1])

    P.op("sync", lambda E: E.dma_start(
        out=x_sb[:].rearrange("p (dc t) -> p dc t", dc=DC),
        in_=x_in.rearrange("(dc p) t -> p dc t", p=128)), inc="dy")
    P.op("gpsimd", lambda E: E.dma_start(
        out=bq_sb[:], in_=bq_a.rearrange("(dc p) -> p dc", p=128), allow_slow_non_contiguous=True), inc="dg")
    P.op("gpsimd", lambda E: E.dma_start(
        out=bk_sb[:], in_=bk_a.rearrange("(dc p) -> p dc", p=128), allow_slow_non_contiguous=True), inc="dg")
    P.op("gpsimd", lambda E: E.dma_start(out=ba_sb[:], in_=bal.unsqueeze(1)), inc="dg")
    P.op("gpsimd", lambda E: E.dma_start(out=bb_sb[:], in_=bbe.unsqueeze(1)), inc="dg")
    P.op("gpsimd", lambda E: E.dma_start(
        out=sg_sb[:], in_=sg_in.rearrange("(hc p) -> p hc", p=128), allow_slow_non_contiguous=True), inc="dg")
    P.op("gpsimd", lambda E: E.dma_start(
        out=su_sb[:], in_=su_in.rearrange("(hc p) -> p hc", p=128), allow_slow_non_contiguous=True), inc="dg")
    P.op("gpsimd", lambda E: E.dma_start(
        out=sd_sb[:], in_=sd_in.rearrange("(dc p) -> p dc", p=128), allow_slow_non_contiguous=True), inc="dg")
    P.op("gpsimd", lambda E: E.dma_start(out=gsel_sb[:], in_=gsel_in), inc="dg")
    P.op("gpsimd", lambda E: E.memset(ident[:], 0.0))
    P.op("gpsimd", lambda E: E.affine_select(
        out=ident[:], in_=ident[:], compare_op=ALU.not_equal, fill=1.0,
        base=0, pattern=[[-1, 128]], channel_multiplier=1))
    P.op("gpsimd", lambda E: E.memset(ident32[:], 0.0))
    P.op("gpsimd", lambda E: E.affine_select(
        out=ident32[:], in_=ident32[:], compare_op=ALU.not_equal, fill=1.0,
        base=0, pattern=[[-1, 128]], channel_multiplier=1))
    P.op("gpsimd", lambda E: E.memset(maskneg[:], 0.0))
    P.op("gpsimd", lambda E: E.affine_select(
        out=maskneg[:], in_=maskneg[:], compare_op=ALU.is_ge, fill=-1e30,
        base=0, pattern=[[1, 128]], channel_multiplier=-1))
    P.op("gpsimd", lambda E: E.memset(EH[:], 1.0))
    for h in range(NH):
        P.op("gpsimd", lambda E, h=h: E.affine_select(
            out=EH[:, h * 128:(h + 1) * 128], in_=EH[:, h * 128:(h + 1) * 128],
            compare_op=ALU.is_ge, fill=0.0, base=-256 * h, pattern=[[1, 128]],
            channel_multiplier=256))
        P.op("gpsimd", lambda E, h=h: E.affine_select(
            out=EH[:, h * 128:(h + 1) * 128], in_=EH[:, h * 128:(h + 1) * 128],
            compare_op=ALU.is_ge, fill=0.0, base=255 + 256 * h, pattern=[[-1, 128]],
            channel_multiplier=-256))
    P.op("gpsimd", lambda E: E.memset(ONEH[:], 1.0))
    for h in range(NH):
        P.op("gpsimd", lambda E, h=h: E.affine_select(
            out=ONEH[:, h * 16:(h + 1) * 16], in_=ONEH[:, h * 16:(h + 1) * 16],
            compare_op=ALU.is_ge, fill=0.0, base=-256 * h, pattern=[[256, 16]],
            channel_multiplier=1))
        P.op("gpsimd", lambda E, h=h: E.affine_select(
            out=ONEH[:, h * 16:(h + 1) * 16], in_=ONEH[:, h * 16:(h + 1) * 16],
            compare_op=ALU.is_ge, fill=0.0, base=255 + 256 * h, pattern=[[-256, 16]],
            channel_multiplier=-1))
    P.op("gpsimd", lambda E: E.memset(ones_r[:], 1.0))
    P.op("gpsimd", lambda E: E.memset(ones_rf[:], 1.0))
    P.op("gpsimd", lambda E: E.memset(ones_c[:], 1.0))
    P.op("gpsimd", lambda E: E.memset(c1em6[:], 1e-6))
    P.op("gpsimd", lambda E: E.memset(dummy[:], 0.0))
    P.op("gpsimd", lambda E: E.memset(neg128_r[:], -128.0), inc="g")
    for e in ("tensor", "vector", "scalar"):
        P.wait(e, "g"); P.wait(e, "dg"); P.wait(e, "dy")

    def sublayer_attn_a():
        with contextlib.ExitStack() as ss:
            SE = ss.enter_context
            cla_hh = SE(nc.sbuf_tensor("cla_hh", [16, 512], F32))
            lbmc_hh = SE(nc.sbuf_tensor("lbmc_hh", [16, 512], F32))
            rn_hh = SE(nc.sbuf_tensor("rn_hh", [16, 512], F32))
            scr_hh = SE(nc.sbuf_tensor("scr_hh", [16, 512], F32))
            ends_h = SE(nc.sbuf_tensor("ends_h", [16, 16], F32))
            ecb_t = SE(nc.sbuf_tensor("ecb_t", [16, TOK], BF16))
            eef = SE(nc.sbuf_tensor("eef", [16, 8], F32))
            EH32 = SE(nc.sbuf_tensor("EH32", [16, NH * 128], F32))
            UTRI = SE(nc.sbuf_tensor("UTRI", [128, 128], F32))
            atm = SE(nc.sbuf_tensor("atm", [128, NCH * 16], F32))
            btm = SE(nc.sbuf_tensor("btm", [128, NCH * 16], F32))
            latm = SE(nc.sbuf_tensor("latm", [128, NCH * 16], F32))
            cla_tm = SE(nc.sbuf_tensor("cla_tm", [128, NCH * 16], F32))
            w_tm = SE(nc.sbuf_tensor("w_tm", [128, NCH * 16], F32))
            etd_sb = SE(nc.sbuf_tensor("etd_sb", [128, 128], F32))
            ba_row = SE(nc.sbuf_tensor("ba_row", [1, 2 * NH], F32))
            S_send = SE(nc.sbuf_tensor("S_send", [128, 2049], F32))
            snap_bf = SE(nc.sbuf_tensor("snap_bf", [128, 128], BF16))
            expd_a = SE(nc.sbuf_tensor("expd_a", [128, 128], F32))
            expd_b = SE(nc.sbuf_tensor("expd_b", [128, 128], F32))
            attnf = SE(nc.sbuf_tensor("attnf", [128, 128], BF16))
            kw_sb = SE(nc.sbuf_tensor("kw_sb", [128, 128], BF16))
            qcd = SE(nc.sbuf_tensor("qcd", [128, 128], BF16))
            q_sb = SE(nc.sbuf_tensor("q_sb", [128, NH * TOK], BF16))
            pb = [SE(nc.psum_tensor(f"paa{i}", [128, 512], F32)) for i in range(6)]
            ptr = SE(nc.psum_tensor("ptra", [128, 512], BF16))

            P.op("gpsimd", lambda E: E.memset(EH32[:], 1.0))
            for h in range(NH):
                P.op("gpsimd", lambda E, h=h: E.affine_select(
                    out=EH32[:, h * 128:(h + 1) * 128],
                    in_=EH32[:, h * 128:(h + 1) * 128],
                    compare_op=ALU.is_ge, fill=0.0, base=-256 * h,
                    pattern=[[1, 128]], channel_multiplier=256))
                P.op("gpsimd", lambda E, h=h: E.affine_select(
                    out=EH32[:, h * 128:(h + 1) * 128],
                    in_=EH32[:, h * 128:(h + 1) * 128],
                    compare_op=ALU.is_ge, fill=0.0, base=255 + 256 * h,
                    pattern=[[-1, 128]], channel_multiplier=-256))
            P.op("gpsimd", lambda E: E.memset(UTRI[:], 1.0))
            P.op("gpsimd", lambda E: E.affine_select(
                out=UTRI[:], in_=UTRI[:], compare_op=ALU.is_ge, fill=0.0,
                base=0, pattern=[[1, 128]], channel_multiplier=-1))
            P.op("gpsimd", lambda E: E.memset(S_send[:], 0.0), inc="g")
            P.op("gpsimd", lambda E: E.dma_start(
                out=ba_row[:, 0:NH], in_=bal.unsqueeze(0)), inc="dg")
            P.op("gpsimd", lambda E: E.dma_start(
                out=ba_row[:, NH:], in_=bbe.unsqueeze(0)), inc="dg")
            P.wait("vector", "g")
            P.wait("tensor", "g")
            P.wait("tensor", "dg")

            st_a = tuple(SE(nc.sbuf_tensor(f"sta{i}", [1, sz], F32))
                         for i, sz in enumerate((TOK, TOK, 512, 512)))
            inner = contextlib.ExitStack()
            IE = inner.enter_context
            bv_row = IE(nc.sbuf_tensor("bv_row", [1, 512], BF16))
            xb_s = IE(nc.sbuf_tensor("a_xb", [128, 512], BF16))
            xq_s = IE(nc.sbuf_tensor("a_xq", [128, 512], BF16))
            ksq = xq_s

            emit_ln_stats(pb[0:4], xb_s, xq_s, st_a)

            wb0 = IE(nc.sbuf_tensor("wba0", [128, DC * 128], BF16))
            wb = [wb0, wb0]
            wvb = [IE(nc.sbuf_tensor(f"wva{i}", [128, 1024], BF16)) for i in range(2)]
            wab = IE(nc.sbuf_tensor("wab", [128, 2 * DC * 16], BF16))
            wsems = ["dw0", "dw1"]
            P.op("gpsimd", lambda E: E.dma_start(
                out=wab[:, 0:DC * 16].rearrange("p (dc m) -> p dc m", dc=DC),
                in_=Wal.rearrange("(dc p) m -> p dc m", p=128)), inc="dg")
            P.op("gpsimd", lambda E: E.dma_start(
                out=wab[:, DC * 16:].rearrange("p (dc m) -> p dc m", dc=DC),
                in_=Wbe.rearrange("(dc p) m -> p dc m", p=128)), inc="dg")

            xn = IE(nc.sbuf_tensor("xn", [128, DC * 512], BF16))
            k_sb = IE(nc.sbuf_tensor("k_sb", [128, NH * 512], BF16))
            v_sb = IE(nc.sbuf_tensor("v_sb", [128, 4 * D], BF16))

            for half in range(2):
                t0, t1 = half * 512, (half + 1) * 512
                emit_ln_norm(pb[0:4], xn, lambda dc, a, t0=t0: dc * 512 + (a - t0),
                             t0, t1, st_a)

                if half == 0:
                    tap("xn0", xn[:])
                def evac_q(ec, tb, bank, t0=t0):
                    P.wait("scalar", "t", P.v("t"))
                    P.op("scalar", lambda E, ec=ec, b=bank, t0=t0: E.activation(
                        q_sb[:, ec * TOK + t0: ec * TOK + t0 + 512],
                        b[:, :], AF.Identity, bias=bq_sb[:, ec:ec + 1]), inc="s")
                    return ("s", P.v("s"))
                emit_proj_fm(xn, 512, Wq_a, wb, wsems, list(range(DC)), 0, 512,
                             pb[0:2], evac_q)

                if half == 0:
                    tap("q0", q_sb[:])
                P.wait("tensor", "dg")
                P.wait("tensor", "v")
                for q_i in range(2):
                    w0 = q_i * DC * 16
                    for nl in range(4):
                        n = half * 4 + nl
                        bank = pb[2 + (nl % 2)]
                        for kc in range(DC):
                            P.op("tensor", lambda E, b=bank, w0=w0, kc=kc, nl=nl:
                                 E.matmul(b[:, 0:16],
                                          xn[:, kc * 512 + nl * 128:
                                             kc * 512 + (nl + 1) * 128],
                                          wab[:, w0 + kc * 16: w0 + (kc + 1) * 16],
                                          start=(kc == 0), stop=False))
                        P.op("tensor", lambda E, b=bank, q_i=q_i: E.matmul(
                            b[:, 0:16], ones_rf[:],
                            ba_row[0:1, q_i * NH:(q_i + 1) * NH],
                            start=False, stop=True), inc="t")
                        P.wait("scalar", "t")
                        if q_i == 0:
                            P.op("scalar", lambda E, b=bank, n=n: E.activation(
                                atm[:, n * 16:(n + 1) * 16], b[:, 0:16], AF.Sigmoid),
                                inc="s")
                        else:
                            P.op("scalar", lambda E, b=bank, n=n: E.activation(
                                btm[:, n * 16:(n + 1) * 16], b[:, 0:16], AF.Identity),
                                inc="s")
                        P.wait("tensor", "s")
                sl_h = slice(half * 4 * 16, (half + 1) * 4 * 16)
                P.op("scalar", lambda E, sl=sl_h: E.activation(
                    latm[:, sl], btm[:, sl], AF.Abs))
                P.op("scalar", lambda E, sl=sl_h: E.activation(
                    latm[:, sl], latm[:, sl], AF.Exp, scale=-1.0))
                P.op("scalar", lambda E, sl=sl_h: E.activation(
                    latm[:, sl], latm[:, sl], AF.Ln, bias=1.0), inc="s")
                P.wait("vector", "s")
                P.op("vector", lambda E, sl=sl_h: E.tensor_scalar_max(
                    btm[:, sl], btm[:, sl], 0.0), inc="v")
                vdrain()
                P.op("vector", lambda E, sl=sl_h: E.tensor_add(
                    btm[:, sl], btm[:, sl], latm[:, sl]), inc="v")
                vdrain()
                P.wait("scalar", "v")
                P.op("scalar", lambda E, sl=sl_h: E.activation(
                    latm[:, sl], atm[:, sl], AF.Ln, bias=1e-6), inc="s")
                P.wait("tensor", "s")
                for nl in range(4):
                    n = half * 4 + nl
                    P.op("tensor", lambda E, n=n, nl=nl: E.matmul(
                        pb[4][:, nl * 16:(nl + 1) * 16], UTRI[:],
                        latm[:, n * 16:(n + 1) * 16], start=True, stop=True),
                        inc=("t" if nl == 3 else None))
                P.wait("vector", "t")
                for nl in range(4):
                    n = half * 4 + nl
                    P.op("vector", lambda E, n=n, nl=nl: E.tensor_copy(
                        cla_tm[:, n * 16:(n + 1) * 16],
                        pb[4][:, nl * 16:(nl + 1) * 16]),
                        inc=("v" if nl == 3 else None))
                P.wait("tensor", "v")
                for nl in range(4):
                    n = half * 4 + nl
                    P.op("tensor", lambda E, n=n, nl=nl: E.transpose(
                        pb[5][0:16, nl * 128:(nl + 1) * 128],
                        cla_tm[:, n * 16:(n + 1) * 16], ident32[:]),
                        inc=("t" if nl == 3 else None))
                P.wait("vector", "t")
                for nl in range(4):
                    P.op("vector", lambda E, nl=nl: E.tensor_copy(
                        cla_hh[:, nl * 128:(nl + 1) * 128],
                        pb[5][0:16, nl * 128:(nl + 1) * 128]),
                        inc=("v" if nl == 3 else None))
                vdrain()
                cends = cla_hh[:].rearrange("h (n c) -> h n c", c=128)[:, :, 127]
                P.op("vector", lambda E, half=half, cends=cends: E.tensor_copy(
                    ends_h[:, half * 4:(half + 1) * 4], cends), inc="v")
                vdrain()
                if half == 0:
                    tap("clh", cla_hh[:])
                    tap("ctm", cla_tm[:])
                P.wait("scalar", "v")
                P.op("scalar", lambda E, t0=t0, t1=t1: E.activation(
                    ecb_t[:, t0:t1], cla_hh[:], AF.Exp), inc="s")
                eendsh = ecb_t[:, 0:TOK].rearrange(
                    "h (n c) -> h n c", c=128)[:, half * 4:(half + 1) * 4, 127]
                P.wait("vector", "s")
                P.op("vector", lambda E, half=half, eendsh=eendsh: E.tensor_copy(
                    eef[:, half * 4:(half + 1) * 4], eendsh), inc="v")
                vdrain()
                P.wait("tensor", "s")
                P.wait("tensor", "v")
                for h in range(NH):
                    P.op("tensor", lambda E, h=h, half=half: E.matmul(
                        pb[4][:, h * 8 + half * 4: h * 8 + half * 4 + 4],
                        EH32[:, h * 128:(h + 1) * 128],
                        eef[:, half * 4:(half + 1) * 4],
                        start=(h == 0), stop=(h == NH - 1)),
                        inc=("t" if h == NH - 1 else None))
                P.wait("vector", "t")
                for h in range(NH):
                    P.op("vector", lambda E, h=h, half=half: E.tensor_copy(
                        etd_sb[:, h * 8 + half * 4: h * 8 + half * 4 + 4],
                        pb[4][:, h * 8 + half * 4: h * 8 + half * 4 + 4]),
                        inc=("v" if h == NH - 1 else None))
                vdrain()

                def evac_k(ec, tb, bank):
                    P.wait("scalar", "t", P.v("t"))
                    P.op("scalar", lambda E, ec=ec, b=bank: E.activation(
                        k_sb[:, ec * 512:(ec + 1) * 512], b[:, :], AF.Identity,
                        bias=bk_sb[:, ec:ec + 1]), inc="s")
                    return ("s", P.v("s"))
                emit_proj_fm(xn, 512, Wk_a, wb, wsems, list(range(DC)), 0, 512,
                             pb[0:2], evac_k)

                def evac_v(tc, eb, bank):
                    P.wait("vector", "t", P.v("t"))
                    P.op("vector", lambda E, b=bank, tc=tc, eb=eb: E.tensor_copy(
                        v_sb[:, tc * D + eb * 512: tc * D + (eb + 1) * 512], b[:, :]),
                        inc="v")
                    return ("v", P.v("v"))
                emit_proj_tm(xn, 512, Wv_a, wvb, 0, list(range(4)), pb[0:4],
                             (bv_row, bv_a), evac_v)

                prev_t = 0
                for h in range(NH):
                    if prev_t:
                        P.wait("scalar", "t", prev_t)
                    P.wait("scalar", "v")
                    P.op("scalar", lambda E, h=h: E.activation(
                        ksq[:], k_sb[:, h * 512:(h + 1) * 512], AF.Square), inc="s")
                    P.wait("tensor", "s")
                    P.op("tensor", lambda E, h=h: E.matmul(
                        pb[4][0:16, :], ONEH[:, h * 16:(h + 1) * 16], ksq[:],
                        start=(h == 0), stop=(h == NH - 1)), inc="t")
                    prev_t = P.v("t")
                P.wait("vector", "t")
                P.op("vector", lambda E: E.tensor_scalar_max(
                    scr_hh[:], pb[4][0:16, :], 1e-24), inc="v")
                rsqrt_refined(rn_hh[:], scr_hh[:], lbmc_hh[:])
                P.wait("tensor", "v")
                for nl in range(4):
                    n = half * 4 + nl
                    P.op("tensor", lambda E, n=n, nl=nl: E.transpose(
                        pb[5][0:16, nl * 128:(nl + 1) * 128],
                        btm[:, n * 16:(n + 1) * 16], ident32[:]),
                        inc=("t" if nl == 3 else None))
                P.wait("vector", "t")
                for nl in range(4):
                    P.op("vector", lambda E, nl=nl: E.tensor_copy(
                        scr_hh[:, nl * 128:(nl + 1) * 128],
                        pb[5][0:16, nl * 128:(nl + 1) * 128]),
                        inc=("v" if nl == 3 else None))
                vdrain()
                P.op("vector", lambda E: E.tensor_mul(scr_hh[:], scr_hh[:], rn_hh[:]),
                     inc="v")
                vdrain()
                P.wait("scalar", "v")
                P.op("scalar", lambda E: E.activation(lbmc_hh[:], scr_hh[:], AF.Ln),
                     inc="s")
                P.wait("vector", "s")
                P.op("vector", lambda E: E.tensor_sub(lbmc_hh[:], lbmc_hh[:],
                                                      cla_hh[:]), inc="v")
                vdrain()
                for nl in range(4):
                    ce = cla_hh[:, nl * 128 + 127: nl * 128 + 128]
                    P.op("vector", lambda E, nl=nl, ce=ce: E.tensor_add(
                        scr_hh[:, nl * 128:(nl + 1) * 128],
                        lbmc_hh[:, nl * 128:(nl + 1) * 128],
                        ce.broadcast_to((16, 128))),
                        inc=("v" if nl == 3 else None))
                vdrain()
                P.wait("scalar", "v")
                P.op("scalar", lambda E: E.activation(scr_hh[:], scr_hh[:], AF.Exp),
                     inc="s")
                P.wait("tensor", "s")
                for nl in range(4):
                    n = half * 4 + nl
                    P.op("tensor", lambda E, n=n, nl=nl: E.transpose(
                        pb[5][:, nl * 16:(nl + 1) * 16],
                        scr_hh[:, nl * 128:(nl + 1) * 128], ident32[0:16, 0:16]),
                        inc=("t" if nl == 3 else None))
                P.wait("vector", "t")
                for nl in range(4):
                    n = half * 4 + nl
                    P.op("vector", lambda E, n=n, nl=nl: E.tensor_copy(
                        w_tm[:, n * 16:(n + 1) * 16], pb[5][:, nl * 16:(nl + 1) * 16]),
                        inc=("v" if nl == 3 else None))
                vdrain()

                if half == 0:
                    tap("v0", v_sb[:])
                for nl in range(4):
                    n = half * 4 + nl
                    if half == 0 and nl == 1:
                        tap("s0", S_send[:, 0:2048])
                    for h in range(NH):
                        qt = q_sb[:, h * TOK + n * 128: h * TOK + (n + 1) * 128]
                        kt = k_sb[:, h * 512 + nl * 128: h * 512 + (nl + 1) * 128]
                        vt = v_sb[:, nl * D + h * 128: nl * D + (h + 1) * 128]
                        clat = cla_hh[:, nl * 128:(nl + 1) * 128]
                        ecbt = ecb_t[:, n * 128:(n + 1) * 128]
                        lbt = lbmc_hh[:, nl * 128:(nl + 1) * 128]
                        ehs = EH[:, h * 128:(h + 1) * 128]
                        ehs32 = EH32[:, h * 128:(h + 1) * 128]
                        P.op("vector", lambda E, h=h: E.tensor_copy(
                            snap_bf[:], S_send[:, h * 128:(h + 1) * 128]), inc="v")
                        P.wait("tensor", "v")
                        P.op("tensor", lambda E, kt=kt: E.transpose(
                            ptr[:, 0:128], kt, ident[:]))
                        P.op("tensor", lambda E, kt=kt, qt=qt: E.matmul(
                            pb[0][:, 0:128], kt, qt, start=True, stop=True))
                        P.op("tensor", lambda E, lbt=lbt, ehs32=ehs32: E.matmul(
                            pb[1][:, 0:128], lbt, ehs32, start=True, stop=False))
                        P.op("tensor", lambda E, clat=clat, ehs32=ehs32: E.matmul(
                            pb[1][:, 0:128], ehs32, clat, start=False, stop=True))
                        P.op("tensor", lambda E, ecbt=ecbt, ehs=ehs: E.matmul(
                            pb[4][:, 0:128], ehs, ecbt, start=True, stop=True),
                            inc="t")
                        P.wait("vector", "t")
                        P.op("vector", lambda E: E.tensor_add(
                            expd_a[:], pb[1][:, 0:128], maskneg[:]), inc="v")
                        P.wait("scalar", "v")
                        P.op("scalar", lambda E: E.activation(
                            expd_b[:], expd_a[:], AF.Exp), inc="s")
                        P.wait("vector", "s")
                        P.op("vector", lambda E: E.tensor_mul(
                            attnf[:], pb[0][:, 0:128], expd_b[:]))
                        P.op("vector", lambda E, n=n, h=h: E.tensor_scalar_mul(
                            kw_sb[:], ptr[:, 0:128],
                            w_tm[:, n * 16 + h: n * 16 + h + 1]))
                        P.op("vector", lambda E, qt=qt: E.tensor_mul(
                            qcd[:], qt, pb[4][:, 0:128]), inc="v")
                        P.wait("tensor", "v")
                        P.op("tensor", lambda E, vt=vt: E.matmul(
                            pb[2][:, 0:128], kw_sb[:], vt, start=True, stop=True))
                        P.op("tensor", lambda E, vt=vt: E.matmul(
                            pb[3][:, 0:128], vt, attnf[:], start=True, stop=False))
                        P.op("tensor", lambda E: E.matmul(
                            pb[3][:, 0:128], snap_bf[:], qcd[:],
                            start=False, stop=True), inc="t")
                        P.wait("vector", "t")
                        P.op("vector", lambda E, h=h, n=n: E.tensor_add(
                            xs(h, n * 128, (n + 1) * 128),
                            xs(h, n * 128, (n + 1) * 128), pb[3][:, 0:128]))
                        P.op("vector", lambda E, h=h, n=n: E.scalar_tensor_tensor(
                            S_send[:, h * 128:(h + 1) * 128],
                            S_send[:, h * 128:(h + 1) * 128],
                            etd_sb[:, h * 8 + n: h * 8 + n + 1],
                            pb[2][:, 0:128], ALU.mult, ALU.add), inc="v")

            P.wait("gpsimd", "dw0"); P.wait("gpsimd", "dw2"); P.wait("gpsimd", "dw3")
            inner.close()

            gcb_t = SE(nc.sbuf_tensor("gcb_t", [16, TOK], F32))
            tap("end", ends_h[:])
            P.op("vector", lambda E: E.tensor_reduce(
                S_send[0:16, 2048:2049], ends_h[:, 0:8], AX.X, ALU.add), inc="v")
            vdrain()
            P.wait("gpsimd", "v")
            P.op("gpsimd", lambda E: E.memset(ends_h[:, 8:16], 0.0), inc="g")
            P.wait("vector", "g")
            P.op("vector", lambda E: E.tensor_tensor_scan(
                latm[0:16, 0:8], ends_h[:, 0:8], ends_h[:, 8:16], 0.0,
                ALU.add, ALU.add), inc="v")
            vdrain()
            P.op("vector", lambda E: E.tensor_copy(ends_h[:, 9:16], latm[0:16, 0:7]),
                 inc="v")
            vdrain()
            for half in range(2):
                P.wait("tensor", "v")
                for nl in range(4):
                    n = half * 4 + nl
                    P.op("tensor", lambda E, n=n, nl=nl: E.transpose(
                        pb[5][0:16, nl * 128:(nl + 1) * 128],
                        cla_tm[:, n * 16:(n + 1) * 16], ident32[:]),
                        inc=("t" if nl == 3 else None))
                P.wait("vector", "t")
                for nl in range(4):
                    n = half * 4 + nl
                    P.op("vector", lambda E, n=n, nl=nl: E.tensor_add(
                        cla_hh[:, nl * 128:(nl + 1) * 128],
                        pb[5][0:16, nl * 128:(nl + 1) * 128],
                        ends_h[:, 8 + n: 9 + n].broadcast_to((16, 128))),
                        inc=("v" if nl == 3 else None))
                P.wait("scalar", "v")
                P.op("scalar", lambda E, half=half: E.activation(
                    gcb_t[:, half * 512:(half + 1) * 512], cla_hh[:], AF.Exp),
                    inc="s")
                P.wait("tensor", "s")

            P.wait("gpsimd", "v")
            P.op("gpsimd", lambda E: E.dma_start(out=ssend_d.ap()[:, :], in_=S_send[:]),
                 inc="dg")
            P.wait("gpsimd", "dg")
            P.op("gpsimd", lambda E: E.collective_compute(
                "AllGather", ALU.bypass, replica_groups=[[0, 1, 2, 3], [4, 5, 6, 7]],
                ins=[ssend_d.ap().opt()], outs=[sgath_d.ap().opt()]), inc="cc")
            P.wait("gpsimd", "cc")
            P.wait("sync", "cc")

            S_in = SE(nc.sbuf_tensor("S_in", [128, 2048], F32))
            S_in_bf = SE(nc.sbuf_tensor("S_in_bf", [128, 2048], BF16))
            gla = SE(nc.sbuf_tensor("gla", [16, 16], F32))
            fc_sb = SE(nc.sbuf_tensor("fc_sb", [128, 64], F32))
            gbuf = [SE(nc.sbuf_tensor(f"gbuf{i}", [128, 2049], F32)) for i in range(2)]
            for p_ in range(4):
                P.op("sync", lambda E, p_=p_: E.dma_start(
                    out=gla[:, p_:p_ + 1], in_=sgath_d.ap()[p_, 0:16, 2048:2049],
                    allow_slow_non_contiguous=True), inc="dy")
            P.wait("vector", "dy")
            P.op("vector", lambda E: E.tensor_mul(gla[:, 4:8], gla[:, 0:4], gsel_sb[:]),
                 inc="v")
            vdrain()
            P.wait("gpsimd", "v")
            P.op("gpsimd", lambda E: E.memset(gla[:, 11:12], 0.0), inc="g")
            P.wait("vector", "g")
            P.op("vector", lambda E: E.tensor_copy(gla[:, 10:11], gla[:, 7:8]),
                 inc="v")
            vdrain()
            P.op("vector", lambda E: E.tensor_add(gla[:, 9:10], gla[:, 10:11],
                                                  gla[:, 6:7]), inc="v")
            vdrain()
            P.op("vector", lambda E: E.tensor_add(gla[:, 8:9], gla[:, 9:10],
                                                  gla[:, 5:6]), inc="v")
            vdrain()
            P.wait("scalar", "v")
            P.op("scalar", lambda E: E.activation(gla[:, 12:16], gla[:, 8:12],
                                                  AF.Exp), inc="s")
            P.wait("vector", "s")
            P.op("vector", lambda E: E.tensor_mul(gla[:, 12:16], gla[:, 12:16],
                                                  gsel_sb[:]), inc="v")
            vdrain()
            P.wait("tensor", "v")
            for h in range(NH):
                P.op("tensor", lambda E, h=h: E.matmul(
                    pb[4][:, h * 4:(h + 1) * 4], EH32[:, h * 128:(h + 1) * 128],
                    gla[:, 12:16], start=(h == 0), stop=(h == NH - 1)),
                    inc=("t" if h == NH - 1 else None))
            P.wait("vector", "t")
            P.op("vector", lambda E: E.tensor_copy(fc_sb[:], pb[4][:, 0:64]), inc="v")
            P.wait("gpsimd", "v")
            P.op("gpsimd", lambda E: E.memset(S_in[:], 0.0), inc="g")
            P.wait("vector", "g")
            gb_wait = [0, 0]
            for p_ in range(4):
                b = gbuf[p_ % 2]
                if gb_wait[p_ % 2]:
                    P.wait("sync", "v", gb_wait[p_ % 2])
                P.op("sync", lambda E, p_=p_, b=b: E.dma_start(
                    out=b[:], in_=sgath_d.ap()[p_]), inc="dy")
                P.wait("vector", "dy")
                for h in range(NH):
                    P.op("vector", lambda E, b=b, h=h, p_=p_: E.scalar_tensor_tensor(
                        S_in[:, h * 128:(h + 1) * 128], b[:, h * 128:(h + 1) * 128],
                        fc_sb[:, h * 4 + p_: h * 4 + p_ + 1],
                        S_in[:, h * 128:(h + 1) * 128], ALU.mult, ALU.add),
                        inc=("v" if h == NH - 1 else None))
                vdrain()
                gb_wait[p_ % 2] = P.v("v")
            vdrain()
            P.op("vector", lambda E: E.tensor_copy(S_in_bf[:], S_in[:]), inc="v")
            tap("sin", S_in[:])
            tap("gla", gla[:])
            tap("gsl", gsel_sb[:])
            tap("fc", fc_sb[:])
            tap("gcb", gcb_t[:])
            tap("ehh", ends_h[:])

            qcd4 = SE(nc.sbuf_tensor("qcd4", [128, 512], BF16))
            for n in range(NCH):
                gcbt = gcb_t[:, n * 128:(n + 1) * 128]
                for hg in range(NH // 4):
                    P.wait("tensor", "v")
                    for j in range(4):
                        h = hg * 4 + j
                        P.op("tensor", lambda E, gcbt=gcbt, h=h, j=j: E.matmul(
                            pb[0][:, j * 128:(j + 1) * 128],
                            EH32[:, h * 128:(h + 1) * 128], gcbt,
                            start=True, stop=True),
                            inc=("t" if j == 3 else None))
                    P.wait("vector", "t")
                    for j in range(4):
                        h = hg * 4 + j
                        P.op("vector", lambda E, h=h, j=j, n=n: E.tensor_mul(
                            qcd4[:, j * 128:(j + 1) * 128],
                            q_sb[:, h * TOK + n * 128: h * TOK + (n + 1) * 128],
                            pb[0][:, j * 128:(j + 1) * 128]),
                            inc=("v" if j == 3 else None))
                    P.wait("tensor", "v")
                    for j in range(4):
                        h = hg * 4 + j
                        P.op("tensor", lambda E, h=h, j=j: E.matmul(
                            pb[1][:, j * 128:(j + 1) * 128],
                            S_in_bf[:, h * 128:(h + 1) * 128],
                            qcd4[:, j * 128:(j + 1) * 128],
                            start=True, stop=True),
                            inc=("t" if j == 3 else None))
                    P.wait("vector", "t")
                    for j in range(4):
                        h = hg * 4 + j
                        P.op("vector", lambda E, h=h, j=j, n=n: E.tensor_add(
                            xs(h, n * 128, (n + 1) * 128),
                            xs(h, n * 128, (n + 1) * 128),
                            pb[1][:, j * 128:(j + 1) * 128]),
                            inc=("v" if j == 3 else None))
        barrier()

    def sublayer_ffn(tag):
        with contextlib.ExitStack() as ss:
            SE = ss.enter_context
            csum = SE(nc.sbuf_tensor(f"csum{tag}", [1, TOK], BF16))
            msum = SE(nc.sbuf_tensor(f"msum{tag}", [1, TOK], BF16))
            t1sb = [SE(nc.sbuf_tensor(f"t1sb{tag}{i}", [128, 512], F32))
                    for i in range(2)]
            midst = [SE(nc.sbuf_tensor(f"mst{tag}{i}", [128, TOK], BF16))
                     for i in range(2)]
            pb = [SE(nc.psum_tensor(f"pf{tag}{i}", [128, 512], F32)) for i in range(6)]
            with contextlib.ExitStack() as xs_scope:
                XL = xs_scope.enter_context
                xn = XL(nc.sbuf_tensor(f"xnf{tag}", [128, DC * TOK], BF16))
                xb_s = XL(nc.sbuf_tensor(f"fxb{tag}", [128, 512], BF16))
                xq_s = XL(nc.sbuf_tensor(f"fxq{tag}", [128, 512], BF16))
                st_f = tuple(XL(nc.sbuf_tensor(f"stf{tag}{i}", [1, sz], F32))
                             for i, sz in enumerate((TOK, TOK, 512, 512)))
                emit_ln_stats(pb[0:4], xb_s, xq_s, st_f)
                emit_ln_norm(pb[0:4], xn, lambda dc, a: dc * TOK + a, 0, TOK, st_f)
                P.wait("tensor", "v")
                for tb in range(2):
                    for kc in range(DC):
                        P.op("tensor", lambda E, tb=tb, kc=kc: E.matmul(
                            pb[4 + tb][0:1, :], ones_c[:],
                            xn[:, kc * TOK + tb * 512: kc * TOK + (tb + 1) * 512],
                            start=(kc == 0), stop=(kc == DC - 1)),
                            inc=("t" if kc == DC - 1 else None))
                P.wait("vector", "t")
                P.op("vector", lambda E: E.tensor_copy(csum[:, 0:512], pb[4][0:1, :]))
                P.op("vector", lambda E: E.tensor_copy(csum[:, 512:1024],
                                                       pb[5][0:1, :]), inc="v")
                with contextlib.ExitStack() as ws:
                    WL = ws.enter_context
                    wg = [WL(nc.sbuf_tensor(f"wg{tag}{i}", [128, DC * 128], BF16))
                          for i in range(2)]
                    wu = [WL(nc.sbuf_tensor(f"wu{tag}{i}", [128, DC * 128], BF16))
                          for i in range(2)]
                    P.wait("tensor", "v")
                    mid_dma = [0, 0]
                    bank_guard = [0, 0]
                    for hc in range(HC):
                        bg, bu = wg[hc % 2], wu[hc % 2]
                        sg_, su_ = ("dw0", "dw1")[hc % 2], ("dw2", "dw3")[hc % 2]
                        if hc == 0:
                            load_wchunk(bg, sg_, WgT, DC, 0, 128)
                            load_wchunk(bu, su_, WuT, DC, 0, 128)
                        P.wait("tensor", sg_)
                        P.wait("tensor", su_)
                        if hc + 1 < HC:
                            load_wchunk(wg[(hc + 1) % 2], ("dw0", "dw1")[(hc + 1) % 2],
                                        WgT, DC, (hc + 1) * 128, 128)
                            load_wchunk(wu[(hc + 1) % 2], ("dw2", "dw3")[(hc + 1) % 2],
                                        WuT, DC, (hc + 1) * 128, 128)
                        mst = midst[hc % 2]
                        for tb in range(2):
                            gbank, ubank = pb[tb * 2], pb[tb * 2 + 1]
                            if bank_guard[tb]:
                                P.wait("tensor", "v", bank_guard[tb])
                            for kc in range(DC):
                                P.op("tensor", lambda E, bg=bg, kc=kc, tb=tb,
                                     b=gbank: E.matmul(
                                         b[:, :], bg[:, kc * 128:(kc + 1) * 128],
                                         xn[:, kc * TOK + tb * 512:
                                            kc * TOK + (tb + 1) * 512],
                                         start=(kc == 0), stop=False))
                            P.op("tensor", lambda E, tb=tb, b=gbank: E.matmul(
                                b[:, :], neg128_r[:],
                                csum[0:1, tb * 512:(tb + 1) * 512],
                                start=False, stop=True), inc="t")
                            for kc in range(DC):
                                P.op("tensor", lambda E, bu=bu, kc=kc, tb=tb,
                                     b=ubank: E.matmul(
                                         b[:, :], bu[:, kc * 128:(kc + 1) * 128],
                                         xn[:, kc * TOK + tb * 512:
                                            kc * TOK + (tb + 1) * 512],
                                         start=(kc == 0), stop=False))
                            P.op("tensor", lambda E, tb=tb, b=ubank: E.matmul(
                                b[:, :], neg128_r[:],
                                csum[0:1, tb * 512:(tb + 1) * 512],
                                start=False, stop=True), inc="t")
                            tu = P.v("t")
                            P.wait("scalar", "t", tu)
                            P.op("scalar", lambda E, hc=hc, tb=tb, b=gbank: E.activation(
                                t1sb[tb][:], b[:, :], AF.Silu,
                                scale=sg_sb[:, hc:hc + 1]), inc="s")
                            P.wait("vector", "s", P.v("s"))
                            if mid_dma[hc % 2]:
                                P.wait("vector", ("vy0", "vy1")[hc % 2],
                                       mid_dma[hc % 2])
                            P.op("vector", lambda E, mst=mst, hc=hc, tb=tb, b=ubank:
                                 E.scalar_tensor_tensor(
                                     mst[:, tb * 512:(tb + 1) * 512],
                                     b[:, :], su_sb[:, hc:hc + 1], t1sb[tb][:],
                                     ALU.mult, ALU.mult), inc="v")
                            bank_guard[tb] = P.v("v")
                        P.wait("sync", "v", P.v("v"))
                        P.op("sync", lambda E, mst=mst, hc=hc: E.dma_start(
                            out=mid_d.ap()[hc], in_=mst[:]),
                            inc=("vy0", "vy1")[hc % 2])
                        mid_dma[hc % 2] = P.v(("vy0", "vy1")[hc % 2])
                        wchunk_done(bg, P.v("t"))
                        wchunk_done(bu, P.v("t"))
            NG = 2
            GH = HC // NG
            with contextlib.ExitStack() as ws:
                WL = ws.enter_context
                P.wait("gpsimd", "t")
                P.wait("sync", "t")
                mgrp = WL(nc.sbuf_tensor(f"mg{tag}", [128, GH * TOK], BF16))
                wd = [WL(nc.sbuf_tensor(f"wd{tag}{i}", [128, GH * 128], BF16))
                      for i in range(2)]
                down_guard = [0, 0]
                P.wait("tensor", "v")
                P.wait("tensor", "s")
                for hg in range(NG):
                    P.wait("sync", ("vy0", "vy1")[0])
                    P.wait("sync", ("vy0", "vy1")[1])
                    if hg > 0:
                        P.wait("sync", "t")
                    P.op("sync", lambda E, hg=hg: E.dma_start(
                        out=mgrp[:].rearrange("p (h t) -> p h t", h=GH),
                        in_=mid_d.ap()[hg * GH:(hg + 1) * GH].transpose([1, 0, 2])),
                        inc="ky0")
                    P.wait("tensor", "ky0")
                    for tb in range(2):
                        for hc in range(GH):
                            P.op("tensor", lambda E, tb=tb, hc=hc, hg=hg: E.matmul(
                                pb[4 + tb][0:1, :], ones_c[:],
                                mgrp[:, hc * TOK + tb * 512: hc * TOK + (tb + 1) * 512],
                                start=(hg == 0 and hc == 0),
                                stop=(hg == NG - 1 and hc == GH - 1)),
                                inc=("t" if hc == GH - 1 else None))
                    if hg == NG - 1:
                        P.wait("vector", "t")
                        P.op("vector", lambda E: E.tensor_copy(msum[:, 0:512],
                                                               pb[4][0:1, :]))
                        P.op("vector", lambda E: E.tensor_copy(msum[:, 512:1024],
                                                               pb[5][0:1, :]), inc="v")
                        P.wait("tensor", "v")
                    for dc in range(DC):
                        bd = wd[dc % 2]
                        sd_ = ("dw0", "dw1")[dc % 2]
                        if dc == 0:
                            load_wchunk_rows(bd, sd_, WdT, GH, hg * GH, dc)
                        P.wait("tensor", sd_)
                        if dc + 1 < DC:
                            load_wchunk_rows(wd[(dc + 1) % 2],
                                             ("dw0", "dw1")[(dc + 1) % 2],
                                             WdT, GH, hg * GH, dc + 1)
                        for tb in range(2):
                            bank = pb[tb]
                            if down_guard[tb]:
                                P.wait("tensor", "v", down_guard[tb])
                            last = NG - 1
                            for hc in range(GH):
                                fin = (hg < last and hc == GH - 1)
                                P.op("tensor",
                                     lambda E, b=bank, bd=bd, hc=hc, tb=tb, fin=fin:
                                     E.matmul(b[:, :], bd[:, hc * 128:(hc + 1) * 128],
                                              mgrp[:, hc * TOK + tb * 512:
                                                   hc * TOK + (tb + 1) * 512],
                                              start=(hc == 0), stop=fin),
                                     inc=("t" if fin else None))
                            if hg == last:
                                P.op("tensor", lambda E, b=bank, tb=tb: E.matmul(
                                    b[:, :], neg128_r[:],
                                    msum[0:1, tb * 512:(tb + 1) * 512],
                                    start=False, stop=True), inc="t")
                            P.wait("vector", "t", P.v("t"))
                            P.op("vector", lambda E, b=bank, dc=dc, tb=tb:
                                 E.scalar_tensor_tensor(
                                     xs(dc, tb * 512, (tb + 1) * 512), b[:, :],
                                     sd_sb[:, dc:dc + 1],
                                     xs(dc, tb * 512, (tb + 1) * 512),
                                     ALU.mult, ALU.add), inc="v")
                            down_guard[tb] = P.v("v")
                        wchunk_done(bd, P.v("t"))
        barrier()

    def sublayer_attn_b():
        with contextlib.ExitStack() as ss:
            SE = ss.enter_context
            q_sb = SE(nc.sbuf_tensor("qb_sb", [128, DC * TOK], BF16))
            PT = SE(nc.sbuf_tensor("PT", [128, 32 * 512], BF16))
            with contextlib.ExitStack() as ps:
                PL = ps.enter_context
                stg = PL(nc.sbuf_tensor("stg", [128, 512], BF16))
                xn = PL(nc.sbuf_tensor("xnb", [128, DC * TOK], BF16))
                pbs = [PL(nc.psum_tensor(f"pbb{i}", [128, 512], F32)) for i in range(8)]
                xb_s = PL(nc.sbuf_tensor("bxb", [128, 512], BF16))
                xq_s = PL(nc.sbuf_tensor("bxq", [128, 512], BF16))
                st_b = tuple(PL(nc.sbuf_tensor(f"stb{i}", [1, sz], F32))
                             for i, sz in enumerate((TOK, TOK, 512, 512)))
                emit_ln_stats(pbs[0:4], xb_s, xq_s, st_b)
                emit_ln_norm(pbs[0:4], xn, lambda dc, a: dc * TOK + a, 0, TOK, st_b)
                wb = [PL(nc.sbuf_tensor(f"wbb{i}", [128, DC * 128], BF16))
                      for i in range(2)]
                wvb = [PL(nc.sbuf_tensor(f"wvbb{i}", [128, 1024], BF16))
                       for i in range(2)]
                wsems = ["dw0", "dw1"]


                def evac_kb(ec, tb, bank):
                    P.wait("vector", "t", P.v("t"))
                    P.wait("vector", "dg")
                    P.op("vector", lambda E, b=bank: E.tensor_copy(stg[:], b[:, :]),
                         inc="v")
                    P.wait("gpsimd", "v", P.v("v"))
                    P.op("gpsimd", lambda E, ec=ec, tb=tb: E.dma_start(
                        out=kown_d.ap()[ec * 128:(ec + 1) * 128,
                                        tb * 512:(tb + 1) * 512],
                        in_=stg[:]), inc="dg")
                    return ("v", P.v("v"))
                emit_proj_fm(xn, TOK, Wk_b, wb, wsems, list(range(DC)), 0, TOK,
                             pbs[0:2], evac_kb)

                def evac_vb(tc, eb, bank):
                    P.wait("vector", "t", P.v("t"))
                    P.wait("vector", "dg")
                    P.op("vector", lambda E, b=bank: E.tensor_copy(stg[:], b[:, :]),
                         inc="v")
                    P.wait("gpsimd", "v", P.v("v"))
                    P.op("gpsimd", lambda E, tc=tc, eb=eb: E.dma_start(
                        out=vown_d.ap()[tc * 128:(tc + 1) * 128,
                                        eb * 512:(eb + 1) * 512],
                        in_=stg[:]), inc="dg")
                    return ("v", P.v("v"))
                emit_proj_tm(xn, TOK, Wv_b, wvb, 0, list(range(8)), pbs[0:8], None,
                             evac_vb)
                P.wait("gpsimd", "dg")
                P.op("gpsimd", lambda E: E.collective_compute(
                    "AllGather", ALU.bypass,
                    replica_groups=[[0, 1, 2, 3], [4, 5, 6, 7]],
                    ins=[kown_d.ap().opt()], outs=[kfull_d.ap().opt()]), inc="cc")
                ccK = P.v("cc")
                P.op("gpsimd", lambda E: E.collective_compute(
                    "AllGather", ALU.bypass,
                    replica_groups=[[0, 1, 2, 3], [4, 5, 6, 7]],
                    ins=[vown_d.ap().opt()], outs=[vfull_d.ap().opt()]), inc="cc")
                ccV = P.v("cc")

                def evac_qb(ec, tb, bank):
                    P.wait("scalar", "t", P.v("t"))
                    P.op("scalar", lambda E, ec=ec, tb=tb, b=bank: E.activation(
                        q_sb[:, ec * TOK + tb * 512: ec * TOK + (tb + 1) * 512],
                        b[:, :], AF.Identity, scale=QSCALE), inc="s")
                    return ("s", P.v("s"))
                emit_proj_fm(xn, TOK, Wq_b, wb, wsems, list(range(DC)), 0, TOK,
                             pbs[0:2], evac_qb)

            bphase = int(os.environ.get("MK_DEV_BPH", "4"))
            for half in range(2 if bphase >= 4 else 1):
                if bphase < 2:
                    break
                with contextlib.ExitStack() as sp:
                    SL = sp.enter_context
                    scsb = [SL(nc.sbuf_tensor(f"sc{half}{i}", [128, 4096], F32))
                            for i in range(2)]
                    psb = SL(nc.sbuf_tensor(f"ps{half}", [128, 4096], BF16))
                    rowst = SL(nc.sbuf_tensor(f"row{half}", [128, 8], F32))
                    kblk = [SL(nc.sbuf_tensor(f"kb{half}{i}", [128, DC * 512], BF16))
                            for i in range(2)]
                    pb2 = [SL(nc.psum_tensor(f"pbs{half}{i}", [128, 512], F32))
                           for i in range(2)]
                    ptrb2 = [SL(nc.psum_tensor(f"ptrb{half}{i}", [128, 512], BF16))
                             for i in range(2)]
                    kb_wait = [0, 0]
                    ptr_wait = [0] * 8
                    P.wait("tensor", "v")
                    P.wait("sync", "cc", ccK)
                    P.wait("sync", "t")
                    P.wait("sync", "v")
                    P.wait("sync", "s")
                    for pr in range(2):
                        qcs = [half * 4 + pr * 2, half * 4 + pr * 2 + 1]
                        for kb in range(8):
                            bi = kb % 2
                            if kb_wait[bi]:
                                P.wait("sync", "t", kb_wait[bi])
                            g, tl = kb // 2, (kb % 2) * 512
                            P.op("sync", lambda E, bi=bi, g=g, tl=tl: E.dma_start(
                                out=kblk[bi][:].rearrange("p (kc m) -> p kc m", kc=DC),
                                in_=kfull_d.ap()[g]
                                .rearrange("(kc p) t -> kc p t", p=128)
                                [:, :, tl:tl + 512].transpose([1, 0, 2])),
                                inc=("ky0", "ky1")[bi])
                            P.wait("tensor", ("ky0", "ky1")[bi])
                            for qi, qc in enumerate(qcs):
                                for kc in range(DC):
                                    P.op("tensor",
                                         lambda E, qi=qi, qc=qc, kc=kc, bi=bi:
                                         E.matmul(pb2[qi][:, :],
                                                  q_sb[:, kc * TOK + qc * 128:
                                                       kc * TOK + (qc + 1) * 128],
                                                  kblk[bi][:,
                                                           kc * 512:(kc + 1) * 512],
                                                  start=(kc == 0),
                                                  stop=(kc == DC - 1)),
                                         inc=("t" if kc == DC - 1 else None))
                            kb_wait[bi] = P.v("t")
                            P.wait("vector", "t")
                            for qi in range(2):
                                P.op("vector", lambda E, qi=qi, kb=kb: E.tensor_copy(
                                    scsb[qi][:, kb * 512:(kb + 1) * 512],
                                    pb2[qi][:, :]), inc="v")
                            P.wait("tensor", "v")
                        vdrain()
                        for qi, qc in enumerate(qcs):
                            qloc = qc % 4
                            P.op("vector", lambda E, qi=qi, qloc=qloc: E.reduce_max(
                                rowst[:, qloc:qloc + 1], scsb[qi][:], axis=AX.X,
                                negate=True), inc="v")
                            vdrain()
                            P.wait("scalar", "v")
                            P.wait("scalar", "t")
                            if os.environ.get("MK_NOSM"):
                                P.op("scalar", lambda E, qi=qi: E.activation(
                                    psb[:], scsb[qi][:], AF.Identity), inc="s")
                            else:
                                P.op("scalar", lambda E, qi=qi, qloc=qloc: E.activation(
                                    psb[:], scsb[qi][:], AF.Exp,
                                    bias=rowst[:, qloc:qloc + 1],
                                    accum_out=rowst[:, 4 + qloc:5 + qloc]), inc="s")
                            P.wait("vector", "s")
                            P.op("vector", lambda E, qloc=qloc: E.reciprocal(
                                rowst[:, 4 + qloc:5 + qloc],
                                rowst[:, 4 + qloc:5 + qloc]), inc="v")
                            vdrain()
                            P.op("vector", lambda E, qloc=qloc: E.tensor_scalar_mul(
                                psb[:], psb[:], rowst[:, 4 + qloc:5 + qloc]),
                                inc="v")
                            vdrain()
                            P.wait("tensor", "v")
                            evacv = ptr_wait
                            for kc in range(0 if os.environ.get("MK_NOTR") else 32):
                                bk = kc % 2
                                sl = (kc // 2) % 4
                                if evacv[(kc - 2) % 8] and kc >= 2:
                                    P.wait("tensor", "v", evacv[(kc - 2) % 8])
                                elif evacv[(kc - 2) % 8]:
                                    P.wait("tensor", "v", evacv[(kc - 2) % 8])
                                P.op("tensor", lambda E, kc=kc, sl=sl, bk=bk:
                                     E.transpose(
                                         ptrb2[bk][:, sl * 128:(sl + 1) * 128],
                                         psb[:, kc * 128:(kc + 1) * 128], ident[:]),
                                     inc="t")
                                P.wait("vector", "t")
                                P.op("vector", lambda E, kc=kc, sl=sl, bk=bk,
                                     qloc=qloc: E.tensor_copy(
                                         PT[:, kc * 512 + qloc * 128:
                                            kc * 512 + (qloc + 1) * 128],
                                         ptrb2[bk][:, sl * 128:(sl + 1) * 128]),
                                     inc="v")
                                evacv[kc % 8] = P.v("v")
                if bphase < 3:
                    continue
                with contextlib.ExitStack() as ap_:
                    AL_ = ap_.enter_context
                    av = AL_(nc.sbuf_tensor(f"av{half}", [128, DC * 512], BF16))
                    vblk = [AL_(nc.sbuf_tensor(f"vb{half}{i}", [128, D], BF16))
                            for i in range(2)]
                    pb8 = [AL_(nc.psum_tensor(f"pav{half}{i}", [128, 512], F32))
                           for i in range(8)]
                    vb_wait = [0, 0]
                    P.wait("tensor", "v")
                    P.wait("sync", "cc", ccV)
                    P.wait("sync", "t")
                    P.wait("sync", "s")
                    P.wait("sync", "v")
                    P.wait("gpsimd", "t")
                    for p8 in range(2):
                        for kc in range(32):
                            bi = kc % 2
                            if vb_wait[bi]:
                                P.wait("sync", "t", vb_wait[bi])
                            g, r = kc // 8, (kc % 8) * 128
                            P.op("sync", lambda E, bi=bi, g=g, r=r: E.dma_start(
                                out=vblk[bi][:], in_=vfull_d.ap()[g, r:r + 128, :]),
                                inc=("vy0", "vy1")[bi])
                            P.wait("tensor", ("vy0", "vy1")[bi])
                            for ei in range(8):
                                ec = p8 * 8 + ei
                                P.op("tensor", lambda E, bi=bi, ec=ec, ei=ei, kc=kc:
                                     E.matmul(pb8[ei][:, :],
                                              vblk[bi][:, ec * 128:(ec + 1) * 128],
                                              PT[:, kc * 512:(kc + 1) * 512],
                                              start=(kc == 0), stop=(kc == 31)),
                                     inc=("t" if ei == 7 else None))
                            vb_wait[bi] = P.v("t")
                        P.wait("vector", "t")
                        for ei in range(8):
                            ec = p8 * 8 + ei
                            P.op("vector", lambda E, ec=ec, ei=ei: E.tensor_copy(
                                av[:, ec * 512:(ec + 1) * 512], pb8[ei][:, :]),
                                inc=("v" if ei == 7 else None))
                        P.wait("tensor", "v")
                    with contextlib.ExitStack() as ows:
                        OL = ows.enter_context
                        wo = [OL(nc.sbuf_tensor(f"wo{half}{i}", [128, DC * 128], BF16))
                              for i in range(2)]
                        for i in range(DC):
                            buf, sem = wo[i % 2], ("dw0", "dw1")[i % 2]
                            if i == 0:
                                load_wchunk(buf, sem, Wo_b, DC, 0, 128)
                            P.wait("tensor", sem)
                            if i + 1 < DC:
                                load_wchunk(wo[(i + 1) % 2],
                                            ("dw0", "dw1")[(i + 1) % 2],
                                            Wo_b, DC, (i + 1) * 128, 128)
                            for kc in range(DC):
                                P.op("tensor", lambda E, buf=buf, kc=kc: E.matmul(
                                    pb8[0][:, :], buf[:, kc * 128:(kc + 1) * 128],
                                    av[:, kc * 512:(kc + 1) * 512],
                                    start=(kc == 0), stop=(kc == DC - 1)),
                                    inc=("t" if kc == DC - 1 else None))
                            P.wait("vector", "t")
                            P.op("vector", lambda E, dc=i, half=half: E.tensor_add(
                                xs(dc, half * 512, (half + 1) * 512),
                                xs(dc, half * 512, (half + 1) * 512),
                                pb8[0][:, :]), inc="v")
                            P.wait("tensor", "v")
                            wchunk_done(buf, P.v("t"))
        barrier()

    sublayer_attn_a()
    if upto >= 2:
        sublayer_ffn("f1")
    if upto >= 3:
        sublayer_attn_b()
    if upto >= 4:
        sublayer_ffn("f2")

    P.wait("sync", "v")
    P.wait("sync", "t")
    P.op("sync", lambda E: E.dma_start(
        out=x_out.rearrange("(dc p) t -> p dc t", p=128),
        in_=x_sb[:].rearrange("p (dc t) -> p dc t", dc=DC)), inc="st")
    P.wait("sync", "st")
    P.wait("gpsimd", "st")

    with nc.Block() as block:
        @block.sync
        def _(e):
            for fn in P.streams["sync"]:
                fn(e)

        @block.gpsimd
        def _(e):
            for fn in P.streams["gpsimd"]:
                fn(e)

        @block.tensor
        def _(e):
            for fn in P.streams["tensor"]:
                fn(e)

        @block.vector
        def _(e):
            for fn in P.streams["vector"]:
                fn(e)

        @block.scalar
        def _(e):
            for fn in P.streams["scalar"]:
                fn(e)

    es.close()
    return nc


def kernel(**inp):
    x = np.asarray(inp["L0_attn_a_x"])
    WgT_h = np.ascontiguousarray(np.asarray(inp["L0_ffn_wq_gate"]).T)
    WuT_h = np.ascontiguousarray(np.asarray(inp["L0_ffn_wq_up"]).T)
    WdT_h = np.ascontiguousarray(np.asarray(inp["L0_ffn_wq_down"]).T)
    shared = {
        "Wq_a": np.asarray(inp["L0_attn_a_Wq"]), "bq_a": np.asarray(inp["L0_attn_a_bq"]),
        "Wk_a": np.asarray(inp["L0_attn_a_Wk"]), "bk_a": np.asarray(inp["L0_attn_a_bk"]),
        "Wv_a": np.asarray(inp["L0_attn_a_Wv"]), "bv_a": np.asarray(inp["L0_attn_a_bv"]),
        "Wal": np.asarray(inp["L0_attn_a_Walpha"]),
        "bal": np.asarray(inp["L0_attn_a_balpha"]),
        "Wbe": np.asarray(inp["L0_attn_a_Wbeta"]),
        "bbe": np.asarray(inp["L0_attn_a_bbeta"]),
        "Wq_b": np.asarray(inp["L0_attn_b_W_q"]), "Wk_b": np.asarray(inp["L0_attn_b_W_k"]),
        "Wv_b": np.asarray(inp["L0_attn_b_W_v"]), "Wo_b": np.asarray(inp["L0_attn_b_W_o"]),
        "WgT": WgT_h, "WuT": WuT_h, "WdT": WdT_h,
        "sg_in": np.asarray(inp["L0_ffn_s_gate"]),
        "su_in": np.asarray(inp["L0_ffn_s_up"]),
        "sd_in": np.asarray(inp["L0_ffn_s_down"]),
    }
    per_core = []
    for c in range(8):
        b, s = divmod(c, 4)
        d = dict(shared)
        d["x_in"] = np.ascontiguousarray(x[b, s * TOK:(s + 1) * TOK, :].T)
        gsel = np.zeros((NH, 4), dtype=np.float32)
        gsel[:, :s] = 1.0
        d["gsel_in"] = gsel
        per_core.append(d)

    upto = int(os.environ.get("MK_DEV_UPTO", "4"))
    nc = build_program(upto)
    res = run_bass_kernel_spmd(nc, per_core, core_ids=list(range(8)))
    out = np.empty((2, 4096, 2048), dtype=np.float32)
    for c in range(8):
        b, s = divmod(c, 4)
        out[b, s * TOK:(s + 1) * TOK, :] = np.asarray(res.results[c]["x_out"]).T
    return out


# version 05
# speedup vs baseline: 1.1757x; 1.0113x over previous
import os
import contextlib
import numpy as np
import concourse.bass as bass
import concourse.mybir as mybir
from concourse.bass_utils import run_bass_kernel_spmd

F32 = mybir.dt.float32
BF16 = mybir.dt.bfloat16
I32 = mybir.dt.int32
AF = mybir.ActivationFunctionType
ALU = mybir.AluOpType
AX = mybir.AxisListType

TOK = 1024
DC = 16
D = 2048
HID = 5632
HC = 44
NH = 16
NCH = 8
QSCALE = float(1.0 / np.sqrt(2048.0))

DMA_SEMS = {"dg", "dy", "dw0", "dw1", "dw2", "dw3", "ky0", "ky1", "vy0", "vy1", "st"}
ENG_SEMS = {"t", "v", "s", "g", "cc"}

SEMS = {}


class Plan:
    ENGINES = ("tensor", "vector", "scalar", "gpsimd", "sync")

    def __init__(self):
        self.streams = {e: [] for e in self.ENGINES}
        self.semv = {}
        self.last_wait = {}

    def v(self, s):
        return self.semv.get(s, 0)

    def inc(self, s):
        by = 16 if s in DMA_SEMS else 1
        self.semv[s] = self.semv.get(s, 0) + by
        return self.semv[s]

    def emit(self, eng, fn):
        self.streams[eng].append(fn)

    def wait(self, eng, s, val=None):
        val = self.semv.get(s, 0) if val is None else val
        if val <= 0 or val <= self.last_wait.get((eng, s), 0):
            return
        self.last_wait[(eng, s)] = val
        self.emit(eng, lambda E, s=s, val=val: E.wait_ge(SEMS[s], val))

    def op(self, eng, build, inc=None):
        if inc is not None:
            self.inc(inc)
        self.emit(eng, lambda E, b=build, i=inc:
                  (b(E).then_inc(SEMS[i], 16 if i in DMA_SEMS else 1) if i else b(E)))


def build_program(upto=4):
    nc = bass.Bass("TRN2", target_bir_lowering=False,
                   detect_race_conditions=False)
    P = Plan()

    x_in = nc.dram_tensor("x_in", [D, TOK], F32, kind="ExternalInput").ap()
    x_out = nc.dram_tensor("x_out", [D, TOK], F32, kind="ExternalOutput").ap()
    Wq_a = nc.dram_tensor("Wq_a", [D, D], F32, kind="ExternalInput").ap()
    Wk_a = nc.dram_tensor("Wk_a", [D, D], F32, kind="ExternalInput").ap()
    Wv_a = nc.dram_tensor("Wv_a", [D, D], F32, kind="ExternalInput").ap()
    bq_a = nc.dram_tensor("bq_a", [D], F32, kind="ExternalInput").ap()
    bk_a = nc.dram_tensor("bk_a", [D], F32, kind="ExternalInput").ap()
    bv_a = nc.dram_tensor("bv_a", [D], F32, kind="ExternalInput").ap()
    Wal = nc.dram_tensor("Wal", [D, NH], F32, kind="ExternalInput").ap()
    Wbe = nc.dram_tensor("Wbe", [D, NH], F32, kind="ExternalInput").ap()
    bal = nc.dram_tensor("bal", [NH], F32, kind="ExternalInput").ap()
    bbe = nc.dram_tensor("bbe", [NH], F32, kind="ExternalInput").ap()
    Wq_b = nc.dram_tensor("Wq_b", [D, D], F32, kind="ExternalInput").ap()
    Wk_b = nc.dram_tensor("Wk_b", [D, D], F32, kind="ExternalInput").ap()
    Wv_b = nc.dram_tensor("Wv_b", [D, D], F32, kind="ExternalInput").ap()
    Wo_b = nc.dram_tensor("Wo_b", [D, D], F32, kind="ExternalInput").ap()
    WgT = nc.dram_tensor("WgT", [D, HID], I32, kind="ExternalInput").ap()
    WuT = nc.dram_tensor("WuT", [D, HID], I32, kind="ExternalInput").ap()
    WdT = nc.dram_tensor("WdT", [HID, D], I32, kind="ExternalInput").ap()
    sg_in = nc.dram_tensor("sg_in", [HID], F32, kind="ExternalInput").ap()
    su_in = nc.dram_tensor("su_in", [HID], F32, kind="ExternalInput").ap()
    sd_in = nc.dram_tensor("sd_in", [D], F32, kind="ExternalInput").ap()
    gsel_in = nc.dram_tensor("gsel_in", [NH, 4], F32, kind="ExternalInput").ap()

    ssend_d = nc.dram_tensor("ssend_d", [128, 2049], F32)
    sgath_d = nc.dram_tensor("sgath_d", [4, 128, 2049], F32)
    kown_d = nc.dram_tensor("kown_d", [D, TOK], BF16)
    vown_d = nc.dram_tensor("vown_d", [TOK, D], BF16)
    kfull_d = nc.dram_tensor("kfull_d", [4, D, TOK], BF16)
    vfull_d = nc.dram_tensor("vfull_d", [4, TOK, D], BF16)
    mid_d = nc.dram_tensor("mid_d", [HC, 128, TOK], BF16)
    tap_names = [t for t in os.environ.get("MK_TAP", "").split(",") if t]
    tap_d = {t: nc.dram_tensor(f"tap_{i}", [128, 16384], F32, kind="ExternalOutput")
             for i, t in enumerate(tap_names)}

    def tap(name, ap):
        if name not in tap_d:
            return
        n = min(ap.shape[-1] if ap.ndim == 2 else 0, 16384)
        part = ap.shape[0]
        P.wait("gpsimd", "v"); P.wait("gpsimd", "t"); P.wait("gpsimd", "s")
        P.op("gpsimd", lambda E, ap=ap, n=n, part=part: E.dma_start(
            out=tap_d[name].ap()[0:part, 0:n], in_=ap[:, 0:n]), inc="dg")
        P.wait("gpsimd", "dg")

    es = contextlib.ExitStack()
    E = es.enter_context

    x_sb = E(nc.sbuf_tensor("x_sb", [128, DC * TOK], F32))
    ident = E(nc.sbuf_tensor("ident", [128, 128], BF16))
    ident32 = E(nc.sbuf_tensor("ident32", [128, 128], F32))
    maskneg = E(nc.sbuf_tensor("maskneg", [128, 128], F32))
    EH = E(nc.sbuf_tensor("EH", [16, NH * 128], BF16))
    ONEH = E(nc.sbuf_tensor("ONEH", [128, NH * 16], BF16))
    ones_r = E(nc.sbuf_tensor("ones_r", [1, 128], BF16))
    ones_rf = E(nc.sbuf_tensor("ones_rf", [1, 128], F32))
    ones_c = E(nc.sbuf_tensor("ones_c", [128, 1], BF16))
    neg128_r = E(nc.sbuf_tensor("neg128_r", [1, 128], BF16))
    dummy = E(nc.sbuf_tensor("dumt", [1, 64], F32))
    bq_sb = E(nc.sbuf_tensor("bq_sb", [128, DC], F32))
    bk_sb = E(nc.sbuf_tensor("bk_sb", [128, DC], F32))
    ba_sb = E(nc.sbuf_tensor("ba_sb", [16, 1], F32))
    bb_sb = E(nc.sbuf_tensor("bb_sb", [16, 1], F32))
    sg_sb = E(nc.sbuf_tensor("sg_sb", [128, HC], F32))
    su_sb = E(nc.sbuf_tensor("su_sb", [128, HC], F32))
    sd_sb = E(nc.sbuf_tensor("sd_sb", [128, DC], F32))
    gsel_sb = E(nc.sbuf_tensor("gsel_sb", [16, 4], F32))
    c1em6 = E(nc.sbuf_tensor("c1em6", [128, 1], F32))
    nc.const_aps.register(F32, 1e-6, c1em6[:])

    for s in sorted(DMA_SEMS | ENG_SEMS):
        SEMS[s] = E(nc.semaphore(s))

    def xs(dc, t0, t1):
        return x_sb[:, dc * TOK + t0: dc * TOK + t1]

    def barrier():
        for s in sorted(DMA_SEMS):
            P.wait("gpsimd", s)
        P.wait("gpsimd", "cc")
        P.op("tensor", lambda E: E.ldweights(ident[:, 0:1]), inc="t")
        P.op("vector", lambda E: E.memset(dummy[:, 0:16], 0.0), inc="v")
        P.op("scalar", lambda E: E.activation(dummy[:, 16:32], dummy[:, 0:16], AF.Copy),
             inc="s")
        P.op("gpsimd", lambda E: E.memset(dummy[:, 32:48], 0.0), inc="g")
        for eng in P.ENGINES:
            for s in ("t", "v", "s", "g"):
                P.wait(eng, s)

    def vdrain():
        P.wait("vector", "v")

    def rsqrt_refined(dst, src, tmp):
        P.wait("scalar", "v")
        P.op("scalar", lambda E: E.activation(tmp, src, AF.Sqrt), inc="s")
        P.wait("vector", "s")
        P.op("vector", lambda E: E.reciprocal(dst, tmp), inc="v")
        vdrain()
        P.op("vector", lambda E: E.tensor_mul(tmp, src, dst), inc="v")
        vdrain()
        P.op("vector", lambda E: E.tensor_mul(tmp, tmp, dst), inc="v")
        vdrain()
        P.op("vector", lambda E: E.tensor_scalar(tmp, tmp, -0.5, 1.5, ALU.mult,
                                                 ALU.add), inc="v")
        vdrain()
        P.op("vector", lambda E: E.tensor_mul(dst, dst, tmp), inc="v")
        vdrain()

    wsem_state = {}
    fm_halves = {}

    def load_wchunk(buf, sem, src_ap, nkc, col0, ncols):
        lastuse = wsem_state.get(id(buf), 0)
        if lastuse:
            P.wait("gpsimd", "t", lastuse)
        src = src_ap.rearrange("(kc p) m -> kc p m", p=128)
        P.op("gpsimd",
             lambda E, b=buf, s=src, c0=col0, ncc=ncols, k=nkc:
             E.dma_start(out=b[:, 0:k * ncc].rearrange("p (kc m) -> p kc m", kc=k),
                         in_=s[:, :, c0:c0 + ncc].transpose([1, 0, 2])),
             inc=sem)

    def wchunk_done(buf, tval):
        wsem_state[id(buf)] = tval

    def load_wchunk_rows(buf, sem, src_ap, nkc, row_kc0, dc):
        lastuse = wsem_state.get(id(buf), 0)
        if lastuse:
            P.wait("gpsimd", "t", lastuse)
        src = src_ap.rearrange("(kc p) m -> kc p m", p=128)
        P.op("gpsimd",
             lambda E, b=buf, s=src, k0=row_kc0, k=nkc, dc=dc:
             E.dma_start(out=b[:, 0:k * 128].rearrange("p (kc m) -> p kc m", kc=k),
                         in_=s[k0:k0 + k, :, dc * 128:(dc + 1) * 128]
                         .transpose([1, 0, 2])),
             inc=sem)

    def emit_ln_stats(pb, xb, xq, st):
        mean_t, rst_t, scr2a, scr2b = st
        prev_t = 0
        for dc in range(DC):
            for tb in range(2):
                if prev_t:
                    P.wait("scalar", "t", prev_t)
                a, b = tb * 512, (tb + 1) * 512
                P.op("scalar", lambda E, dc=dc, a=a, b=b: E.activation(
                    xb[:], xs(dc, a, b), AF.Copy))
                P.op("scalar", lambda E, dc=dc, a=a, b=b: E.activation(
                    xq[:], xs(dc, a, b), AF.Square), inc="s")
                P.wait("tensor", "s")
                P.op("tensor", lambda E, dc=dc, tb=tb: E.matmul(
                    pb[tb][0:1, :], ones_c[:], xb[:],
                    start=(dc == 0), stop=(dc == DC - 1)))
                P.op("tensor", lambda E, dc=dc, tb=tb: E.matmul(
                    pb[2 + tb][0:1, :], ones_c[:], xq[:],
                    start=(dc == 0), stop=(dc == DC - 1)), inc="t")
                prev_t = P.v("t")
        P.wait("vector", "t")
        P.op("vector", lambda E: E.tensor_scalar_mul(mean_t[0:1, 0:512], pb[0][0:1, :],
                                                     1.0 / 2048.0))
        P.op("vector", lambda E: E.tensor_scalar_mul(mean_t[0:1, 512:1024],
                                                     pb[1][0:1, :], 1.0 / 2048.0),
             inc="v")
        vdrain()
        for hb in range(2):
            sl = hb * 512
            P.op("vector", lambda E, hb=hb, sl=sl: E.tensor_scalar_mul(
                scr2a[:], pb[2 + hb][0:1, :], 1.0 / 2048.0), inc="v")
            P.op("vector", lambda E, sl=sl: E.tensor_mul(
                rst_t[0:1, sl:sl + 512], mean_t[0:1, sl:sl + 512],
                mean_t[0:1, sl:sl + 512]), inc="v")
            vdrain()
            P.op("vector", lambda E, sl=sl: E.tensor_sub(
                scr2a[:], scr2a[:], rst_t[0:1, sl:sl + 512]), inc="v")
            vdrain()
            P.op("vector", lambda E: E.tensor_scalar_add(
                scr2a[:], scr2a[:], 1e-5), inc="v")
            vdrain()
            rsqrt_refined(rst_t[0:1, sl:sl + 512], scr2a[:], scr2b[:])

    def emit_ln_norm(pb, xn_sb, xoff, t0, t1, st):
        mean_t, rst_t, scr2a, scr2b = st
        P.wait("tensor", "v")
        ntb = (t1 - t0) // 512
        for tb in range(ntb):
            a = t0 + tb * 512
            P.op("tensor", lambda E, tb=tb, a=a: E.matmul(
                pb[tb][:, :], ones_rf[:], mean_t[0:1, a:a + 512],
                start=True, stop=True))
            P.op("tensor", lambda E, tb=tb, a=a: E.matmul(
                pb[2 + tb][:, :], ones_rf[:], rst_t[0:1, a:a + 512],
                start=True, stop=True), inc=("t" if tb == ntb - 1 else None))
        P.wait("vector", "t")
        for dc in range(DC):
            for tb in range(ntb):
                a = t0 + tb * 512
                o = xoff(dc, a)
                dst = xn_sb[:, o:o + 512]
                P.op("vector", lambda E, a=a, dst=dst, tb=tb, dc=dc: E.tensor_sub(
                    dst, xs(dc, a, a + 512), pb[tb][:, :]))
                P.op("vector", lambda E, dst=dst, tb=tb: E.tensor_mul(
                    dst, dst, pb[2 + tb][:, :]), inc="v")

    def emit_proj_fm(xn_sb, xn_tok, w_src, wbufs, wsems, ec_list, t0, t1, pb2, evac):
        P.wait("tensor", "v")
        ntb = (t1 - t0) // 512
        buf = wbufs[0]
        nb = len(pb2)
        guards = [None] * nb
        src2 = w_src.rearrange("(kc p) m -> kc p m", p=128)
        hlast = fm_halves.setdefault(id(buf), [0, 0])

        def load_half(ec, hb):
            w = max(hlast[hb], wsem_state.get(id(buf), 0))
            if w:
                P.wait("gpsimd", "t", w)
            P.op("gpsimd", lambda E, ec=ec, hb=hb: E.dma_start(
                out=buf[:, hb * 8 * 128:(hb + 1) * 8 * 128]
                .rearrange("p (kc m) -> p kc m", kc=8),
                in_=src2[hb * 8:(hb + 1) * 8, :, ec * 128:(ec + 1) * 128]
                .transpose([1, 0, 2])), inc=wsems[hb])

        load_half(ec_list[0], 0)
        for i, ec in enumerate(ec_list):
            for tb in range(ntb):
                bi = (i * ntb + tb) % nb
                bank = pb2[bi]
                if guards[bi]:
                    P.wait("tensor", guards[bi][0], guards[bi][1])
                for hb in range(2):
                    P.wait("tensor", wsems[hb])
                    if tb == 0 and hb == 0:
                        load_half(ec, 1)
                    for kcl in range(8):
                        kc = hb * 8 + kcl
                        P.op("tensor", lambda E, b=bank, kc=kc, tb=tb:
                             E.matmul(b[:, :], buf[:, kc * 128:(kc + 1) * 128],
                                      xn_sb[:, kc * xn_tok + t0 + tb * 512:
                                            kc * xn_tok + t0 + (tb + 1) * 512],
                                      start=(kc == 0), stop=(kc == DC - 1)),
                             inc=("t" if (kcl == 7) else None))
                    hlast[hb] = P.v("t")
                    if tb == ntb - 1 and hb == 1 and i + 1 < len(ec_list):
                        load_half(ec_list[i + 1], 0)
                guards[bi] = evac(ec, tb, bank)
        wchunk_done(buf, max(hlast))
        for g in guards:
            if g:
                P.wait("tensor", g[0], g[1])

    def emit_proj_tm(xn_sb, xn_tok, w_src, wvb, t0, tcs, pbl, bias_pair, evac):
        P.wait("tensor", "v")
        src = w_src.rearrange("(kc p) m -> kc p m", p=128)
        bias_use = 0
        guards = [None] * len(pbl)
        for eb in range(4):
            if bias_pair is not None:
                bias_row, bias_dram = bias_pair
                if bias_use:
                    P.wait("gpsimd", "t", bias_use)
                P.op("gpsimd", lambda E, eb=eb: E.dma_start(
                    out=bias_row[:], in_=bias_dram[eb * 512:(eb + 1) * 512]
                    .unsqueeze(0)), inc="dg")
            for kc2 in range(DC // 2):
                i = eb * (DC // 2) + kc2
                buf, sem = wvb[i % 2], ("dw2", "dw3")[i % 2]
                lastuse = wsem_state.get(id(buf), 0)
                if lastuse:
                    P.wait("gpsimd", "t", lastuse)
                P.op("gpsimd", lambda E, b=buf, s=src, kc2=kc2, eb=eb: E.dma_start(
                    out=b[:].rearrange("p (u m) -> p u m", u=2),
                    in_=s[2 * kc2:2 * kc2 + 2, :, eb * 512:(eb + 1) * 512]
                    .transpose([1, 0, 2])), inc=sem)
                P.wait("tensor", sem)
                for u in range(2):
                    kc = 2 * kc2 + u
                    if kc == 0:
                        for ti in range(len(tcs)):
                            if guards[ti]:
                                P.wait("tensor", guards[ti][0], guards[ti][1])
                    for ti, tc in enumerate(tcs):
                        stop_k = (kc == DC - 1) and (bias_pair is None)
                        P.op("tensor", lambda E, b=pbl[ti], buf=buf, kc=kc, tc=tc,
                             u=u, stop_k=stop_k:
                             E.matmul(b[:, :],
                                      xn_sb[:, kc * xn_tok + t0 + tc * 128:
                                            kc * xn_tok + t0 + (tc + 1) * 128],
                                      buf[:, u * 512:(u + 1) * 512],
                                      start=(kc == 0), stop=stop_k),
                             inc=("t" if ti == len(tcs) - 1 else None))
                wchunk_done(buf, P.v("t"))
            if bias_pair is not None:
                P.wait("tensor", "dg")
            for ti, tc in enumerate(tcs):
                if bias_pair is not None:
                    P.op("tensor", lambda E, b=pbl[ti]: E.matmul(
                        b[:, :], ones_r[:], bias_pair[0][0:1, :],
                        start=False, stop=True), inc="t")
                guards[ti] = evac(tc, eb, pbl[ti])
            bias_use = P.v("t")
        for g in guards:
            if g:
                P.wait("tensor", g[0], g[1])

    P.op("sync", lambda E: E.dma_start(
        out=x_sb[:].rearrange("p (dc t) -> p dc t", dc=DC),
        in_=x_in.rearrange("(dc p) t -> p dc t", p=128)), inc="dy")
    P.op("gpsimd", lambda E: E.dma_start(
        out=bq_sb[:], in_=bq_a.rearrange("(dc p) -> p dc", p=128), allow_slow_non_contiguous=True), inc="dg")
    P.op("gpsimd", lambda E: E.dma_start(
        out=bk_sb[:], in_=bk_a.rearrange("(dc p) -> p dc", p=128), allow_slow_non_contiguous=True), inc="dg")
    P.op("gpsimd", lambda E: E.dma_start(out=ba_sb[:], in_=bal.unsqueeze(1)), inc="dg")
    P.op("gpsimd", lambda E: E.dma_start(out=bb_sb[:], in_=bbe.unsqueeze(1)), inc="dg")
    P.op("gpsimd", lambda E: E.dma_start(
        out=sg_sb[:], in_=sg_in.rearrange("(hc p) -> p hc", p=128), allow_slow_non_contiguous=True), inc="dg")
    P.op("gpsimd", lambda E: E.dma_start(
        out=su_sb[:], in_=su_in.rearrange("(hc p) -> p hc", p=128), allow_slow_non_contiguous=True), inc="dg")
    P.op("gpsimd", lambda E: E.dma_start(
        out=sd_sb[:], in_=sd_in.rearrange("(dc p) -> p dc", p=128), allow_slow_non_contiguous=True), inc="dg")
    P.op("gpsimd", lambda E: E.dma_start(out=gsel_sb[:], in_=gsel_in), inc="dg")
    P.op("gpsimd", lambda E: E.memset(ident[:], 0.0))
    P.op("gpsimd", lambda E: E.affine_select(
        out=ident[:], in_=ident[:], compare_op=ALU.not_equal, fill=1.0,
        base=0, pattern=[[-1, 128]], channel_multiplier=1))
    P.op("gpsimd", lambda E: E.memset(ident32[:], 0.0))
    P.op("gpsimd", lambda E: E.affine_select(
        out=ident32[:], in_=ident32[:], compare_op=ALU.not_equal, fill=1.0,
        base=0, pattern=[[-1, 128]], channel_multiplier=1))
    P.op("gpsimd", lambda E: E.memset(maskneg[:], 0.0))
    P.op("gpsimd", lambda E: E.affine_select(
        out=maskneg[:], in_=maskneg[:], compare_op=ALU.is_ge, fill=-1e30,
        base=0, pattern=[[1, 128]], channel_multiplier=-1))
    P.op("gpsimd", lambda E: E.memset(EH[:], 1.0))
    for h in range(NH):
        P.op("gpsimd", lambda E, h=h: E.affine_select(
            out=EH[:, h * 128:(h + 1) * 128], in_=EH[:, h * 128:(h + 1) * 128],
            compare_op=ALU.is_ge, fill=0.0, base=-256 * h, pattern=[[1, 128]],
            channel_multiplier=256))
        P.op("gpsimd", lambda E, h=h: E.affine_select(
            out=EH[:, h * 128:(h + 1) * 128], in_=EH[:, h * 128:(h + 1) * 128],
            compare_op=ALU.is_ge, fill=0.0, base=255 + 256 * h, pattern=[[-1, 128]],
            channel_multiplier=-256))
    P.op("gpsimd", lambda E: E.memset(ONEH[:], 1.0))
    for h in range(NH):
        P.op("gpsimd", lambda E, h=h: E.affine_select(
            out=ONEH[:, h * 16:(h + 1) * 16], in_=ONEH[:, h * 16:(h + 1) * 16],
            compare_op=ALU.is_ge, fill=0.0, base=-256 * h, pattern=[[256, 16]],
            channel_multiplier=1))
        P.op("gpsimd", lambda E, h=h: E.affine_select(
            out=ONEH[:, h * 16:(h + 1) * 16], in_=ONEH[:, h * 16:(h + 1) * 16],
            compare_op=ALU.is_ge, fill=0.0, base=255 + 256 * h, pattern=[[-256, 16]],
            channel_multiplier=-1))
    P.op("gpsimd", lambda E: E.memset(ones_r[:], 1.0))
    P.op("gpsimd", lambda E: E.memset(ones_rf[:], 1.0))
    P.op("gpsimd", lambda E: E.memset(ones_c[:], 1.0))
    P.op("gpsimd", lambda E: E.memset(c1em6[:], 1e-6))
    P.op("gpsimd", lambda E: E.memset(dummy[:], 0.0))
    P.op("gpsimd", lambda E: E.memset(neg128_r[:], -128.0), inc="g")
    for e in ("tensor", "vector", "scalar"):
        P.wait(e, "g"); P.wait(e, "dg"); P.wait(e, "dy")

    def sublayer_attn_a():
        with contextlib.ExitStack() as ss:
            SE = ss.enter_context
            cla_hh = SE(nc.sbuf_tensor("cla_hh", [16, 512], F32))
            lbmc_hh = SE(nc.sbuf_tensor("lbmc_hh", [16, 512], F32))
            rn_hh = SE(nc.sbuf_tensor("rn_hh", [16, 512], F32))
            scr_hh = SE(nc.sbuf_tensor("scr_hh", [16, 512], F32))
            ends_h = SE(nc.sbuf_tensor("ends_h", [16, 16], F32))
            ecb_t = SE(nc.sbuf_tensor("ecb_t", [16, TOK], BF16))
            eef = SE(nc.sbuf_tensor("eef", [16, 8], F32))
            EH32 = SE(nc.sbuf_tensor("EH32", [16, NH * 128], F32))
            UTRI = SE(nc.sbuf_tensor("UTRI", [128, 128], F32))
            atm = SE(nc.sbuf_tensor("atm", [128, NCH * 16], F32))
            btm = SE(nc.sbuf_tensor("btm", [128, NCH * 16], F32))
            latm = SE(nc.sbuf_tensor("latm", [128, NCH * 16], F32))
            cla_tm = SE(nc.sbuf_tensor("cla_tm", [128, NCH * 16], F32))
            w_tm = SE(nc.sbuf_tensor("w_tm", [128, NCH * 16], F32))
            etd_sb = SE(nc.sbuf_tensor("etd_sb", [128, 128], F32))
            ba_row = SE(nc.sbuf_tensor("ba_row", [1, 2 * NH], F32))
            S_send = SE(nc.sbuf_tensor("S_send", [128, 2049], F32))
            snap_bf = SE(nc.sbuf_tensor("snap_bf", [128, 128], BF16))
            expd_a = SE(nc.sbuf_tensor("expd_a", [128, 128], F32))
            expd_b = SE(nc.sbuf_tensor("expd_b", [128, 128], F32))
            attnf = SE(nc.sbuf_tensor("attnf", [128, 128], BF16))
            kw_sb = SE(nc.sbuf_tensor("kw_sb", [128, 128], BF16))
            qcd = SE(nc.sbuf_tensor("qcd", [128, 128], BF16))
            q_sb = SE(nc.sbuf_tensor("q_sb", [128, NH * TOK], BF16))
            pb = [SE(nc.psum_tensor(f"paa{i}", [128, 512], F32)) for i in range(6)]
            ptr = SE(nc.psum_tensor("ptra", [128, 512], BF16))

            P.op("gpsimd", lambda E: E.memset(EH32[:], 1.0))
            for h in range(NH):
                P.op("gpsimd", lambda E, h=h: E.affine_select(
                    out=EH32[:, h * 128:(h + 1) * 128],
                    in_=EH32[:, h * 128:(h + 1) * 128],
                    compare_op=ALU.is_ge, fill=0.0, base=-256 * h,
                    pattern=[[1, 128]], channel_multiplier=256))
                P.op("gpsimd", lambda E, h=h: E.affine_select(
                    out=EH32[:, h * 128:(h + 1) * 128],
                    in_=EH32[:, h * 128:(h + 1) * 128],
                    compare_op=ALU.is_ge, fill=0.0, base=255 + 256 * h,
                    pattern=[[-1, 128]], channel_multiplier=-256))
            P.op("gpsimd", lambda E: E.memset(UTRI[:], 1.0))
            P.op("gpsimd", lambda E: E.affine_select(
                out=UTRI[:], in_=UTRI[:], compare_op=ALU.is_ge, fill=0.0,
                base=0, pattern=[[1, 128]], channel_multiplier=-1))
            P.op("gpsimd", lambda E: E.memset(S_send[:], 0.0), inc="g")
            P.op("gpsimd", lambda E: E.dma_start(
                out=ba_row[:, 0:NH], in_=bal.unsqueeze(0)), inc="dg")
            P.op("gpsimd", lambda E: E.dma_start(
                out=ba_row[:, NH:], in_=bbe.unsqueeze(0)), inc="dg")
            P.wait("vector", "g")
            P.wait("tensor", "g")
            P.wait("tensor", "dg")

            st_a = tuple(SE(nc.sbuf_tensor(f"sta{i}", [1, sz], F32))
                         for i, sz in enumerate((TOK, TOK, 512, 512)))
            inner = contextlib.ExitStack()
            IE = inner.enter_context
            bv_row = IE(nc.sbuf_tensor("bv_row", [1, 512], BF16))
            xb_s = IE(nc.sbuf_tensor("a_xb", [128, 512], BF16))
            xq_s = IE(nc.sbuf_tensor("a_xq", [128, 512], BF16))
            ksq = xq_s

            emit_ln_stats(pb[0:4], xb_s, xq_s, st_a)

            wb0 = IE(nc.sbuf_tensor("wba0", [128, DC * 128], BF16))
            wb = [wb0, wb0]
            wvb = [IE(nc.sbuf_tensor(f"wva{i}", [128, 1024], BF16)) for i in range(2)]
            wab = IE(nc.sbuf_tensor("wab", [128, 2 * DC * 16], BF16))
            wsems = ["dw0", "dw1"]
            P.op("gpsimd", lambda E: E.dma_start(
                out=wab[:, 0:DC * 16].rearrange("p (dc m) -> p dc m", dc=DC),
                in_=Wal.rearrange("(dc p) m -> p dc m", p=128)), inc="dg")
            P.op("gpsimd", lambda E: E.dma_start(
                out=wab[:, DC * 16:].rearrange("p (dc m) -> p dc m", dc=DC),
                in_=Wbe.rearrange("(dc p) m -> p dc m", p=128)), inc="dg")

            xn = IE(nc.sbuf_tensor("xn", [128, DC * 512], BF16))
            k_sb = IE(nc.sbuf_tensor("k_sb", [128, NH * 512], BF16))
            v_sb = IE(nc.sbuf_tensor("v_sb", [128, 4 * D], BF16))

            for half in range(2):
                t0, t1 = half * 512, (half + 1) * 512
                emit_ln_norm(pb[0:4], xn, lambda dc, a, t0=t0: dc * 512 + (a - t0),
                             t0, t1, st_a)

                if half == 0:
                    tap("xn0", xn[:])
                def evac_q(ec, tb, bank, t0=t0):
                    P.wait("scalar", "t", P.v("t"))
                    P.op("scalar", lambda E, ec=ec, b=bank, t0=t0: E.activation(
                        q_sb[:, ec * TOK + t0: ec * TOK + t0 + 512],
                        b[:, :], AF.Identity, bias=bq_sb[:, ec:ec + 1]), inc="s")
                    return ("s", P.v("s"))
                emit_proj_fm(xn, 512, Wq_a, wb, wsems, list(range(DC)), 0, 512,
                             pb[0:2], evac_q)

                if half == 0:
                    tap("q0", q_sb[:])
                P.wait("tensor", "dg")
                P.wait("tensor", "v")
                for q_i in range(2):
                    w0 = q_i * DC * 16
                    for nl in range(4):
                        n = half * 4 + nl
                        bank = pb[2 + (nl % 2)]
                        for kc in range(DC):
                            P.op("tensor", lambda E, b=bank, w0=w0, kc=kc, nl=nl:
                                 E.matmul(b[:, 0:16],
                                          xn[:, kc * 512 + nl * 128:
                                             kc * 512 + (nl + 1) * 128],
                                          wab[:, w0 + kc * 16: w0 + (kc + 1) * 16],
                                          start=(kc == 0), stop=False))
                        P.op("tensor", lambda E, b=bank, q_i=q_i: E.matmul(
                            b[:, 0:16], ones_rf[:],
                            ba_row[0:1, q_i * NH:(q_i + 1) * NH],
                            start=False, stop=True), inc="t")
                        P.wait("scalar", "t")
                        if q_i == 0:
                            P.op("scalar", lambda E, b=bank, n=n: E.activation(
                                atm[:, n * 16:(n + 1) * 16], b[:, 0:16], AF.Sigmoid),
                                inc="s")
                        else:
                            P.op("scalar", lambda E, b=bank, n=n: E.activation(
                                btm[:, n * 16:(n + 1) * 16], b[:, 0:16], AF.Identity),
                                inc="s")
                        P.wait("tensor", "s")
                sl_h = slice(half * 4 * 16, (half + 1) * 4 * 16)
                P.op("scalar", lambda E, sl=sl_h: E.activation(
                    latm[:, sl], btm[:, sl], AF.Abs))
                P.op("scalar", lambda E, sl=sl_h: E.activation(
                    latm[:, sl], latm[:, sl], AF.Exp, scale=-1.0))
                P.op("scalar", lambda E, sl=sl_h: E.activation(
                    latm[:, sl], latm[:, sl], AF.Ln, bias=1.0), inc="s")
                P.wait("vector", "s")
                P.op("vector", lambda E, sl=sl_h: E.tensor_scalar_max(
                    btm[:, sl], btm[:, sl], 0.0), inc="v")
                vdrain()
                P.op("vector", lambda E, sl=sl_h: E.tensor_add(
                    btm[:, sl], btm[:, sl], latm[:, sl]), inc="v")
                vdrain()
                P.wait("scalar", "v")
                P.op("scalar", lambda E, sl=sl_h: E.activation(
                    latm[:, sl], atm[:, sl], AF.Ln, bias=1e-6), inc="s")
                P.wait("tensor", "s")
                for nl in range(4):
                    n = half * 4 + nl
                    P.op("tensor", lambda E, n=n, nl=nl: E.matmul(
                        pb[4][:, nl * 16:(nl + 1) * 16], UTRI[:],
                        latm[:, n * 16:(n + 1) * 16], start=True, stop=True),
                        inc=("t" if nl == 3 else None))
                P.wait("vector", "t")
                for nl in range(4):
                    n = half * 4 + nl
                    P.op("vector", lambda E, n=n, nl=nl: E.tensor_copy(
                        cla_tm[:, n * 16:(n + 1) * 16],
                        pb[4][:, nl * 16:(nl + 1) * 16]),
                        inc=("v" if nl == 3 else None))
                P.wait("tensor", "v")
                for nl in range(4):
                    n = half * 4 + nl
                    P.op("tensor", lambda E, n=n, nl=nl: E.transpose(
                        pb[5][0:16, nl * 128:(nl + 1) * 128],
                        cla_tm[:, n * 16:(n + 1) * 16], ident32[:]),
                        inc=("t" if nl == 3 else None))
                P.wait("vector", "t")
                for nl in range(4):
                    P.op("vector", lambda E, nl=nl: E.tensor_copy(
                        cla_hh[:, nl * 128:(nl + 1) * 128],
                        pb[5][0:16, nl * 128:(nl + 1) * 128]),
                        inc=("v" if nl == 3 else None))
                vdrain()
                cends = cla_hh[:].rearrange("h (n c) -> h n c", c=128)[:, :, 127]
                P.op("vector", lambda E, half=half, cends=cends: E.tensor_copy(
                    ends_h[:, half * 4:(half + 1) * 4], cends), inc="v")
                vdrain()
                if half == 0:
                    tap("clh", cla_hh[:])
                    tap("ctm", cla_tm[:])
                P.wait("scalar", "v")
                P.op("scalar", lambda E, t0=t0, t1=t1: E.activation(
                    ecb_t[:, t0:t1], cla_hh[:], AF.Exp), inc="s")
                eendsh = ecb_t[:, 0:TOK].rearrange(
                    "h (n c) -> h n c", c=128)[:, half * 4:(half + 1) * 4, 127]
                P.wait("vector", "s")
                P.op("vector", lambda E, half=half, eendsh=eendsh: E.tensor_copy(
                    eef[:, half * 4:(half + 1) * 4], eendsh), inc="v")
                vdrain()
                P.wait("tensor", "s")
                P.wait("tensor", "v")
                for h in range(NH):
                    P.op("tensor", lambda E, h=h, half=half: E.matmul(
                        pb[4][:, h * 8 + half * 4: h * 8 + half * 4 + 4],
                        EH32[:, h * 128:(h + 1) * 128],
                        eef[:, half * 4:(half + 1) * 4],
                        start=(h == 0), stop=(h == NH - 1)),
                        inc=("t" if h == NH - 1 else None))
                P.wait("vector", "t")
                for h in range(NH):
                    P.op("vector", lambda E, h=h, half=half: E.tensor_copy(
                        etd_sb[:, h * 8 + half * 4: h * 8 + half * 4 + 4],
                        pb[4][:, h * 8 + half * 4: h * 8 + half * 4 + 4]),
                        inc=("v" if h == NH - 1 else None))
                vdrain()

                def evac_k(ec, tb, bank):
                    P.wait("scalar", "t", P.v("t"))
                    P.op("scalar", lambda E, ec=ec, b=bank: E.activation(
                        k_sb[:, ec * 512:(ec + 1) * 512], b[:, :], AF.Identity,
                        bias=bk_sb[:, ec:ec + 1]), inc="s")
                    return ("s", P.v("s"))
                emit_proj_fm(xn, 512, Wk_a, wb, wsems, list(range(DC)), 0, 512,
                             pb[0:2], evac_k)

                def evac_v(tc, eb, bank):
                    P.wait("vector", "t", P.v("t"))
                    P.op("vector", lambda E, b=bank, tc=tc, eb=eb: E.tensor_copy(
                        v_sb[:, tc * D + eb * 512: tc * D + (eb + 1) * 512], b[:, :]),
                        inc="v")
                    return ("v", P.v("v"))
                emit_proj_tm(xn, 512, Wv_a, wvb, 0, list(range(4)), pb[0:4],
                             (bv_row, bv_a), evac_v)

                prev_t = 0
                for h in range(NH):
                    if prev_t:
                        P.wait("scalar", "t", prev_t)
                    P.wait("scalar", "v")
                    P.op("scalar", lambda E, h=h: E.activation(
                        ksq[:], k_sb[:, h * 512:(h + 1) * 512], AF.Square), inc="s")
                    P.wait("tensor", "s")
                    P.op("tensor", lambda E, h=h: E.matmul(
                        pb[4][0:16, :], ONEH[:, h * 16:(h + 1) * 16], ksq[:],
                        start=(h == 0), stop=(h == NH - 1)), inc="t")
                    prev_t = P.v("t")
                P.wait("vector", "t")
                P.op("vector", lambda E: E.tensor_scalar_max(
                    scr_hh[:], pb[4][0:16, :], 1e-24), inc="v")
                rsqrt_refined(rn_hh[:], scr_hh[:], lbmc_hh[:])
                P.wait("tensor", "v")
                for nl in range(4):
                    n = half * 4 + nl
                    P.op("tensor", lambda E, n=n, nl=nl: E.transpose(
                        pb[5][0:16, nl * 128:(nl + 1) * 128],
                        btm[:, n * 16:(n + 1) * 16], ident32[:]),
                        inc=("t" if nl == 3 else None))
                P.wait("vector", "t")
                for nl in range(4):
                    P.op("vector", lambda E, nl=nl: E.tensor_copy(
                        scr_hh[:, nl * 128:(nl + 1) * 128],
                        pb[5][0:16, nl * 128:(nl + 1) * 128]),
                        inc=("v" if nl == 3 else None))
                vdrain()
                P.op("vector", lambda E: E.tensor_mul(scr_hh[:], scr_hh[:], rn_hh[:]),
                     inc="v")
                vdrain()
                P.wait("scalar", "v")
                P.op("scalar", lambda E: E.activation(lbmc_hh[:], scr_hh[:], AF.Ln),
                     inc="s")
                P.wait("vector", "s")
                P.op("vector", lambda E: E.tensor_sub(lbmc_hh[:], lbmc_hh[:],
                                                      cla_hh[:]), inc="v")
                vdrain()
                for nl in range(4):
                    ce = cla_hh[:, nl * 128 + 127: nl * 128 + 128]
                    P.op("vector", lambda E, nl=nl, ce=ce: E.tensor_add(
                        scr_hh[:, nl * 128:(nl + 1) * 128],
                        lbmc_hh[:, nl * 128:(nl + 1) * 128],
                        ce.broadcast_to((16, 128))),
                        inc=("v" if nl == 3 else None))
                vdrain()
                P.wait("scalar", "v")
                P.op("scalar", lambda E: E.activation(scr_hh[:], scr_hh[:], AF.Exp),
                     inc="s")
                P.wait("tensor", "s")
                for nl in range(4):
                    n = half * 4 + nl
                    P.op("tensor", lambda E, n=n, nl=nl: E.transpose(
                        pb[5][:, nl * 16:(nl + 1) * 16],
                        scr_hh[:, nl * 128:(nl + 1) * 128], ident32[0:16, 0:16]),
                        inc=("t" if nl == 3 else None))
                P.wait("vector", "t")
                for nl in range(4):
                    n = half * 4 + nl
                    P.op("vector", lambda E, n=n, nl=nl: E.tensor_copy(
                        w_tm[:, n * 16:(n + 1) * 16], pb[5][:, nl * 16:(nl + 1) * 16]),
                        inc=("v" if nl == 3 else None))
                vdrain()

                if half == 0:
                    tap("v0", v_sb[:])
                for nl in range(4):
                    n = half * 4 + nl
                    if half == 0 and nl == 1:
                        tap("s0", S_send[:, 0:2048])
                    for h in range(NH):
                        qt = q_sb[:, h * TOK + n * 128: h * TOK + (n + 1) * 128]
                        kt = k_sb[:, h * 512 + nl * 128: h * 512 + (nl + 1) * 128]
                        vt = v_sb[:, nl * D + h * 128: nl * D + (h + 1) * 128]
                        clat = cla_hh[:, nl * 128:(nl + 1) * 128]
                        ecbt = ecb_t[:, n * 128:(n + 1) * 128]
                        lbt = lbmc_hh[:, nl * 128:(nl + 1) * 128]
                        ehs = EH[:, h * 128:(h + 1) * 128]
                        ehs32 = EH32[:, h * 128:(h + 1) * 128]
                        P.op("vector", lambda E, h=h: E.tensor_copy(
                            snap_bf[:], S_send[:, h * 128:(h + 1) * 128]), inc="v")
                        P.wait("tensor", "v")
                        P.op("tensor", lambda E, kt=kt: E.transpose(
                            ptr[:, 0:128], kt, ident[:]))
                        P.op("tensor", lambda E, kt=kt, qt=qt: E.matmul(
                            pb[0][:, 0:128], kt, qt, start=True, stop=True))
                        P.op("tensor", lambda E, lbt=lbt, ehs32=ehs32: E.matmul(
                            pb[1][:, 0:128], lbt, ehs32, start=True, stop=False))
                        P.op("tensor", lambda E, clat=clat, ehs32=ehs32: E.matmul(
                            pb[1][:, 0:128], ehs32, clat, start=False, stop=True))
                        P.op("tensor", lambda E, ecbt=ecbt, ehs=ehs: E.matmul(
                            pb[4][:, 0:128], ehs, ecbt, start=True, stop=True),
                            inc="t")
                        P.wait("vector", "t")
                        P.op("vector", lambda E: E.tensor_add(
                            expd_a[:], pb[1][:, 0:128], maskneg[:]), inc="v")
                        P.wait("scalar", "v")
                        P.op("scalar", lambda E: E.activation(
                            expd_b[:], expd_a[:], AF.Exp), inc="s")
                        P.wait("vector", "s")
                        P.op("vector", lambda E: E.tensor_mul(
                            attnf[:], pb[0][:, 0:128], expd_b[:]))
                        P.op("vector", lambda E, n=n, h=h: E.tensor_scalar_mul(
                            kw_sb[:], ptr[:, 0:128],
                            w_tm[:, n * 16 + h: n * 16 + h + 1]))
                        P.op("vector", lambda E, qt=qt: E.tensor_mul(
                            qcd[:], qt, pb[4][:, 0:128]), inc="v")
                        P.wait("tensor", "v")
                        P.op("tensor", lambda E, vt=vt: E.matmul(
                            pb[2][:, 0:128], kw_sb[:], vt, start=True, stop=True))
                        P.op("tensor", lambda E, vt=vt: E.matmul(
                            pb[3][:, 0:128], vt, attnf[:], start=True, stop=False))
                        P.op("tensor", lambda E: E.matmul(
                            pb[3][:, 0:128], snap_bf[:], qcd[:],
                            start=False, stop=True), inc="t")
                        P.wait("vector", "t")
                        P.op("vector", lambda E, h=h, n=n: E.tensor_add(
                            xs(h, n * 128, (n + 1) * 128),
                            xs(h, n * 128, (n + 1) * 128), pb[3][:, 0:128]))
                        P.op("vector", lambda E, h=h, n=n: E.scalar_tensor_tensor(
                            S_send[:, h * 128:(h + 1) * 128],
                            S_send[:, h * 128:(h + 1) * 128],
                            etd_sb[:, h * 8 + n: h * 8 + n + 1],
                            pb[2][:, 0:128], ALU.mult, ALU.add), inc="v")

            P.wait("gpsimd", "dw0"); P.wait("gpsimd", "dw2"); P.wait("gpsimd", "dw3")
            inner.close()

            gcb_t = SE(nc.sbuf_tensor("gcb_t", [16, TOK], F32))
            tap("end", ends_h[:])
            P.op("vector", lambda E: E.tensor_reduce(
                S_send[0:16, 2048:2049], ends_h[:, 0:8], AX.X, ALU.add), inc="v")
            vdrain()
            P.wait("gpsimd", "v")
            P.op("gpsimd", lambda E: E.memset(ends_h[:, 8:16], 0.0), inc="g")
            P.wait("vector", "g")
            P.op("vector", lambda E: E.tensor_tensor_scan(
                latm[0:16, 0:8], ends_h[:, 0:8], ends_h[:, 8:16], 0.0,
                ALU.add, ALU.add), inc="v")
            vdrain()
            P.op("vector", lambda E: E.tensor_copy(ends_h[:, 9:16], latm[0:16, 0:7]),
                 inc="v")
            vdrain()
            for half in range(2):
                P.wait("tensor", "v")
                for nl in range(4):
                    n = half * 4 + nl
                    P.op("tensor", lambda E, n=n, nl=nl: E.transpose(
                        pb[5][0:16, nl * 128:(nl + 1) * 128],
                        cla_tm[:, n * 16:(n + 1) * 16], ident32[:]),
                        inc=("t" if nl == 3 else None))
                P.wait("vector", "t")
                for nl in range(4):
                    n = half * 4 + nl
                    P.op("vector", lambda E, n=n, nl=nl: E.tensor_add(
                        cla_hh[:, nl * 128:(nl + 1) * 128],
                        pb[5][0:16, nl * 128:(nl + 1) * 128],
                        ends_h[:, 8 + n: 9 + n].broadcast_to((16, 128))),
                        inc=("v" if nl == 3 else None))
                P.wait("scalar", "v")
                P.op("scalar", lambda E, half=half: E.activation(
                    gcb_t[:, half * 512:(half + 1) * 512], cla_hh[:], AF.Exp),
                    inc="s")
                P.wait("tensor", "s")

            P.wait("gpsimd", "v")
            P.op("gpsimd", lambda E: E.dma_start(out=ssend_d.ap()[:, :], in_=S_send[:]),
                 inc="dg")
            P.wait("gpsimd", "dg")
            P.op("gpsimd", lambda E: E.collective_compute(
                "AllGather", ALU.bypass, replica_groups=[[0, 1, 2, 3], [4, 5, 6, 7]],
                ins=[ssend_d.ap().opt()], outs=[sgath_d.ap().opt()]), inc="cc")
            P.wait("gpsimd", "cc")
            P.wait("sync", "cc")

            S_in = SE(nc.sbuf_tensor("S_in", [128, 2048], F32))
            S_in_bf = SE(nc.sbuf_tensor("S_in_bf", [128, 2048], BF16))
            gla = SE(nc.sbuf_tensor("gla", [16, 16], F32))
            fc_sb = SE(nc.sbuf_tensor("fc_sb", [128, 64], F32))
            gbuf = [SE(nc.sbuf_tensor(f"gbuf{i}", [128, 2049], F32)) for i in range(2)]
            for p_ in range(4):
                P.op("sync", lambda E, p_=p_: E.dma_start(
                    out=gla[:, p_:p_ + 1], in_=sgath_d.ap()[p_, 0:16, 2048:2049],
                    allow_slow_non_contiguous=True), inc="dy")
            P.wait("vector", "dy")
            P.op("vector", lambda E: E.tensor_mul(gla[:, 4:8], gla[:, 0:4], gsel_sb[:]),
                 inc="v")
            vdrain()
            P.wait("gpsimd", "v")
            P.op("gpsimd", lambda E: E.memset(gla[:, 11:12], 0.0), inc="g")
            P.wait("vector", "g")
            P.op("vector", lambda E: E.tensor_copy(gla[:, 10:11], gla[:, 7:8]),
                 inc="v")
            vdrain()
            P.op("vector", lambda E: E.tensor_add(gla[:, 9:10], gla[:, 10:11],
                                                  gla[:, 6:7]), inc="v")
            vdrain()
            P.op("vector", lambda E: E.tensor_add(gla[:, 8:9], gla[:, 9:10],
                                                  gla[:, 5:6]), inc="v")
            vdrain()
            P.wait("scalar", "v")
            P.op("scalar", lambda E: E.activation(gla[:, 12:16], gla[:, 8:12],
                                                  AF.Exp), inc="s")
            P.wait("vector", "s")
            P.op("vector", lambda E: E.tensor_mul(gla[:, 12:16], gla[:, 12:16],
                                                  gsel_sb[:]), inc="v")
            vdrain()
            P.wait("tensor", "v")
            for h in range(NH):
                P.op("tensor", lambda E, h=h: E.matmul(
                    pb[4][:, h * 4:(h + 1) * 4], EH32[:, h * 128:(h + 1) * 128],
                    gla[:, 12:16], start=(h == 0), stop=(h == NH - 1)),
                    inc=("t" if h == NH - 1 else None))
            P.wait("vector", "t")
            P.op("vector", lambda E: E.tensor_copy(fc_sb[:], pb[4][:, 0:64]), inc="v")
            P.wait("gpsimd", "v")
            P.op("gpsimd", lambda E: E.memset(S_in[:], 0.0), inc="g")
            P.wait("vector", "g")
            gb_wait = [0, 0]
            for p_ in range(4):
                b = gbuf[p_ % 2]
                if gb_wait[p_ % 2]:
                    P.wait("sync", "v", gb_wait[p_ % 2])
                P.op("sync", lambda E, p_=p_, b=b: E.dma_start(
                    out=b[:], in_=sgath_d.ap()[p_]), inc="dy")
                P.wait("vector", "dy")
                for h in range(NH):
                    P.op("vector", lambda E, b=b, h=h, p_=p_: E.scalar_tensor_tensor(
                        S_in[:, h * 128:(h + 1) * 128], b[:, h * 128:(h + 1) * 128],
                        fc_sb[:, h * 4 + p_: h * 4 + p_ + 1],
                        S_in[:, h * 128:(h + 1) * 128], ALU.mult, ALU.add),
                        inc=("v" if h == NH - 1 else None))
                vdrain()
                gb_wait[p_ % 2] = P.v("v")
            vdrain()
            P.op("vector", lambda E: E.tensor_copy(S_in_bf[:], S_in[:]), inc="v")
            tap("sin", S_in[:])
            tap("gla", gla[:])
            tap("gsl", gsel_sb[:])
            tap("fc", fc_sb[:])
            tap("gcb", gcb_t[:])
            tap("ehh", ends_h[:])

            qcd4 = SE(nc.sbuf_tensor("qcd4", [128, 512], BF16))
            for n in range(NCH):
                gcbt = gcb_t[:, n * 128:(n + 1) * 128]
                for hg in range(NH // 4):
                    P.wait("tensor", "v")
                    for j in range(4):
                        h = hg * 4 + j
                        P.op("tensor", lambda E, gcbt=gcbt, h=h, j=j: E.matmul(
                            pb[0][:, j * 128:(j + 1) * 128],
                            EH32[:, h * 128:(h + 1) * 128], gcbt,
                            start=True, stop=True),
                            inc=("t" if j == 3 else None))
                    P.wait("vector", "t")
                    for j in range(4):
                        h = hg * 4 + j
                        P.op("vector", lambda E, h=h, j=j, n=n: E.tensor_mul(
                            qcd4[:, j * 128:(j + 1) * 128],
                            q_sb[:, h * TOK + n * 128: h * TOK + (n + 1) * 128],
                            pb[0][:, j * 128:(j + 1) * 128]),
                            inc=("v" if j == 3 else None))
                    P.wait("tensor", "v")
                    for j in range(4):
                        h = hg * 4 + j
                        P.op("tensor", lambda E, h=h, j=j: E.matmul(
                            pb[1][:, j * 128:(j + 1) * 128],
                            S_in_bf[:, h * 128:(h + 1) * 128],
                            qcd4[:, j * 128:(j + 1) * 128],
                            start=True, stop=True),
                            inc=("t" if j == 3 else None))
                    P.wait("vector", "t")
                    for j in range(4):
                        h = hg * 4 + j
                        P.op("vector", lambda E, h=h, j=j, n=n: E.tensor_add(
                            xs(h, n * 128, (n + 1) * 128),
                            xs(h, n * 128, (n + 1) * 128),
                            pb[1][:, j * 128:(j + 1) * 128]),
                            inc=("v" if j == 3 else None))
        barrier()

    def sublayer_ffn(tag):
        with contextlib.ExitStack() as ss:
            SE = ss.enter_context
            csum = SE(nc.sbuf_tensor(f"csum{tag}", [1, TOK], BF16))
            msum = SE(nc.sbuf_tensor(f"msum{tag}", [1, TOK], BF16))
            t1sb = [SE(nc.sbuf_tensor(f"t1sb{tag}{i}", [128, 512], F32))
                    for i in range(2)]
            midst = [SE(nc.sbuf_tensor(f"mst{tag}{i}", [128, TOK], BF16))
                     for i in range(2)]
            pb = [SE(nc.psum_tensor(f"pf{tag}{i}", [128, 512], F32)) for i in range(6)]
            with contextlib.ExitStack() as xs_scope:
                XL = xs_scope.enter_context
                xn = XL(nc.sbuf_tensor(f"xnf{tag}", [128, DC * TOK], BF16))
                xb_s = XL(nc.sbuf_tensor(f"fxb{tag}", [128, 512], BF16))
                xq_s = XL(nc.sbuf_tensor(f"fxq{tag}", [128, 512], BF16))
                st_f = tuple(XL(nc.sbuf_tensor(f"stf{tag}{i}", [1, sz], F32))
                             for i, sz in enumerate((TOK, TOK, 512, 512)))
                emit_ln_stats(pb[0:4], xb_s, xq_s, st_f)
                emit_ln_norm(pb[0:4], xn, lambda dc, a: dc * TOK + a, 0, TOK, st_f)
                P.wait("tensor", "v")
                for tb in range(2):
                    for kc in range(DC):
                        P.op("tensor", lambda E, tb=tb, kc=kc: E.matmul(
                            pb[4 + tb][0:1, :], ones_c[:],
                            xn[:, kc * TOK + tb * 512: kc * TOK + (tb + 1) * 512],
                            start=(kc == 0), stop=(kc == DC - 1)),
                            inc=("t" if kc == DC - 1 else None))
                P.wait("vector", "t")
                P.op("vector", lambda E: E.tensor_copy(csum[:, 0:512], pb[4][0:1, :]))
                P.op("vector", lambda E: E.tensor_copy(csum[:, 512:1024],
                                                       pb[5][0:1, :]), inc="v")
                with contextlib.ExitStack() as ws:
                    WL = ws.enter_context
                    wg = [WL(nc.sbuf_tensor(f"wg{tag}{i}", [128, DC * 128], BF16))
                          for i in range(2)]
                    wu = [WL(nc.sbuf_tensor(f"wu{tag}{i}", [128, DC * 128], BF16))
                          for i in range(2)]
                    P.wait("tensor", "v")
                    mid_dma = [0, 0]
                    bank_guard = [0, 0]
                    for hc in range(HC):
                        bg, bu = wg[hc % 2], wu[hc % 2]
                        sg_, su_ = ("dw0", "dw1")[hc % 2], ("dw2", "dw3")[hc % 2]
                        if hc == 0:
                            load_wchunk(bg, sg_, WgT, DC, 0, 128)
                            load_wchunk(bu, su_, WuT, DC, 0, 128)
                        P.wait("tensor", sg_)
                        P.wait("tensor", su_)
                        if hc + 1 < HC:
                            load_wchunk(wg[(hc + 1) % 2], ("dw0", "dw1")[(hc + 1) % 2],
                                        WgT, DC, (hc + 1) * 128, 128)
                            load_wchunk(wu[(hc + 1) % 2], ("dw2", "dw3")[(hc + 1) % 2],
                                        WuT, DC, (hc + 1) * 128, 128)
                        mst = midst[hc % 2]
                        for tb in range(2):
                            gbank, ubank = pb[tb * 2], pb[tb * 2 + 1]
                            if bank_guard[tb]:
                                P.wait("tensor", "v", bank_guard[tb])
                            for kc in range(DC):
                                P.op("tensor", lambda E, bg=bg, kc=kc, tb=tb,
                                     b=gbank: E.matmul(
                                         b[:, :], bg[:, kc * 128:(kc + 1) * 128],
                                         xn[:, kc * TOK + tb * 512:
                                            kc * TOK + (tb + 1) * 512],
                                         start=(kc == 0), stop=False))
                            P.op("tensor", lambda E, tb=tb, b=gbank: E.matmul(
                                b[:, :], neg128_r[:],
                                csum[0:1, tb * 512:(tb + 1) * 512],
                                start=False, stop=True), inc="t")
                            for kc in range(DC):
                                P.op("tensor", lambda E, bu=bu, kc=kc, tb=tb,
                                     b=ubank: E.matmul(
                                         b[:, :], bu[:, kc * 128:(kc + 1) * 128],
                                         xn[:, kc * TOK + tb * 512:
                                            kc * TOK + (tb + 1) * 512],
                                         start=(kc == 0), stop=False))
                            P.op("tensor", lambda E, tb=tb, b=ubank: E.matmul(
                                b[:, :], neg128_r[:],
                                csum[0:1, tb * 512:(tb + 1) * 512],
                                start=False, stop=True), inc="t")
                            tu = P.v("t")
                            P.wait("scalar", "t", tu)
                            P.op("scalar", lambda E, hc=hc, tb=tb, b=gbank: E.activation(
                                t1sb[tb][:], b[:, :], AF.Silu,
                                scale=sg_sb[:, hc:hc + 1]), inc="s")
                            P.wait("vector", "s", P.v("s"))
                            if mid_dma[hc % 2]:
                                P.wait("vector", ("vy0", "vy1")[hc % 2],
                                       mid_dma[hc % 2])
                            P.op("vector", lambda E, mst=mst, hc=hc, tb=tb, b=ubank:
                                 E.scalar_tensor_tensor(
                                     mst[:, tb * 512:(tb + 1) * 512],
                                     b[:, :], su_sb[:, hc:hc + 1], t1sb[tb][:],
                                     ALU.mult, ALU.mult), inc="v")
                            bank_guard[tb] = P.v("v")
                        P.wait("sync", "v", P.v("v"))
                        P.op("sync", lambda E, mst=mst, hc=hc: E.dma_start(
                            out=mid_d.ap()[hc], in_=mst[:]),
                            inc=("vy0", "vy1")[hc % 2])
                        mid_dma[hc % 2] = P.v(("vy0", "vy1")[hc % 2])
                        wchunk_done(bg, P.v("t"))
                        wchunk_done(bu, P.v("t"))
            NG = 2
            GH = HC // NG
            with contextlib.ExitStack() as ws:
                WL = ws.enter_context
                P.wait("gpsimd", "t")
                P.wait("sync", "t")
                mgrp = WL(nc.sbuf_tensor(f"mg{tag}", [128, GH * TOK], BF16))
                wd = [WL(nc.sbuf_tensor(f"wd{tag}{i}", [128, GH * 128], BF16))
                      for i in range(2)]
                down_guard = [0, 0]
                P.wait("tensor", "v")
                P.wait("tensor", "s")
                for hg in range(NG):
                    P.wait("sync", ("vy0", "vy1")[0])
                    P.wait("sync", ("vy0", "vy1")[1])
                    if hg > 0:
                        P.wait("sync", "t")
                    P.op("sync", lambda E, hg=hg: E.dma_start(
                        out=mgrp[:].rearrange("p (h t) -> p h t", h=GH),
                        in_=mid_d.ap()[hg * GH:(hg + 1) * GH].transpose([1, 0, 2])),
                        inc="ky0")
                    P.wait("tensor", "ky0")
                    for tb in range(2):
                        for hc in range(GH):
                            P.op("tensor", lambda E, tb=tb, hc=hc, hg=hg: E.matmul(
                                pb[4 + tb][0:1, :], ones_c[:],
                                mgrp[:, hc * TOK + tb * 512: hc * TOK + (tb + 1) * 512],
                                start=(hg == 0 and hc == 0),
                                stop=(hg == NG - 1 and hc == GH - 1)),
                                inc=("t" if hc == GH - 1 else None))
                    if hg == NG - 1:
                        P.wait("vector", "t")
                        P.op("vector", lambda E: E.tensor_copy(msum[:, 0:512],
                                                               pb[4][0:1, :]))
                        P.op("vector", lambda E: E.tensor_copy(msum[:, 512:1024],
                                                               pb[5][0:1, :]), inc="v")
                        P.wait("tensor", "v")
                    for dc in range(DC):
                        bd = wd[dc % 2]
                        sd_ = ("dw0", "dw1")[dc % 2]
                        if dc == 0:
                            load_wchunk_rows(bd, sd_, WdT, GH, hg * GH, dc)
                        P.wait("tensor", sd_)
                        if dc + 1 < DC:
                            load_wchunk_rows(wd[(dc + 1) % 2],
                                             ("dw0", "dw1")[(dc + 1) % 2],
                                             WdT, GH, hg * GH, dc + 1)
                        for tb in range(2):
                            bank = pb[tb]
                            if down_guard[tb]:
                                P.wait("tensor", "v", down_guard[tb])
                            last = NG - 1
                            for hc in range(GH):
                                fin = (hg < last and hc == GH - 1)
                                P.op("tensor",
                                     lambda E, b=bank, bd=bd, hc=hc, tb=tb, fin=fin:
                                     E.matmul(b[:, :], bd[:, hc * 128:(hc + 1) * 128],
                                              mgrp[:, hc * TOK + tb * 512:
                                                   hc * TOK + (tb + 1) * 512],
                                              start=(hc == 0), stop=fin),
                                     inc=("t" if fin else None))
                            if hg == last:
                                P.op("tensor", lambda E, b=bank, tb=tb: E.matmul(
                                    b[:, :], neg128_r[:],
                                    msum[0:1, tb * 512:(tb + 1) * 512],
                                    start=False, stop=True), inc="t")
                            P.wait("vector", "t", P.v("t"))
                            P.op("vector", lambda E, b=bank, dc=dc, tb=tb:
                                 E.scalar_tensor_tensor(
                                     xs(dc, tb * 512, (tb + 1) * 512), b[:, :],
                                     sd_sb[:, dc:dc + 1],
                                     xs(dc, tb * 512, (tb + 1) * 512),
                                     ALU.mult, ALU.add), inc="v")
                            down_guard[tb] = P.v("v")
                        wchunk_done(bd, P.v("t"))
        barrier()

    def sublayer_attn_b():
        with contextlib.ExitStack() as ss:
            SE = ss.enter_context
            q_sb = SE(nc.sbuf_tensor("qb_sb", [128, DC * TOK], BF16))
            PT = SE(nc.sbuf_tensor("PT", [128, 32 * 512], BF16))
            with contextlib.ExitStack() as ps:
                PL = ps.enter_context
                stg = PL(nc.sbuf_tensor("stg", [128, 512], BF16))
                xn = PL(nc.sbuf_tensor("xnb", [128, DC * TOK], BF16))
                pbs = [PL(nc.psum_tensor(f"pbb{i}", [128, 512], F32)) for i in range(8)]
                xb_s = PL(nc.sbuf_tensor("bxb", [128, 512], BF16))
                xq_s = PL(nc.sbuf_tensor("bxq", [128, 512], BF16))
                st_b = tuple(PL(nc.sbuf_tensor(f"stb{i}", [1, sz], F32))
                             for i, sz in enumerate((TOK, TOK, 512, 512)))
                emit_ln_stats(pbs[0:4], xb_s, xq_s, st_b)
                emit_ln_norm(pbs[0:4], xn, lambda dc, a: dc * TOK + a, 0, TOK, st_b)
                wb = [PL(nc.sbuf_tensor(f"wbb{i}", [128, DC * 128], BF16))
                      for i in range(2)]
                wvb = [PL(nc.sbuf_tensor(f"wvbb{i}", [128, 1024], BF16))
                       for i in range(2)]
                wsems = ["dw0", "dw1"]


                def evac_kb(ec, tb, bank):
                    P.wait("vector", "t", P.v("t"))
                    P.wait("vector", "dg")
                    P.op("vector", lambda E, b=bank: E.tensor_copy(stg[:], b[:, :]),
                         inc="v")
                    P.wait("gpsimd", "v", P.v("v"))
                    P.op("gpsimd", lambda E, ec=ec, tb=tb: E.dma_start(
                        out=kown_d.ap()[ec * 128:(ec + 1) * 128,
                                        tb * 512:(tb + 1) * 512],
                        in_=stg[:]), inc="dg")
                    return ("v", P.v("v"))
                emit_proj_fm(xn, TOK, Wk_b, wb, wsems, list(range(DC)), 0, TOK,
                             pbs[0:2], evac_kb)

                def evac_vb(tc, eb, bank):
                    P.wait("vector", "t", P.v("t"))
                    P.wait("vector", "dg")
                    P.op("vector", lambda E, b=bank: E.tensor_copy(stg[:], b[:, :]),
                         inc="v")
                    P.wait("gpsimd", "v", P.v("v"))
                    P.op("gpsimd", lambda E, tc=tc, eb=eb: E.dma_start(
                        out=vown_d.ap()[tc * 128:(tc + 1) * 128,
                                        eb * 512:(eb + 1) * 512],
                        in_=stg[:]), inc="dg")
                    return ("v", P.v("v"))
                emit_proj_tm(xn, TOK, Wv_b, wvb, 0, list(range(8)), pbs[0:8], None,
                             evac_vb)
                P.wait("gpsimd", "dg")
                P.op("gpsimd", lambda E: E.collective_compute(
                    "AllGather", ALU.bypass,
                    replica_groups=[[0, 1, 2, 3], [4, 5, 6, 7]],
                    ins=[kown_d.ap().opt()], outs=[kfull_d.ap().opt()]), inc="cc")
                ccK = P.v("cc")
                P.op("gpsimd", lambda E: E.collective_compute(
                    "AllGather", ALU.bypass,
                    replica_groups=[[0, 1, 2, 3], [4, 5, 6, 7]],
                    ins=[vown_d.ap().opt()], outs=[vfull_d.ap().opt()]), inc="cc")
                ccV = P.v("cc")

                def evac_qb(ec, tb, bank):
                    P.wait("scalar", "t", P.v("t"))
                    P.op("scalar", lambda E, ec=ec, tb=tb, b=bank: E.activation(
                        q_sb[:, ec * TOK + tb * 512: ec * TOK + (tb + 1) * 512],
                        b[:, :], AF.Identity, scale=QSCALE), inc="s")
                    return ("s", P.v("s"))
                emit_proj_fm(xn, TOK, Wq_b, wb, wsems, list(range(DC)), 0, TOK,
                             pbs[0:2], evac_qb)

            bphase = int(os.environ.get("MK_DEV_BPH", "4"))
            for half in range(2 if bphase >= 4 else 1):
                if bphase < 2:
                    break
                with contextlib.ExitStack() as sp:
                    SL = sp.enter_context
                    scsb = [SL(nc.sbuf_tensor(f"sc{half}{i}", [128, 4096], F32))
                            for i in range(2)]
                    psb = SL(nc.sbuf_tensor(f"ps{half}", [128, 4096], BF16))
                    rowst = SL(nc.sbuf_tensor(f"row{half}", [128, 8], F32))
                    kblk = [SL(nc.sbuf_tensor(f"kb{half}{i}", [128, DC * 512], BF16))
                            for i in range(2)]
                    pb2 = [SL(nc.psum_tensor(f"pbs{half}{i}", [128, 512], F32))
                           for i in range(4)]
                    ptrb2 = [SL(nc.psum_tensor(f"ptrb{half}{i}", [128, 512], BF16))
                             for i in range(2)]
                    kb_wait = [0, 0]
                    ptr_wait = [0] * 8
                    ev_g = [0, 0, 0, 0]
                    P.wait("tensor", "v")
                    P.wait("sync", "cc", ccK)
                    P.wait("sync", "t")
                    P.wait("sync", "v")
                    P.wait("sync", "s")
                    for pr in range(2):
                        qcs = [half * 4 + pr * 2, half * 4 + pr * 2 + 1]
                        for kb in range(8):
                            bi = kb % 2
                            if kb_wait[bi]:
                                P.wait("sync", "t", kb_wait[bi])
                            g, tl = kb // 2, (kb % 2) * 512
                            P.op("sync", lambda E, bi=bi, g=g, tl=tl: E.dma_start(
                                out=kblk[bi][:].rearrange("p (kc m) -> p kc m", kc=DC),
                                in_=kfull_d.ap()[g]
                                .rearrange("(kc p) t -> kc p t", p=128)
                                [:, :, tl:tl + 512].transpose([1, 0, 2])),
                                inc=("ky0", "ky1")[bi])
                            P.wait("tensor", ("ky0", "ky1")[bi])
                            for qi, qc in enumerate(qcs):
                                bk2 = qi * 2 + bi
                                if ev_g[bk2]:
                                    P.wait("tensor", "v", ev_g[bk2])
                                for kc in range(DC):
                                    P.op("tensor",
                                         lambda E, bk2=bk2, qc=qc, kc=kc, bi=bi:
                                         E.matmul(pb2[bk2][:, :],
                                                  q_sb[:, kc * TOK + qc * 128:
                                                       kc * TOK + (qc + 1) * 128],
                                                  kblk[bi][:,
                                                           kc * 512:(kc + 1) * 512],
                                                  start=(kc == 0),
                                                  stop=(kc == DC - 1)),
                                         inc=("t" if kc == DC - 1 else None))
                            kb_wait[bi] = P.v("t")
                            P.wait("vector", "t")
                            for qi in range(2):
                                bk2 = qi * 2 + bi
                                P.op("vector", lambda E, qi=qi, kb=kb, bk2=bk2:
                                     E.tensor_copy(
                                         scsb[qi][:, kb * 512:(kb + 1) * 512],
                                         pb2[bk2][:, :]), inc="v")
                                ev_g[bk2] = P.v("v")
                        vdrain()
                        for qi, qc in enumerate(qcs):
                            qloc = qc % 4
                            P.op("vector", lambda E, qi=qi, qloc=qloc: E.reduce_max(
                                rowst[:, qloc:qloc + 1], scsb[qi][:], axis=AX.X,
                                negate=True), inc="v")
                            vdrain()
                            P.wait("scalar", "v")
                            P.wait("scalar", "t")
                            if os.environ.get("MK_NOSM"):
                                P.op("scalar", lambda E, qi=qi: E.activation(
                                    psb[:], scsb[qi][:], AF.Identity), inc="s")
                            else:
                                P.op("scalar", lambda E, qi=qi, qloc=qloc: E.activation(
                                    psb[:], scsb[qi][:], AF.Exp,
                                    bias=rowst[:, qloc:qloc + 1],
                                    accum_out=rowst[:, 4 + qloc:5 + qloc]), inc="s")
                            P.wait("vector", "s")
                            P.op("vector", lambda E, qloc=qloc: E.reciprocal(
                                rowst[:, 4 + qloc:5 + qloc],
                                rowst[:, 4 + qloc:5 + qloc]), inc="v")
                            vdrain()
                            P.op("vector", lambda E, qloc=qloc: E.tensor_scalar_mul(
                                psb[:], psb[:], rowst[:, 4 + qloc:5 + qloc]),
                                inc="v")
                            vdrain()
                            P.wait("tensor", "v")
                            evacv = ptr_wait
                            for kc in range(0 if os.environ.get("MK_NOTR") else 32):
                                bk = kc % 2
                                sl = (kc // 2) % 4
                                if evacv[(kc - 2) % 8] and kc >= 2:
                                    P.wait("tensor", "v", evacv[(kc - 2) % 8])
                                elif evacv[(kc - 2) % 8]:
                                    P.wait("tensor", "v", evacv[(kc - 2) % 8])
                                P.op("tensor", lambda E, kc=kc, sl=sl, bk=bk:
                                     E.transpose(
                                         ptrb2[bk][:, sl * 128:(sl + 1) * 128],
                                         psb[:, kc * 128:(kc + 1) * 128], ident[:]),
                                     inc="t")
                                P.wait("vector", "t")
                                P.op("vector", lambda E, kc=kc, sl=sl, bk=bk,
                                     qloc=qloc: E.tensor_copy(
                                         PT[:, kc * 512 + qloc * 128:
                                            kc * 512 + (qloc + 1) * 128],
                                         ptrb2[bk][:, sl * 128:(sl + 1) * 128]),
                                     inc="v")
                                evacv[kc % 8] = P.v("v")
                if bphase < 3:
                    continue
                with contextlib.ExitStack() as ap_:
                    AL_ = ap_.enter_context
                    av = AL_(nc.sbuf_tensor(f"av{half}", [128, DC * 512], BF16))
                    vblk = [AL_(nc.sbuf_tensor(f"vb{half}{i}", [128, D], BF16))
                            for i in range(2)]
                    pb8 = [AL_(nc.psum_tensor(f"pav{half}{i}", [128, 512], F32))
                           for i in range(8)]
                    vb_wait = [0, 0]
                    P.wait("tensor", "v")
                    P.wait("sync", "cc", ccV)
                    P.wait("sync", "t")
                    P.wait("sync", "s")
                    P.wait("sync", "v")
                    P.wait("gpsimd", "t")
                    for p8 in range(2):
                        for kc in range(32):
                            bi = kc % 2
                            if vb_wait[bi]:
                                P.wait("sync", "t", vb_wait[bi])
                            g, r = kc // 8, (kc % 8) * 128
                            P.op("sync", lambda E, bi=bi, g=g, r=r: E.dma_start(
                                out=vblk[bi][:], in_=vfull_d.ap()[g, r:r + 128, :]),
                                inc=("vy0", "vy1")[bi])
                            P.wait("tensor", ("vy0", "vy1")[bi])
                            for ei in range(8):
                                ec = p8 * 8 + ei
                                P.op("tensor", lambda E, bi=bi, ec=ec, ei=ei, kc=kc:
                                     E.matmul(pb8[ei][:, :],
                                              vblk[bi][:, ec * 128:(ec + 1) * 128],
                                              PT[:, kc * 512:(kc + 1) * 512],
                                              start=(kc == 0), stop=(kc == 31)),
                                     inc=("t" if ei == 7 else None))
                            vb_wait[bi] = P.v("t")
                        P.wait("vector", "t")
                        for ei in range(8):
                            ec = p8 * 8 + ei
                            P.op("vector", lambda E, ec=ec, ei=ei: E.tensor_copy(
                                av[:, ec * 512:(ec + 1) * 512], pb8[ei][:, :]),
                                inc=("v" if ei == 7 else None))
                        P.wait("tensor", "v")
                    with contextlib.ExitStack() as ows:
                        OL = ows.enter_context
                        wo = [OL(nc.sbuf_tensor(f"wo{half}{i}", [128, DC * 128], BF16))
                              for i in range(2)]
                        for i in range(DC):
                            buf, sem = wo[i % 2], ("dw0", "dw1")[i % 2]
                            if i == 0:
                                load_wchunk(buf, sem, Wo_b, DC, 0, 128)
                            P.wait("tensor", sem)
                            if i + 1 < DC:
                                load_wchunk(wo[(i + 1) % 2],
                                            ("dw0", "dw1")[(i + 1) % 2],
                                            Wo_b, DC, (i + 1) * 128, 128)
                            for kc in range(DC):
                                P.op("tensor", lambda E, buf=buf, kc=kc: E.matmul(
                                    pb8[0][:, :], buf[:, kc * 128:(kc + 1) * 128],
                                    av[:, kc * 512:(kc + 1) * 512],
                                    start=(kc == 0), stop=(kc == DC - 1)),
                                    inc=("t" if kc == DC - 1 else None))
                            P.wait("vector", "t")
                            P.op("vector", lambda E, dc=i, half=half: E.tensor_add(
                                xs(dc, half * 512, (half + 1) * 512),
                                xs(dc, half * 512, (half + 1) * 512),
                                pb8[0][:, :]), inc="v")
                            P.wait("tensor", "v")
                            wchunk_done(buf, P.v("t"))
        barrier()

    sublayer_attn_a()
    if upto >= 2:
        sublayer_ffn("f1")
    if upto >= 3:
        sublayer_attn_b()
    if upto >= 4:
        sublayer_ffn("f2")

    P.wait("sync", "v")
    P.wait("sync", "t")
    P.op("sync", lambda E: E.dma_start(
        out=x_out.rearrange("(dc p) t -> p dc t", p=128),
        in_=x_sb[:].rearrange("p (dc t) -> p dc t", dc=DC)), inc="st")
    P.wait("sync", "st")
    P.wait("gpsimd", "st")

    with nc.Block() as block:
        @block.sync
        def _(e):
            for fn in P.streams["sync"]:
                fn(e)

        @block.gpsimd
        def _(e):
            for fn in P.streams["gpsimd"]:
                fn(e)

        @block.tensor
        def _(e):
            for fn in P.streams["tensor"]:
                fn(e)

        @block.vector
        def _(e):
            for fn in P.streams["vector"]:
                fn(e)

        @block.scalar
        def _(e):
            for fn in P.streams["scalar"]:
                fn(e)

    es.close()
    return nc


def kernel(**inp):
    x = np.asarray(inp["L0_attn_a_x"])
    WgT_h = np.ascontiguousarray(np.asarray(inp["L0_ffn_wq_gate"]).T)
    WuT_h = np.ascontiguousarray(np.asarray(inp["L0_ffn_wq_up"]).T)
    WdT_h = np.ascontiguousarray(np.asarray(inp["L0_ffn_wq_down"]).T)
    shared = {
        "Wq_a": np.asarray(inp["L0_attn_a_Wq"]), "bq_a": np.asarray(inp["L0_attn_a_bq"]),
        "Wk_a": np.asarray(inp["L0_attn_a_Wk"]), "bk_a": np.asarray(inp["L0_attn_a_bk"]),
        "Wv_a": np.asarray(inp["L0_attn_a_Wv"]), "bv_a": np.asarray(inp["L0_attn_a_bv"]),
        "Wal": np.asarray(inp["L0_attn_a_Walpha"]),
        "bal": np.asarray(inp["L0_attn_a_balpha"]),
        "Wbe": np.asarray(inp["L0_attn_a_Wbeta"]),
        "bbe": np.asarray(inp["L0_attn_a_bbeta"]),
        "Wq_b": np.asarray(inp["L0_attn_b_W_q"]), "Wk_b": np.asarray(inp["L0_attn_b_W_k"]),
        "Wv_b": np.asarray(inp["L0_attn_b_W_v"]), "Wo_b": np.asarray(inp["L0_attn_b_W_o"]),
        "WgT": WgT_h, "WuT": WuT_h, "WdT": WdT_h,
        "sg_in": np.asarray(inp["L0_ffn_s_gate"]),
        "su_in": np.asarray(inp["L0_ffn_s_up"]),
        "sd_in": np.asarray(inp["L0_ffn_s_down"]),
    }
    per_core = []
    for c in range(8):
        b, s = divmod(c, 4)
        d = dict(shared)
        d["x_in"] = np.ascontiguousarray(x[b, s * TOK:(s + 1) * TOK, :].T)
        gsel = np.zeros((NH, 4), dtype=np.float32)
        gsel[:, :s] = 1.0
        d["gsel_in"] = gsel
        per_core.append(d)

    upto = int(os.environ.get("MK_DEV_UPTO", "4"))
    nc = build_program(upto)
    res = run_bass_kernel_spmd(nc, per_core, core_ids=list(range(8)))
    out = np.empty((2, 4096, 2048), dtype=np.float32)
    for c in range(8):
        b, s = divmod(c, 4)
        out[b, s * TOK:(s + 1) * TOK, :] = np.asarray(res.results[c]["x_out"]).T
    return out


# version 07 (final)
# speedup vs baseline: 1.1931x; 1.0148x over previous
import os
import contextlib
import numpy as np
import concourse.bass as bass
import concourse.mybir as mybir
from concourse.bass_utils import run_bass_kernel_spmd

F32 = mybir.dt.float32
BF16 = mybir.dt.bfloat16
I32 = mybir.dt.int32
AF = mybir.ActivationFunctionType
ALU = mybir.AluOpType
AX = mybir.AxisListType

TOK = 1024
DC = 16
D = 2048
HID = 5632
HC = 44
NH = 16
NCH = 8
QSCALE = float(1.0 / np.sqrt(2048.0))

DMA_SEMS = {"dg", "dy", "dw0", "dw1", "dw2", "dw3", "ky0", "ky1", "vy0", "vy1", "st"}
ENG_SEMS = {"t", "v", "s", "g", "cc"}

SEMS = {}


class Plan:
    ENGINES = ("tensor", "vector", "scalar", "gpsimd", "sync")

    def __init__(self):
        self.streams = {e: [] for e in self.ENGINES}
        self.semv = {}
        self.last_wait = {}

    def v(self, s):
        return self.semv.get(s, 0)

    def inc(self, s):
        by = 16 if s in DMA_SEMS else 1
        self.semv[s] = self.semv.get(s, 0) + by
        return self.semv[s]

    def emit(self, eng, fn):
        self.streams[eng].append(fn)

    def wait(self, eng, s, val=None):
        val = self.semv.get(s, 0) if val is None else val
        if val <= 0 or val <= self.last_wait.get((eng, s), 0):
            return
        self.last_wait[(eng, s)] = val
        self.emit(eng, lambda E, s=s, val=val: E.wait_ge(SEMS[s], val))

    def op(self, eng, build, inc=None):
        if inc is not None:
            self.inc(inc)
        self.emit(eng, lambda E, b=build, i=inc:
                  (b(E).then_inc(SEMS[i], 16 if i in DMA_SEMS else 1) if i else b(E)))


def build_program(upto=4):
    nc = bass.Bass("TRN2", target_bir_lowering=False,
                   detect_race_conditions=False)
    P = Plan()

    x_in = nc.dram_tensor("x_in", [D, TOK], F32, kind="ExternalInput").ap()
    x_out = nc.dram_tensor("x_out", [D, TOK], F32, kind="ExternalOutput").ap()
    Wq_a = nc.dram_tensor("Wq_a", [D, D], F32, kind="ExternalInput").ap()
    Wk_a = nc.dram_tensor("Wk_a", [D, D], F32, kind="ExternalInput").ap()
    Wv_a = nc.dram_tensor("Wv_a", [D, D], F32, kind="ExternalInput").ap()
    bq_a = nc.dram_tensor("bq_a", [D], F32, kind="ExternalInput").ap()
    bk_a = nc.dram_tensor("bk_a", [D], F32, kind="ExternalInput").ap()
    bv_a = nc.dram_tensor("bv_a", [D], F32, kind="ExternalInput").ap()
    Wal = nc.dram_tensor("Wal", [D, NH], F32, kind="ExternalInput").ap()
    Wbe = nc.dram_tensor("Wbe", [D, NH], F32, kind="ExternalInput").ap()
    bal = nc.dram_tensor("bal", [NH], F32, kind="ExternalInput").ap()
    bbe = nc.dram_tensor("bbe", [NH], F32, kind="ExternalInput").ap()
    Wq_b = nc.dram_tensor("Wq_b", [D, D], F32, kind="ExternalInput").ap()
    Wk_b = nc.dram_tensor("Wk_b", [D, D], F32, kind="ExternalInput").ap()
    Wv_b = nc.dram_tensor("Wv_b", [D, D], F32, kind="ExternalInput").ap()
    Wo_b = nc.dram_tensor("Wo_b", [D, D], F32, kind="ExternalInput").ap()
    WgT = nc.dram_tensor("WgT", [D, HID], I32, kind="ExternalInput").ap()
    WuT = nc.dram_tensor("WuT", [D, HID], I32, kind="ExternalInput").ap()
    WdT = nc.dram_tensor("WdT", [HID, D], I32, kind="ExternalInput").ap()
    sg_in = nc.dram_tensor("sg_in", [HID], F32, kind="ExternalInput").ap()
    su_in = nc.dram_tensor("su_in", [HID], F32, kind="ExternalInput").ap()
    sd_in = nc.dram_tensor("sd_in", [D], F32, kind="ExternalInput").ap()
    gsel_in = nc.dram_tensor("gsel_in", [NH, 4], F32, kind="ExternalInput").ap()

    ssend_d = nc.dram_tensor("ssend_d", [128, 2049], F32)
    sgath_d = nc.dram_tensor("sgath_d", [4, 128, 2049], F32)
    kown_d = nc.dram_tensor("kown_d", [D, TOK], BF16)
    vown_d = nc.dram_tensor("vown_d", [TOK, D], BF16)
    kfull_d = nc.dram_tensor("kfull_d", [4, D, TOK], BF16)
    vfull_d = nc.dram_tensor("vfull_d", [4, TOK, D], BF16)
    mid_d = nc.dram_tensor("mid_d", [HC, 128, TOK], BF16)
    tap_names = [t for t in os.environ.get("MK_TAP", "").split(",") if t]
    tap_d = {t: nc.dram_tensor(f"tap_{i}", [128, 16384], F32, kind="ExternalOutput")
             for i, t in enumerate(tap_names)}

    def tap(name, ap):
        if name not in tap_d:
            return
        n = min(ap.shape[-1] if ap.ndim == 2 else 0, 16384)
        part = ap.shape[0]
        P.wait("gpsimd", "v"); P.wait("gpsimd", "t"); P.wait("gpsimd", "s")
        P.op("gpsimd", lambda E, ap=ap, n=n, part=part: E.dma_start(
            out=tap_d[name].ap()[0:part, 0:n], in_=ap[:, 0:n]), inc="dg")
        P.wait("gpsimd", "dg")

    es = contextlib.ExitStack()
    E = es.enter_context

    x_sb = E(nc.sbuf_tensor("x_sb", [128, DC * TOK], F32))
    ident = E(nc.sbuf_tensor("ident", [128, 128], BF16))
    ident32 = E(nc.sbuf_tensor("ident32", [128, 128], F32))
    maskneg = E(nc.sbuf_tensor("maskneg", [128, 128], F32))
    EH = E(nc.sbuf_tensor("EH", [16, NH * 128], BF16))
    ONEH = E(nc.sbuf_tensor("ONEH", [128, NH * 16], BF16))
    ones_r = E(nc.sbuf_tensor("ones_r", [1, 128], BF16))
    ones_rf = E(nc.sbuf_tensor("ones_rf", [1, 128], F32))
    ones_c = E(nc.sbuf_tensor("ones_c", [128, 1], BF16))
    neg128_r = E(nc.sbuf_tensor("neg128_r", [1, 128], BF16))
    dummy = E(nc.sbuf_tensor("dumt", [1, 64], F32))
    bq_sb = E(nc.sbuf_tensor("bq_sb", [128, DC], F32))
    bk_sb = E(nc.sbuf_tensor("bk_sb", [128, DC], F32))
    ba_sb = E(nc.sbuf_tensor("ba_sb", [16, 1], F32))
    bb_sb = E(nc.sbuf_tensor("bb_sb", [16, 1], F32))
    sg_sb = E(nc.sbuf_tensor("sg_sb", [128, HC], F32))
    su_sb = E(nc.sbuf_tensor("su_sb", [128, HC], F32))
    sd_sb = E(nc.sbuf_tensor("sd_sb", [128, DC], F32))
    gsel_sb = E(nc.sbuf_tensor("gsel_sb", [16, 4], F32))
    c1em6 = E(nc.sbuf_tensor("c1em6", [128, 1], F32))
    nc.const_aps.register(F32, 1e-6, c1em6[:])

    for s in sorted(DMA_SEMS | ENG_SEMS):
        SEMS[s] = E(nc.semaphore(s))

    def xs(dc, t0, t1):
        return x_sb[:, dc * TOK + t0: dc * TOK + t1]

    def barrier():
        for s in sorted(DMA_SEMS):
            P.wait("gpsimd", s)
        P.wait("gpsimd", "cc")
        P.op("tensor", lambda E: E.ldweights(ident[:, 0:1]), inc="t")
        P.op("vector", lambda E: E.memset(dummy[:, 0:16], 0.0), inc="v")
        P.op("scalar", lambda E: E.activation(dummy[:, 16:32], dummy[:, 0:16], AF.Copy),
             inc="s")
        P.op("gpsimd", lambda E: E.memset(dummy[:, 32:48], 0.0), inc="g")
        for eng in P.ENGINES:
            for s in ("t", "v", "s", "g"):
                P.wait(eng, s)

    def vdrain():
        P.wait("vector", "v")

    def rsqrt_refined(dst, src, tmp):
        P.wait("scalar", "v")
        P.op("scalar", lambda E: E.activation(tmp, src, AF.Sqrt), inc="s")
        P.wait("vector", "s")
        P.op("vector", lambda E: E.reciprocal(dst, tmp), inc="v")
        vdrain()
        P.op("vector", lambda E: E.tensor_mul(tmp, src, dst), inc="v")
        vdrain()
        P.op("vector", lambda E: E.tensor_mul(tmp, tmp, dst), inc="v")
        vdrain()
        P.op("vector", lambda E: E.tensor_scalar(tmp, tmp, -0.5, 1.5, ALU.mult,
                                                 ALU.add), inc="v")
        vdrain()
        P.op("vector", lambda E: E.tensor_mul(dst, dst, tmp), inc="v")
        vdrain()

    wsem_state = {}
    fm_halves = {}

    def load_wchunk(buf, sem, src_ap, nkc, col0, ncols):
        lastuse = wsem_state.get(id(buf), 0)
        if lastuse:
            P.wait("gpsimd", "t", lastuse)
        src = src_ap.rearrange("(kc p) m -> kc p m", p=128)
        P.op("gpsimd",
             lambda E, b=buf, s=src, c0=col0, ncc=ncols, k=nkc:
             E.dma_start(out=b[:, 0:k * ncc].rearrange("p (kc m) -> p kc m", kc=k),
                         in_=s[:, :, c0:c0 + ncc].transpose([1, 0, 2])),
             inc=sem)

    def wchunk_done(buf, tval):
        wsem_state[id(buf)] = tval

    def load_wchunk_rows(buf, sem, src_ap, nkc, row_kc0, dc):
        lastuse = wsem_state.get(id(buf), 0)
        if lastuse:
            P.wait("gpsimd", "t", lastuse)
        src = src_ap.rearrange("(kc p) m -> kc p m", p=128)
        P.op("gpsimd",
             lambda E, b=buf, s=src, k0=row_kc0, k=nkc, dc=dc:
             E.dma_start(out=b[:, 0:k * 128].rearrange("p (kc m) -> p kc m", kc=k),
                         in_=s[k0:k0 + k, :, dc * 128:(dc + 1) * 128]
                         .transpose([1, 0, 2])),
             inc=sem)

    def emit_ln_stats(pb, xb2, xq2, st):
        xbs = xb2 if isinstance(xb2, list) else [xb2, xb2]
        xqs = xq2 if isinstance(xq2, list) else [xq2, xq2]
        depth = 1 if xbs[0] is xbs[1] else 2
        mean_t, rst_t, scr2a, scr2b = st
        tv = []
        it = 0
        for dc in range(DC):
            for tb in range(2):
                if it >= depth:
                    P.wait("scalar", "t", tv[it - depth])
                xb, xq = xbs[it % 2], xqs[it % 2]
                a, b = tb * 512, (tb + 1) * 512
                P.op("scalar", lambda E, xb=xb, dc=dc, a=a, b=b: E.activation(
                    xb[:], xs(dc, a, b), AF.Copy))
                P.op("scalar", lambda E, xq=xq, dc=dc, a=a, b=b: E.activation(
                    xq[:], xs(dc, a, b), AF.Square), inc="s")
                P.wait("tensor", "s")
                P.op("tensor", lambda E, xb=xb, dc=dc, tb=tb: E.matmul(
                    pb[tb][0:1, :], ones_c[:], xb[:],
                    start=(dc == 0), stop=(dc == DC - 1)))
                P.op("tensor", lambda E, xq=xq, dc=dc, tb=tb: E.matmul(
                    pb[2 + tb][0:1, :], ones_c[:], xq[:],
                    start=(dc == 0), stop=(dc == DC - 1)), inc="t")
                tv.append(P.v("t"))
                it += 1
        P.wait("vector", "t")
        P.op("vector", lambda E: E.tensor_scalar_mul(mean_t[0:1, 0:512], pb[0][0:1, :],
                                                     1.0 / 2048.0))
        P.op("vector", lambda E: E.tensor_scalar_mul(mean_t[0:1, 512:1024],
                                                     pb[1][0:1, :], 1.0 / 2048.0),
             inc="v")
        vdrain()
        for hb in range(2):
            sl = hb * 512
            P.op("vector", lambda E, hb=hb, sl=sl: E.tensor_scalar_mul(
                scr2a[:], pb[2 + hb][0:1, :], 1.0 / 2048.0), inc="v")
            P.op("vector", lambda E, sl=sl: E.tensor_mul(
                rst_t[0:1, sl:sl + 512], mean_t[0:1, sl:sl + 512],
                mean_t[0:1, sl:sl + 512]), inc="v")
            vdrain()
            P.op("vector", lambda E, sl=sl: E.tensor_sub(
                scr2a[:], scr2a[:], rst_t[0:1, sl:sl + 512]), inc="v")
            vdrain()
            P.op("vector", lambda E: E.tensor_scalar_add(
                scr2a[:], scr2a[:], 1e-5), inc="v")
            vdrain()
            rsqrt_refined(rst_t[0:1, sl:sl + 512], scr2a[:], scr2b[:])

    def emit_ln_norm(pb, xn_sb, xoff, t0, t1, st):
        mean_t, rst_t, scr2a, scr2b = st
        P.wait("tensor", "v")
        ntb = (t1 - t0) // 512
        for tb in range(ntb):
            a = t0 + tb * 512
            P.op("tensor", lambda E, tb=tb, a=a: E.matmul(
                pb[tb][:, :], ones_rf[:], mean_t[0:1, a:a + 512],
                start=True, stop=True))
            P.op("tensor", lambda E, tb=tb, a=a: E.matmul(
                pb[2 + tb][:, :], ones_rf[:], rst_t[0:1, a:a + 512],
                start=True, stop=True), inc=("t" if tb == ntb - 1 else None))
        P.wait("vector", "t")
        for dc in range(DC):
            for tb in range(ntb):
                a = t0 + tb * 512
                o = xoff(dc, a)
                dst = xn_sb[:, o:o + 512]
                P.op("vector", lambda E, a=a, dst=dst, tb=tb, dc=dc: E.tensor_sub(
                    dst, xs(dc, a, a + 512), pb[tb][:, :]))
                P.op("vector", lambda E, dst=dst, tb=tb: E.tensor_mul(
                    dst, dst, pb[2 + tb][:, :]), inc="v")

    def emit_proj_fm(xn_sb, xn_tok, w_src, wbufs, wsems, ec_list, t0, t1, pb2, evac):
        P.wait("tensor", "v")
        ntb = (t1 - t0) // 512
        buf = wbufs[0]
        nb = len(pb2)
        guards = [None] * nb
        src2 = w_src.rearrange("(kc p) m -> kc p m", p=128)
        hlast = fm_halves.setdefault(id(buf), [0, 0])

        def load_half(ec, hb):
            w = max(hlast[hb], wsem_state.get(id(buf), 0))
            if w:
                P.wait("gpsimd", "t", w)
            P.op("gpsimd", lambda E, ec=ec, hb=hb: E.dma_start(
                out=buf[:, hb * 8 * 128:(hb + 1) * 8 * 128]
                .rearrange("p (kc m) -> p kc m", kc=8),
                in_=src2[hb * 8:(hb + 1) * 8, :, ec * 128:(ec + 1) * 128]
                .transpose([1, 0, 2])), inc=wsems[hb])

        load_half(ec_list[0], 0)
        for i, ec in enumerate(ec_list):
            for tb in range(ntb):
                bi = (i * ntb + tb) % nb
                bank = pb2[bi]
                if guards[bi]:
                    P.wait("tensor", guards[bi][0], guards[bi][1])
                for hb in range(2):
                    P.wait("tensor", wsems[hb])
                    if tb == 0 and hb == 0:
                        load_half(ec, 1)
                    for kcl in range(8):
                        kc = hb * 8 + kcl
                        P.op("tensor", lambda E, b=bank, kc=kc, tb=tb:
                             E.matmul(b[:, :], buf[:, kc * 128:(kc + 1) * 128],
                                      xn_sb[:, kc * xn_tok + t0 + tb * 512:
                                            kc * xn_tok + t0 + (tb + 1) * 512],
                                      start=(kc == 0), stop=(kc == DC - 1)),
                             inc=("t" if (kcl == 7) else None))
                    hlast[hb] = P.v("t")
                    if tb == ntb - 1 and hb == 1 and i + 1 < len(ec_list):
                        load_half(ec_list[i + 1], 0)
                guards[bi] = evac(ec, tb, bank)
        wchunk_done(buf, max(hlast))
        for g in guards:
            if g:
                P.wait("tensor", g[0], g[1])

    def emit_proj_tm(xn_sb, xn_tok, w_src, wvb, t0, tcs, pbl, bias_pair, evac):
        P.wait("tensor", "v")
        src = w_src.rearrange("(kc p) m -> kc p m", p=128)
        bias_use = 0
        guards = [None] * len(pbl)
        for eb in range(4):
            if bias_pair is not None:
                bias_row, bias_dram = bias_pair
                if bias_use:
                    P.wait("gpsimd", "t", bias_use)
                P.op("gpsimd", lambda E, eb=eb: E.dma_start(
                    out=bias_row[:], in_=bias_dram[eb * 512:(eb + 1) * 512]
                    .unsqueeze(0)), inc="dg")
            for kc2 in range(DC // 2):
                i = eb * (DC // 2) + kc2
                buf, sem = wvb[i % 2], ("dw2", "dw3")[i % 2]
                lastuse = wsem_state.get(id(buf), 0)
                if lastuse:
                    P.wait("gpsimd", "t", lastuse)
                P.op("gpsimd", lambda E, b=buf, s=src, kc2=kc2, eb=eb: E.dma_start(
                    out=b[:].rearrange("p (u m) -> p u m", u=2),
                    in_=s[2 * kc2:2 * kc2 + 2, :, eb * 512:(eb + 1) * 512]
                    .transpose([1, 0, 2])), inc=sem)
                P.wait("tensor", sem)
                for u in range(2):
                    kc = 2 * kc2 + u
                    if kc == 0:
                        for ti in range(len(tcs)):
                            if guards[ti]:
                                P.wait("tensor", guards[ti][0], guards[ti][1])
                    for ti, tc in enumerate(tcs):
                        stop_k = (kc == DC - 1) and (bias_pair is None)
                        P.op("tensor", lambda E, b=pbl[ti], buf=buf, kc=kc, tc=tc,
                             u=u, stop_k=stop_k:
                             E.matmul(b[:, :],
                                      xn_sb[:, kc * xn_tok + t0 + tc * 128:
                                            kc * xn_tok + t0 + (tc + 1) * 128],
                                      buf[:, u * 512:(u + 1) * 512],
                                      start=(kc == 0), stop=stop_k),
                             inc=("t" if ti == len(tcs) - 1 else None))
                wchunk_done(buf, P.v("t"))
            if bias_pair is not None:
                P.wait("tensor", "dg")
            for ti, tc in enumerate(tcs):
                if bias_pair is not None:
                    P.op("tensor", lambda E, b=pbl[ti]: E.matmul(
                        b[:, :], ones_r[:], bias_pair[0][0:1, :],
                        start=False, stop=True), inc="t")
                guards[ti] = evac(tc, eb, pbl[ti])
            bias_use = P.v("t")
        for g in guards:
            if g:
                P.wait("tensor", g[0], g[1])

    P.op("sync", lambda E: E.dma_start(
        out=x_sb[:].rearrange("p (dc t) -> p dc t", dc=DC),
        in_=x_in.rearrange("(dc p) t -> p dc t", p=128)), inc="dy")
    P.op("gpsimd", lambda E: E.dma_start(
        out=bq_sb[:], in_=bq_a.rearrange("(dc p) -> p dc", p=128), allow_slow_non_contiguous=True), inc="dg")
    P.op("gpsimd", lambda E: E.dma_start(
        out=bk_sb[:], in_=bk_a.rearrange("(dc p) -> p dc", p=128), allow_slow_non_contiguous=True), inc="dg")
    P.op("gpsimd", lambda E: E.dma_start(out=ba_sb[:], in_=bal.unsqueeze(1)), inc="dg")
    P.op("gpsimd", lambda E: E.dma_start(out=bb_sb[:], in_=bbe.unsqueeze(1)), inc="dg")
    P.op("gpsimd", lambda E: E.dma_start(
        out=sg_sb[:], in_=sg_in.rearrange("(hc p) -> p hc", p=128), allow_slow_non_contiguous=True), inc="dg")
    P.op("gpsimd", lambda E: E.dma_start(
        out=su_sb[:], in_=su_in.rearrange("(hc p) -> p hc", p=128), allow_slow_non_contiguous=True), inc="dg")
    P.op("gpsimd", lambda E: E.dma_start(
        out=sd_sb[:], in_=sd_in.rearrange("(dc p) -> p dc", p=128), allow_slow_non_contiguous=True), inc="dg")
    P.op("gpsimd", lambda E: E.dma_start(out=gsel_sb[:], in_=gsel_in), inc="dg")
    P.op("gpsimd", lambda E: E.memset(ident[:], 0.0))
    P.op("gpsimd", lambda E: E.affine_select(
        out=ident[:], in_=ident[:], compare_op=ALU.not_equal, fill=1.0,
        base=0, pattern=[[-1, 128]], channel_multiplier=1))
    P.op("gpsimd", lambda E: E.memset(ident32[:], 0.0))
    P.op("gpsimd", lambda E: E.affine_select(
        out=ident32[:], in_=ident32[:], compare_op=ALU.not_equal, fill=1.0,
        base=0, pattern=[[-1, 128]], channel_multiplier=1))
    P.op("gpsimd", lambda E: E.memset(maskneg[:], 0.0))
    P.op("gpsimd", lambda E: E.affine_select(
        out=maskneg[:], in_=maskneg[:], compare_op=ALU.is_ge, fill=-1e30,
        base=0, pattern=[[1, 128]], channel_multiplier=-1))
    P.op("gpsimd", lambda E: E.memset(EH[:], 1.0))
    for h in range(NH):
        P.op("gpsimd", lambda E, h=h: E.affine_select(
            out=EH[:, h * 128:(h + 1) * 128], in_=EH[:, h * 128:(h + 1) * 128],
            compare_op=ALU.is_ge, fill=0.0, base=-256 * h, pattern=[[1, 128]],
            channel_multiplier=256))
        P.op("gpsimd", lambda E, h=h: E.affine_select(
            out=EH[:, h * 128:(h + 1) * 128], in_=EH[:, h * 128:(h + 1) * 128],
            compare_op=ALU.is_ge, fill=0.0, base=255 + 256 * h, pattern=[[-1, 128]],
            channel_multiplier=-256))
    P.op("gpsimd", lambda E: E.memset(ONEH[:], 1.0))
    for h in range(NH):
        P.op("gpsimd", lambda E, h=h: E.affine_select(
            out=ONEH[:, h * 16:(h + 1) * 16], in_=ONEH[:, h * 16:(h + 1) * 16],
            compare_op=ALU.is_ge, fill=0.0, base=-256 * h, pattern=[[256, 16]],
            channel_multiplier=1))
        P.op("gpsimd", lambda E, h=h: E.affine_select(
            out=ONEH[:, h * 16:(h + 1) * 16], in_=ONEH[:, h * 16:(h + 1) * 16],
            compare_op=ALU.is_ge, fill=0.0, base=255 + 256 * h, pattern=[[-256, 16]],
            channel_multiplier=-1))
    P.op("gpsimd", lambda E: E.memset(ones_r[:], 1.0))
    P.op("gpsimd", lambda E: E.memset(ones_rf[:], 1.0))
    P.op("gpsimd", lambda E: E.memset(ones_c[:], 1.0))
    P.op("gpsimd", lambda E: E.memset(c1em6[:], 1e-6))
    P.op("gpsimd", lambda E: E.memset(dummy[:], 0.0))
    P.op("gpsimd", lambda E: E.memset(neg128_r[:], -128.0), inc="g")
    for e in ("tensor", "vector", "scalar"):
        P.wait(e, "g"); P.wait(e, "dg"); P.wait(e, "dy")

    def sublayer_attn_a():
        with contextlib.ExitStack() as ss:
            SE = ss.enter_context
            cla_hh = SE(nc.sbuf_tensor("cla_hh", [16, 512], F32))
            lbmc_hh = SE(nc.sbuf_tensor("lbmc_hh", [16, 512], F32))
            rn_hh = SE(nc.sbuf_tensor("rn_hh", [16, 512], F32))
            scr_hh = SE(nc.sbuf_tensor("scr_hh", [16, 512], F32))
            ends_h = SE(nc.sbuf_tensor("ends_h", [16, 16], F32))
            ecb_t = SE(nc.sbuf_tensor("ecb_t", [16, TOK], BF16))
            eef = SE(nc.sbuf_tensor("eef", [16, 8], F32))
            EH32 = SE(nc.sbuf_tensor("EH32", [16, NH * 128], F32))
            UTRI = SE(nc.sbuf_tensor("UTRI", [128, 128], F32))
            atm = SE(nc.sbuf_tensor("atm", [128, NCH * 16], F32))
            btm = SE(nc.sbuf_tensor("btm", [128, NCH * 16], F32))
            latm = SE(nc.sbuf_tensor("latm", [128, NCH * 16], F32))
            cla_tm = SE(nc.sbuf_tensor("cla_tm", [128, NCH * 16], F32))
            w_tm = SE(nc.sbuf_tensor("w_tm", [128, NCH * 16], F32))
            etd_sb = SE(nc.sbuf_tensor("etd_sb", [128, 128], F32))
            ba_row = SE(nc.sbuf_tensor("ba_row", [1, 2 * NH], F32))
            S_send = SE(nc.sbuf_tensor("S_send", [128, 2049], F32))
            snap_bf = SE(nc.sbuf_tensor("snap_bf", [128, 128], BF16))
            expd_a = SE(nc.sbuf_tensor("expd_a", [128, 128], F32))
            expd_b = SE(nc.sbuf_tensor("expd_b", [128, 128], F32))
            attnf = SE(nc.sbuf_tensor("attnf", [128, 128], BF16))
            kw_sb = SE(nc.sbuf_tensor("kw_sb", [128, 128], BF16))
            qcd = SE(nc.sbuf_tensor("qcd", [128, 128], BF16))
            q_sb = SE(nc.sbuf_tensor("q_sb", [128, NH * TOK], BF16))
            pb = [SE(nc.psum_tensor(f"paa{i}", [128, 512], F32)) for i in range(6)]
            ptr = SE(nc.psum_tensor("ptra", [128, 512], BF16))

            P.op("gpsimd", lambda E: E.memset(EH32[:], 1.0))
            for h in range(NH):
                P.op("gpsimd", lambda E, h=h: E.affine_select(
                    out=EH32[:, h * 128:(h + 1) * 128],
                    in_=EH32[:, h * 128:(h + 1) * 128],
                    compare_op=ALU.is_ge, fill=0.0, base=-256 * h,
                    pattern=[[1, 128]], channel_multiplier=256))
                P.op("gpsimd", lambda E, h=h: E.affine_select(
                    out=EH32[:, h * 128:(h + 1) * 128],
                    in_=EH32[:, h * 128:(h + 1) * 128],
                    compare_op=ALU.is_ge, fill=0.0, base=255 + 256 * h,
                    pattern=[[-1, 128]], channel_multiplier=-256))
            P.op("gpsimd", lambda E: E.memset(UTRI[:], 1.0))
            P.op("gpsimd", lambda E: E.affine_select(
                out=UTRI[:], in_=UTRI[:], compare_op=ALU.is_ge, fill=0.0,
                base=0, pattern=[[1, 128]], channel_multiplier=-1))
            P.op("gpsimd", lambda E: E.memset(S_send[:], 0.0), inc="g")
            P.op("gpsimd", lambda E: E.dma_start(
                out=ba_row[:, 0:NH], in_=bal.unsqueeze(0)), inc="dg")
            P.op("gpsimd", lambda E: E.dma_start(
                out=ba_row[:, NH:], in_=bbe.unsqueeze(0)), inc="dg")
            P.wait("vector", "g")
            P.wait("tensor", "g")
            P.wait("tensor", "dg")

            st_a = tuple(SE(nc.sbuf_tensor(f"sta{i}", [1, sz], F32))
                         for i, sz in enumerate((TOK, TOK, 512, 512)))
            inner = contextlib.ExitStack()
            IE = inner.enter_context
            bv_row = IE(nc.sbuf_tensor("bv_row", [1, 512], BF16))
            xb_s = IE(nc.sbuf_tensor("a_xb", [128, 512], BF16))
            xq_s = IE(nc.sbuf_tensor("a_xq", [128, 512], BF16))
            ksq = xq_s

            emit_ln_stats(pb[0:4], xb_s, xq_s, st_a)

            wb0 = IE(nc.sbuf_tensor("wba0", [128, DC * 128], BF16))
            wb = [wb0, wb0]
            wvb = [IE(nc.sbuf_tensor(f"wva{i}", [128, 1024], BF16)) for i in range(2)]
            wab = IE(nc.sbuf_tensor("wab", [128, 2 * DC * 16], BF16))
            wsems = ["dw0", "dw1"]
            P.op("gpsimd", lambda E: E.dma_start(
                out=wab[:, 0:DC * 16].rearrange("p (dc m) -> p dc m", dc=DC),
                in_=Wal.rearrange("(dc p) m -> p dc m", p=128)), inc="dg")
            P.op("gpsimd", lambda E: E.dma_start(
                out=wab[:, DC * 16:].rearrange("p (dc m) -> p dc m", dc=DC),
                in_=Wbe.rearrange("(dc p) m -> p dc m", p=128)), inc="dg")

            xn = IE(nc.sbuf_tensor("xn", [128, DC * 512], BF16))
            k_sb = IE(nc.sbuf_tensor("k_sb", [128, NH * 512], BF16))
            v_sb = IE(nc.sbuf_tensor("v_sb", [128, 4 * D], BF16))

            for half in range(2):
                t0, t1 = half * 512, (half + 1) * 512
                emit_ln_norm(pb[0:4], xn, lambda dc, a, t0=t0: dc * 512 + (a - t0),
                             t0, t1, st_a)

                if half == 0:
                    tap("xn0", xn[:])
                def evac_q(ec, tb, bank, t0=t0):
                    P.wait("scalar", "t", P.v("t"))
                    P.op("scalar", lambda E, ec=ec, b=bank, t0=t0: E.activation(
                        q_sb[:, ec * TOK + t0: ec * TOK + t0 + 512],
                        b[:, :], AF.Identity, bias=bq_sb[:, ec:ec + 1]), inc="s")
                    return ("s", P.v("s"))
                emit_proj_fm(xn, 512, Wq_a, wb, wsems, list(range(DC)), 0, 512,
                             pb[0:2], evac_q)

                if half == 0:
                    tap("q0", q_sb[:])
                P.wait("tensor", "dg")
                P.wait("tensor", "v")
                for q_i in range(2):
                    w0 = q_i * DC * 16
                    for nl in range(4):
                        n = half * 4 + nl
                        bank = pb[2 + (nl % 2)]
                        for kc in range(DC):
                            P.op("tensor", lambda E, b=bank, w0=w0, kc=kc, nl=nl:
                                 E.matmul(b[:, 0:16],
                                          xn[:, kc * 512 + nl * 128:
                                             kc * 512 + (nl + 1) * 128],
                                          wab[:, w0 + kc * 16: w0 + (kc + 1) * 16],
                                          start=(kc == 0), stop=False))
                        P.op("tensor", lambda E, b=bank, q_i=q_i: E.matmul(
                            b[:, 0:16], ones_rf[:],
                            ba_row[0:1, q_i * NH:(q_i + 1) * NH],
                            start=False, stop=True), inc="t")
                        P.wait("scalar", "t")
                        if q_i == 0:
                            P.op("scalar", lambda E, b=bank, n=n: E.activation(
                                atm[:, n * 16:(n + 1) * 16], b[:, 0:16], AF.Sigmoid),
                                inc="s")
                        else:
                            P.op("scalar", lambda E, b=bank, n=n: E.activation(
                                btm[:, n * 16:(n + 1) * 16], b[:, 0:16], AF.Identity),
                                inc="s")
                        P.wait("tensor", "s")
                sl_h = slice(half * 4 * 16, (half + 1) * 4 * 16)
                P.op("scalar", lambda E, sl=sl_h: E.activation(
                    latm[:, sl], btm[:, sl], AF.Abs))
                P.op("scalar", lambda E, sl=sl_h: E.activation(
                    latm[:, sl], latm[:, sl], AF.Exp, scale=-1.0))
                P.op("scalar", lambda E, sl=sl_h: E.activation(
                    latm[:, sl], latm[:, sl], AF.Ln, bias=1.0), inc="s")
                P.wait("vector", "s")
                P.op("vector", lambda E, sl=sl_h: E.tensor_scalar_max(
                    btm[:, sl], btm[:, sl], 0.0), inc="v")
                vdrain()
                P.op("vector", lambda E, sl=sl_h: E.tensor_add(
                    btm[:, sl], btm[:, sl], latm[:, sl]), inc="v")
                vdrain()
                P.wait("scalar", "v")
                P.op("scalar", lambda E, sl=sl_h: E.activation(
                    latm[:, sl], atm[:, sl], AF.Ln, bias=1e-6), inc="s")
                P.wait("tensor", "s")
                for nl in range(4):
                    n = half * 4 + nl
                    P.op("tensor", lambda E, n=n, nl=nl: E.matmul(
                        pb[4][:, nl * 16:(nl + 1) * 16], UTRI[:],
                        latm[:, n * 16:(n + 1) * 16], start=True, stop=True),
                        inc=("t" if nl == 3 else None))
                P.wait("vector", "t")
                for nl in range(4):
                    n = half * 4 + nl
                    P.op("vector", lambda E, n=n, nl=nl: E.tensor_copy(
                        cla_tm[:, n * 16:(n + 1) * 16],
                        pb[4][:, nl * 16:(nl + 1) * 16]),
                        inc=("v" if nl == 3 else None))
                P.wait("tensor", "v")
                for nl in range(4):
                    n = half * 4 + nl
                    P.op("tensor", lambda E, n=n, nl=nl: E.transpose(
                        pb[5][0:16, nl * 128:(nl + 1) * 128],
                        cla_tm[:, n * 16:(n + 1) * 16], ident32[:]),
                        inc=("t" if nl == 3 else None))
                P.wait("vector", "t")
                for nl in range(4):
                    P.op("vector", lambda E, nl=nl: E.tensor_copy(
                        cla_hh[:, nl * 128:(nl + 1) * 128],
                        pb[5][0:16, nl * 128:(nl + 1) * 128]),
                        inc=("v" if nl == 3 else None))
                vdrain()
                cends = cla_hh[:].rearrange("h (n c) -> h n c", c=128)[:, :, 127]
                P.op("vector", lambda E, half=half, cends=cends: E.tensor_copy(
                    ends_h[:, half * 4:(half + 1) * 4], cends), inc="v")
                vdrain()
                if half == 0:
                    tap("clh", cla_hh[:])
                    tap("ctm", cla_tm[:])
                P.wait("scalar", "v")
                P.op("scalar", lambda E, t0=t0, t1=t1: E.activation(
                    ecb_t[:, t0:t1], cla_hh[:], AF.Exp), inc="s")
                eendsh = ecb_t[:, 0:TOK].rearrange(
                    "h (n c) -> h n c", c=128)[:, half * 4:(half + 1) * 4, 127]
                P.wait("vector", "s")
                P.op("vector", lambda E, half=half, eendsh=eendsh: E.tensor_copy(
                    eef[:, half * 4:(half + 1) * 4], eendsh), inc="v")
                vdrain()
                P.wait("tensor", "s")
                P.wait("tensor", "v")
                for h in range(NH):
                    P.op("tensor", lambda E, h=h, half=half: E.matmul(
                        pb[4][:, h * 8 + half * 4: h * 8 + half * 4 + 4],
                        EH32[:, h * 128:(h + 1) * 128],
                        eef[:, half * 4:(half + 1) * 4],
                        start=(h == 0), stop=(h == NH - 1)),
                        inc=("t" if h == NH - 1 else None))
                P.wait("vector", "t")
                for h in range(NH):
                    P.op("vector", lambda E, h=h, half=half: E.tensor_copy(
                        etd_sb[:, h * 8 + half * 4: h * 8 + half * 4 + 4],
                        pb[4][:, h * 8 + half * 4: h * 8 + half * 4 + 4]),
                        inc=("v" if h == NH - 1 else None))
                vdrain()

                def evac_k(ec, tb, bank):
                    P.wait("scalar", "t", P.v("t"))
                    P.op("scalar", lambda E, ec=ec, b=bank: E.activation(
                        k_sb[:, ec * 512:(ec + 1) * 512], b[:, :], AF.Identity,
                        bias=bk_sb[:, ec:ec + 1]), inc="s")
                    return ("s", P.v("s"))
                emit_proj_fm(xn, 512, Wk_a, wb, wsems, list(range(DC)), 0, 512,
                             pb[0:2], evac_k)

                def evac_v(tc, eb, bank):
                    P.wait("vector", "t", P.v("t"))
                    P.op("vector", lambda E, b=bank, tc=tc, eb=eb: E.tensor_copy(
                        v_sb[:, tc * D + eb * 512: tc * D + (eb + 1) * 512], b[:, :]),
                        inc="v")
                    return ("v", P.v("v"))
                emit_proj_tm(xn, 512, Wv_a, wvb, 0, list(range(4)), pb[0:4],
                             (bv_row, bv_a), evac_v)

                prev_t = 0
                for h in range(NH):
                    if prev_t:
                        P.wait("scalar", "t", prev_t)
                    P.wait("scalar", "v")
                    P.op("scalar", lambda E, h=h: E.activation(
                        ksq[:], k_sb[:, h * 512:(h + 1) * 512], AF.Square), inc="s")
                    P.wait("tensor", "s")
                    P.op("tensor", lambda E, h=h: E.matmul(
                        pb[4][0:16, :], ONEH[:, h * 16:(h + 1) * 16], ksq[:],
                        start=(h == 0), stop=(h == NH - 1)), inc="t")
                    prev_t = P.v("t")
                P.wait("vector", "t")
                P.op("vector", lambda E: E.tensor_scalar_max(
                    scr_hh[:], pb[4][0:16, :], 1e-24), inc="v")
                rsqrt_refined(rn_hh[:], scr_hh[:], lbmc_hh[:])
                P.wait("tensor", "v")
                for nl in range(4):
                    n = half * 4 + nl
                    P.op("tensor", lambda E, n=n, nl=nl: E.transpose(
                        pb[5][0:16, nl * 128:(nl + 1) * 128],
                        btm[:, n * 16:(n + 1) * 16], ident32[:]),
                        inc=("t" if nl == 3 else None))
                P.wait("vector", "t")
                for nl in range(4):
                    P.op("vector", lambda E, nl=nl: E.tensor_copy(
                        scr_hh[:, nl * 128:(nl + 1) * 128],
                        pb[5][0:16, nl * 128:(nl + 1) * 128]),
                        inc=("v" if nl == 3 else None))
                vdrain()
                P.op("vector", lambda E: E.tensor_mul(scr_hh[:], scr_hh[:], rn_hh[:]),
                     inc="v")
                vdrain()
                P.wait("scalar", "v")
                P.op("scalar", lambda E: E.activation(lbmc_hh[:], scr_hh[:], AF.Ln),
                     inc="s")
                P.wait("vector", "s")
                P.op("vector", lambda E: E.tensor_sub(lbmc_hh[:], lbmc_hh[:],
                                                      cla_hh[:]), inc="v")
                vdrain()
                for nl in range(4):
                    ce = cla_hh[:, nl * 128 + 127: nl * 128 + 128]
                    P.op("vector", lambda E, nl=nl, ce=ce: E.tensor_add(
                        scr_hh[:, nl * 128:(nl + 1) * 128],
                        lbmc_hh[:, nl * 128:(nl + 1) * 128],
                        ce.broadcast_to((16, 128))),
                        inc=("v" if nl == 3 else None))
                vdrain()
                P.wait("scalar", "v")
                P.op("scalar", lambda E: E.activation(scr_hh[:], scr_hh[:], AF.Exp),
                     inc="s")
                P.wait("tensor", "s")
                for nl in range(4):
                    n = half * 4 + nl
                    P.op("tensor", lambda E, n=n, nl=nl: E.transpose(
                        pb[5][:, nl * 16:(nl + 1) * 16],
                        scr_hh[:, nl * 128:(nl + 1) * 128], ident32[0:16, 0:16]),
                        inc=("t" if nl == 3 else None))
                P.wait("vector", "t")
                for nl in range(4):
                    n = half * 4 + nl
                    P.op("vector", lambda E, n=n, nl=nl: E.tensor_copy(
                        w_tm[:, n * 16:(n + 1) * 16], pb[5][:, nl * 16:(nl + 1) * 16]),
                        inc=("v" if nl == 3 else None))
                vdrain()

                if half == 0:
                    tap("v0", v_sb[:])
                for nl in range(4):
                    n = half * 4 + nl
                    if half == 0 and nl == 1:
                        tap("s0", S_send[:, 0:2048])
                    for h in range(NH):
                        qt = q_sb[:, h * TOK + n * 128: h * TOK + (n + 1) * 128]
                        kt = k_sb[:, h * 512 + nl * 128: h * 512 + (nl + 1) * 128]
                        vt = v_sb[:, nl * D + h * 128: nl * D + (h + 1) * 128]
                        clat = cla_hh[:, nl * 128:(nl + 1) * 128]
                        ecbt = ecb_t[:, n * 128:(n + 1) * 128]
                        lbt = lbmc_hh[:, nl * 128:(nl + 1) * 128]
                        ehs = EH[:, h * 128:(h + 1) * 128]
                        ehs32 = EH32[:, h * 128:(h + 1) * 128]
                        P.op("vector", lambda E, h=h: E.tensor_copy(
                            snap_bf[:], S_send[:, h * 128:(h + 1) * 128]), inc="v")
                        P.wait("tensor", "v")
                        P.op("tensor", lambda E, kt=kt: E.transpose(
                            ptr[:, 0:128], kt, ident[:]))
                        P.op("tensor", lambda E, kt=kt, qt=qt: E.matmul(
                            pb[0][:, 0:128], kt, qt, start=True, stop=True))
                        P.op("tensor", lambda E, lbt=lbt, ehs32=ehs32: E.matmul(
                            pb[1][:, 0:128], lbt, ehs32, start=True, stop=False))
                        P.op("tensor", lambda E, clat=clat, ehs32=ehs32: E.matmul(
                            pb[1][:, 0:128], ehs32, clat, start=False, stop=True))
                        P.op("tensor", lambda E, ecbt=ecbt, ehs=ehs: E.matmul(
                            pb[4][:, 0:128], ehs, ecbt, start=True, stop=True),
                            inc="t")
                        P.wait("vector", "t")
                        P.op("vector", lambda E: E.tensor_add(
                            expd_a[:], pb[1][:, 0:128], maskneg[:]), inc="v")
                        P.wait("scalar", "v")
                        P.op("scalar", lambda E: E.activation(
                            expd_b[:], expd_a[:], AF.Exp), inc="s")
                        P.wait("vector", "s")
                        P.op("vector", lambda E: E.tensor_mul(
                            attnf[:], pb[0][:, 0:128], expd_b[:]))
                        P.op("vector", lambda E, n=n, h=h: E.tensor_scalar_mul(
                            kw_sb[:], ptr[:, 0:128],
                            w_tm[:, n * 16 + h: n * 16 + h + 1]))
                        P.op("vector", lambda E, qt=qt: E.tensor_mul(
                            qcd[:], qt, pb[4][:, 0:128]), inc="v")
                        P.wait("tensor", "v")
                        P.op("tensor", lambda E, vt=vt: E.matmul(
                            pb[2][:, 0:128], kw_sb[:], vt, start=True, stop=True))
                        P.op("tensor", lambda E, vt=vt: E.matmul(
                            pb[3][:, 0:128], vt, attnf[:], start=True, stop=False))
                        P.op("tensor", lambda E: E.matmul(
                            pb[3][:, 0:128], snap_bf[:], qcd[:],
                            start=False, stop=True), inc="t")
                        P.wait("vector", "t")
                        P.op("vector", lambda E, h=h, n=n: E.tensor_add(
                            xs(h, n * 128, (n + 1) * 128),
                            xs(h, n * 128, (n + 1) * 128), pb[3][:, 0:128]))
                        P.op("vector", lambda E, h=h, n=n: E.scalar_tensor_tensor(
                            S_send[:, h * 128:(h + 1) * 128],
                            S_send[:, h * 128:(h + 1) * 128],
                            etd_sb[:, h * 8 + n: h * 8 + n + 1],
                            pb[2][:, 0:128], ALU.mult, ALU.add), inc="v")

            P.wait("gpsimd", "dw0"); P.wait("gpsimd", "dw2"); P.wait("gpsimd", "dw3")
            inner.close()

            gcb_t = SE(nc.sbuf_tensor("gcb_t", [16, TOK], F32))
            tap("end", ends_h[:])
            P.op("vector", lambda E: E.tensor_reduce(
                S_send[0:16, 2048:2049], ends_h[:, 0:8], AX.X, ALU.add), inc="v")
            vdrain()
            P.wait("gpsimd", "v")
            P.op("gpsimd", lambda E: E.memset(ends_h[:, 8:16], 0.0), inc="g")
            P.wait("vector", "g")
            P.op("vector", lambda E: E.tensor_tensor_scan(
                latm[0:16, 0:8], ends_h[:, 0:8], ends_h[:, 8:16], 0.0,
                ALU.add, ALU.add), inc="v")
            vdrain()
            P.op("vector", lambda E: E.tensor_copy(ends_h[:, 9:16], latm[0:16, 0:7]),
                 inc="v")
            vdrain()
            for half in range(2):
                P.wait("tensor", "v")
                for nl in range(4):
                    n = half * 4 + nl
                    P.op("tensor", lambda E, n=n, nl=nl: E.transpose(
                        pb[5][0:16, nl * 128:(nl + 1) * 128],
                        cla_tm[:, n * 16:(n + 1) * 16], ident32[:]),
                        inc=("t" if nl == 3 else None))
                P.wait("vector", "t")
                for nl in range(4):
                    n = half * 4 + nl
                    P.op("vector", lambda E, n=n, nl=nl: E.tensor_add(
                        cla_hh[:, nl * 128:(nl + 1) * 128],
                        pb[5][0:16, nl * 128:(nl + 1) * 128],
                        ends_h[:, 8 + n: 9 + n].broadcast_to((16, 128))),
                        inc=("v" if nl == 3 else None))
                P.wait("scalar", "v")
                P.op("scalar", lambda E, half=half: E.activation(
                    gcb_t[:, half * 512:(half + 1) * 512], cla_hh[:], AF.Exp),
                    inc="s")
                P.wait("tensor", "s")

            P.wait("gpsimd", "v")
            P.op("gpsimd", lambda E: E.dma_start(out=ssend_d.ap()[:, :], in_=S_send[:]),
                 inc="dg")
            P.wait("gpsimd", "dg")
            P.op("gpsimd", lambda E: E.collective_compute(
                "AllGather", ALU.bypass, replica_groups=[[0, 1, 2, 3], [4, 5, 6, 7]],
                ins=[ssend_d.ap().opt()], outs=[sgath_d.ap().opt()]), inc="cc")
            P.wait("gpsimd", "cc")
            P.wait("sync", "cc")

            S_in = SE(nc.sbuf_tensor("S_in", [128, 2048], F32))
            S_in_bf = SE(nc.sbuf_tensor("S_in_bf", [128, 2048], BF16))
            gla = SE(nc.sbuf_tensor("gla", [16, 16], F32))
            fc_sb = SE(nc.sbuf_tensor("fc_sb", [128, 64], F32))
            gbuf = [SE(nc.sbuf_tensor(f"gbuf{i}", [128, 2049], F32)) for i in range(2)]
            for p_ in range(4):
                P.op("sync", lambda E, p_=p_: E.dma_start(
                    out=gla[:, p_:p_ + 1], in_=sgath_d.ap()[p_, 0:16, 2048:2049],
                    allow_slow_non_contiguous=True), inc="dy")
            P.wait("vector", "dy")
            P.op("vector", lambda E: E.tensor_mul(gla[:, 4:8], gla[:, 0:4], gsel_sb[:]),
                 inc="v")
            vdrain()
            P.wait("gpsimd", "v")
            P.op("gpsimd", lambda E: E.memset(gla[:, 11:12], 0.0), inc="g")
            P.wait("vector", "g")
            P.op("vector", lambda E: E.tensor_copy(gla[:, 10:11], gla[:, 7:8]),
                 inc="v")
            vdrain()
            P.op("vector", lambda E: E.tensor_add(gla[:, 9:10], gla[:, 10:11],
                                                  gla[:, 6:7]), inc="v")
            vdrain()
            P.op("vector", lambda E: E.tensor_add(gla[:, 8:9], gla[:, 9:10],
                                                  gla[:, 5:6]), inc="v")
            vdrain()
            P.wait("scalar", "v")
            P.op("scalar", lambda E: E.activation(gla[:, 12:16], gla[:, 8:12],
                                                  AF.Exp), inc="s")
            P.wait("vector", "s")
            P.op("vector", lambda E: E.tensor_mul(gla[:, 12:16], gla[:, 12:16],
                                                  gsel_sb[:]), inc="v")
            vdrain()
            P.wait("tensor", "v")
            for h in range(NH):
                P.op("tensor", lambda E, h=h: E.matmul(
                    pb[4][:, h * 4:(h + 1) * 4], EH32[:, h * 128:(h + 1) * 128],
                    gla[:, 12:16], start=(h == 0), stop=(h == NH - 1)),
                    inc=("t" if h == NH - 1 else None))
            P.wait("vector", "t")
            P.op("vector", lambda E: E.tensor_copy(fc_sb[:], pb[4][:, 0:64]), inc="v")
            P.wait("gpsimd", "v")
            P.op("gpsimd", lambda E: E.memset(S_in[:], 0.0), inc="g")
            P.wait("vector", "g")
            gb_wait = [0, 0]
            for p_ in range(4):
                b = gbuf[p_ % 2]
                if gb_wait[p_ % 2]:
                    P.wait("sync", "v", gb_wait[p_ % 2])
                P.op("sync", lambda E, p_=p_, b=b: E.dma_start(
                    out=b[:], in_=sgath_d.ap()[p_]), inc="dy")
                P.wait("vector", "dy")
                for h in range(NH):
                    P.op("vector", lambda E, b=b, h=h, p_=p_: E.scalar_tensor_tensor(
                        S_in[:, h * 128:(h + 1) * 128], b[:, h * 128:(h + 1) * 128],
                        fc_sb[:, h * 4 + p_: h * 4 + p_ + 1],
                        S_in[:, h * 128:(h + 1) * 128], ALU.mult, ALU.add),
                        inc=("v" if h == NH - 1 else None))
                vdrain()
                gb_wait[p_ % 2] = P.v("v")
            vdrain()
            P.op("vector", lambda E: E.tensor_copy(S_in_bf[:], S_in[:]), inc="v")
            tap("sin", S_in[:])
            tap("gla", gla[:])
            tap("gsl", gsel_sb[:])
            tap("fc", fc_sb[:])
            tap("gcb", gcb_t[:])
            tap("ehh", ends_h[:])

            qcd4 = SE(nc.sbuf_tensor("qcd4", [128, 512], BF16))
            for n in range(NCH):
                gcbt = gcb_t[:, n * 128:(n + 1) * 128]
                for hg in range(NH // 4):
                    P.wait("tensor", "v")
                    for j in range(4):
                        h = hg * 4 + j
                        P.op("tensor", lambda E, gcbt=gcbt, h=h, j=j: E.matmul(
                            pb[0][:, j * 128:(j + 1) * 128],
                            EH32[:, h * 128:(h + 1) * 128], gcbt,
                            start=True, stop=True),
                            inc=("t" if j == 3 else None))
                    P.wait("vector", "t")
                    for j in range(4):
                        h = hg * 4 + j
                        P.op("vector", lambda E, h=h, j=j, n=n: E.tensor_mul(
                            qcd4[:, j * 128:(j + 1) * 128],
                            q_sb[:, h * TOK + n * 128: h * TOK + (n + 1) * 128],
                            pb[0][:, j * 128:(j + 1) * 128]),
                            inc=("v" if j == 3 else None))
                    P.wait("tensor", "v")
                    for j in range(4):
                        h = hg * 4 + j
                        P.op("tensor", lambda E, h=h, j=j: E.matmul(
                            pb[1][:, j * 128:(j + 1) * 128],
                            S_in_bf[:, h * 128:(h + 1) * 128],
                            qcd4[:, j * 128:(j + 1) * 128],
                            start=True, stop=True),
                            inc=("t" if j == 3 else None))
                    P.wait("vector", "t")
                    for j in range(4):
                        h = hg * 4 + j
                        P.op("vector", lambda E, h=h, j=j, n=n: E.tensor_add(
                            xs(h, n * 128, (n + 1) * 128),
                            xs(h, n * 128, (n + 1) * 128),
                            pb[1][:, j * 128:(j + 1) * 128]),
                            inc=("v" if j == 3 else None))
        barrier()

    def sublayer_ffn(tag):
        with contextlib.ExitStack() as ss:
            SE = ss.enter_context
            csum = SE(nc.sbuf_tensor(f"csum{tag}", [1, TOK], BF16))
            msum = SE(nc.sbuf_tensor(f"msum{tag}", [1, TOK], BF16))
            t1sb = [SE(nc.sbuf_tensor(f"t1sb{tag}{i}", [128, 512], F32))
                    for i in range(2)]
            midst = [SE(nc.sbuf_tensor(f"mst{tag}{i}", [128, TOK], BF16))
                     for i in range(2)]
            pb = [SE(nc.psum_tensor(f"pf{tag}{i}", [128, 512], F32)) for i in range(6)]
            with contextlib.ExitStack() as xs_scope:
                XL = xs_scope.enter_context
                xn = XL(nc.sbuf_tensor(f"xnf{tag}", [128, DC * TOK], BF16))
                xb_s = [XL(nc.sbuf_tensor(f"fxb{tag}{i}", [128, 512], BF16))
                        for i in range(2)]
                xq_s = [XL(nc.sbuf_tensor(f"fxq{tag}{i}", [128, 512], BF16))
                        for i in range(2)]
                st_f = tuple(XL(nc.sbuf_tensor(f"stf{tag}{i}", [1, sz], F32))
                             for i, sz in enumerate((TOK, TOK, 512, 512)))
                emit_ln_stats(pb[0:4], xb_s, xq_s, st_f)
                emit_ln_norm(pb[0:4], xn, lambda dc, a: dc * TOK + a, 0, TOK, st_f)
                P.wait("tensor", "v")
                for tb in range(2):
                    for kc in range(DC):
                        P.op("tensor", lambda E, tb=tb, kc=kc: E.matmul(
                            pb[4 + tb][0:1, :], ones_c[:],
                            xn[:, kc * TOK + tb * 512: kc * TOK + (tb + 1) * 512],
                            start=(kc == 0), stop=(kc == DC - 1)),
                            inc=("t" if kc == DC - 1 else None))
                P.wait("vector", "t")
                P.op("vector", lambda E: E.tensor_copy(csum[:, 0:512], pb[4][0:1, :]))
                P.op("vector", lambda E: E.tensor_copy(csum[:, 512:1024],
                                                       pb[5][0:1, :]), inc="v")
                with contextlib.ExitStack() as ws:
                    WL = ws.enter_context
                    wg = [WL(nc.sbuf_tensor(f"wg{tag}{i}", [128, DC * 128], BF16))
                          for i in range(2)]
                    wu = [WL(nc.sbuf_tensor(f"wu{tag}{i}", [128, DC * 128], BF16))
                          for i in range(2)]
                    P.wait("tensor", "v")
                    mid_dma = [0, 0]
                    bank_guard = [0, 0]
                    for hc in range(HC):
                        bg, bu = wg[hc % 2], wu[hc % 2]
                        sg_, su_ = ("dw0", "dw1")[hc % 2], ("dw2", "dw3")[hc % 2]
                        if hc == 0:
                            load_wchunk(bg, sg_, WgT, DC, 0, 128)
                            load_wchunk(bu, su_, WuT, DC, 0, 128)
                        P.wait("tensor", sg_)
                        P.wait("tensor", su_)
                        if hc + 1 < HC:
                            load_wchunk(wg[(hc + 1) % 2], ("dw0", "dw1")[(hc + 1) % 2],
                                        WgT, DC, (hc + 1) * 128, 128)
                            load_wchunk(wu[(hc + 1) % 2], ("dw2", "dw3")[(hc + 1) % 2],
                                        WuT, DC, (hc + 1) * 128, 128)
                        mst = midst[hc % 2]
                        for tb in range(2):
                            gbank, ubank = pb[tb * 2], pb[tb * 2 + 1]
                            if bank_guard[tb]:
                                P.wait("tensor", "v", bank_guard[tb])
                            for kc in range(DC):
                                P.op("tensor", lambda E, bg=bg, kc=kc, tb=tb,
                                     b=gbank: E.matmul(
                                         b[:, :], bg[:, kc * 128:(kc + 1) * 128],
                                         xn[:, kc * TOK + tb * 512:
                                            kc * TOK + (tb + 1) * 512],
                                         start=(kc == 0), stop=False))
                            P.op("tensor", lambda E, tb=tb, b=gbank: E.matmul(
                                b[:, :], neg128_r[:],
                                csum[0:1, tb * 512:(tb + 1) * 512],
                                start=False, stop=True), inc="t")
                            for kc in range(DC):
                                P.op("tensor", lambda E, bu=bu, kc=kc, tb=tb,
                                     b=ubank: E.matmul(
                                         b[:, :], bu[:, kc * 128:(kc + 1) * 128],
                                         xn[:, kc * TOK + tb * 512:
                                            kc * TOK + (tb + 1) * 512],
                                         start=(kc == 0), stop=False))
                            P.op("tensor", lambda E, tb=tb, b=ubank: E.matmul(
                                b[:, :], neg128_r[:],
                                csum[0:1, tb * 512:(tb + 1) * 512],
                                start=False, stop=True), inc="t")
                            tu = P.v("t")
                            P.wait("scalar", "t", tu)
                            P.op("scalar", lambda E, hc=hc, tb=tb, b=gbank: E.activation(
                                t1sb[tb][:], b[:, :], AF.Silu,
                                scale=sg_sb[:, hc:hc + 1]), inc="s")
                            P.wait("vector", "s", P.v("s"))
                            if mid_dma[hc % 2]:
                                P.wait("vector", ("vy0", "vy1")[hc % 2],
                                       mid_dma[hc % 2])
                            P.op("vector", lambda E, mst=mst, hc=hc, tb=tb, b=ubank:
                                 E.scalar_tensor_tensor(
                                     mst[:, tb * 512:(tb + 1) * 512],
                                     b[:, :], su_sb[:, hc:hc + 1], t1sb[tb][:],
                                     ALU.mult, ALU.mult), inc="v")
                            bank_guard[tb] = P.v("v")
                        P.wait("sync", "v", P.v("v"))
                        P.op("sync", lambda E, mst=mst, hc=hc: E.dma_start(
                            out=mid_d.ap()[hc], in_=mst[:]),
                            inc=("vy0", "vy1")[hc % 2])
                        mid_dma[hc % 2] = P.v(("vy0", "vy1")[hc % 2])
                        wchunk_done(bg, P.v("t"))
                        wchunk_done(bu, P.v("t"))
            NG = 2
            GH = HC // NG
            with contextlib.ExitStack() as ws:
                WL = ws.enter_context
                P.wait("gpsimd", "t")
                P.wait("sync", "t")
                mgrp = WL(nc.sbuf_tensor(f"mg{tag}", [128, GH * TOK], BF16))
                wd = [WL(nc.sbuf_tensor(f"wd{tag}{i}", [128, GH * 128], BF16))
                      for i in range(2)]
                down_guard = [0, 0]
                P.wait("tensor", "v")
                P.wait("tensor", "s")
                for hg in range(NG):
                    P.wait("sync", ("vy0", "vy1")[0])
                    P.wait("sync", ("vy0", "vy1")[1])
                    if hg > 0:
                        P.wait("sync", "t")
                    P.op("sync", lambda E, hg=hg: E.dma_start(
                        out=mgrp[:].rearrange("p (h t) -> p h t", h=GH),
                        in_=mid_d.ap()[hg * GH:(hg + 1) * GH].transpose([1, 0, 2])),
                        inc="ky0")
                    P.wait("tensor", "ky0")
                    for tb in range(2):
                        for hc in range(GH):
                            P.op("tensor", lambda E, tb=tb, hc=hc, hg=hg: E.matmul(
                                pb[4 + tb][0:1, :], ones_c[:],
                                mgrp[:, hc * TOK + tb * 512: hc * TOK + (tb + 1) * 512],
                                start=(hg == 0 and hc == 0),
                                stop=(hg == NG - 1 and hc == GH - 1)),
                                inc=("t" if hc == GH - 1 else None))
                    if hg == NG - 1:
                        P.wait("vector", "t")
                        P.op("vector", lambda E: E.tensor_copy(msum[:, 0:512],
                                                               pb[4][0:1, :]))
                        P.op("vector", lambda E: E.tensor_copy(msum[:, 512:1024],
                                                               pb[5][0:1, :]), inc="v")
                        P.wait("tensor", "v")
                    for dc in range(DC):
                        bd = wd[dc % 2]
                        sd_ = ("dw0", "dw1")[dc % 2]
                        if dc == 0:
                            load_wchunk_rows(bd, sd_, WdT, GH, hg * GH, dc)
                        P.wait("tensor", sd_)
                        if dc + 1 < DC:
                            load_wchunk_rows(wd[(dc + 1) % 2],
                                             ("dw0", "dw1")[(dc + 1) % 2],
                                             WdT, GH, hg * GH, dc + 1)
                        for tb in range(2):
                            bank = pb[tb]
                            if down_guard[tb]:
                                P.wait("tensor", "v", down_guard[tb])
                            last = NG - 1
                            for hc in range(GH):
                                fin = (hg < last and hc == GH - 1)
                                P.op("tensor",
                                     lambda E, b=bank, bd=bd, hc=hc, tb=tb, fin=fin:
                                     E.matmul(b[:, :], bd[:, hc * 128:(hc + 1) * 128],
                                              mgrp[:, hc * TOK + tb * 512:
                                                   hc * TOK + (tb + 1) * 512],
                                              start=(hc == 0), stop=fin),
                                     inc=("t" if fin else None))
                            if hg == last:
                                P.op("tensor", lambda E, b=bank, tb=tb: E.matmul(
                                    b[:, :], neg128_r[:],
                                    msum[0:1, tb * 512:(tb + 1) * 512],
                                    start=False, stop=True), inc="t")
                            P.wait("vector", "t", P.v("t"))
                            P.op("vector", lambda E, b=bank, dc=dc, tb=tb:
                                 E.scalar_tensor_tensor(
                                     xs(dc, tb * 512, (tb + 1) * 512), b[:, :],
                                     sd_sb[:, dc:dc + 1],
                                     xs(dc, tb * 512, (tb + 1) * 512),
                                     ALU.mult, ALU.add), inc="v")
                            down_guard[tb] = P.v("v")
                        wchunk_done(bd, P.v("t"))
        barrier()

    def sublayer_attn_b():
        with contextlib.ExitStack() as ss:
            SE = ss.enter_context
            q_sb = SE(nc.sbuf_tensor("qb_sb", [128, DC * TOK], BF16))
            PT = SE(nc.sbuf_tensor("PT", [128, 32 * 512], BF16))
            with contextlib.ExitStack() as ps:
                PL = ps.enter_context
                stg = PL(nc.sbuf_tensor("stg", [128, 512], BF16))
                xn = PL(nc.sbuf_tensor("xnb", [128, DC * TOK], BF16))
                pbs = [PL(nc.psum_tensor(f"pbb{i}", [128, 512], F32)) for i in range(8)]
                xb_s = [PL(nc.sbuf_tensor(f"bxb{i}", [128, 512], BF16))
                        for i in range(2)]
                xq_s = [PL(nc.sbuf_tensor(f"bxq{i}", [128, 512], BF16))
                        for i in range(2)]
                st_b = tuple(PL(nc.sbuf_tensor(f"stb{i}", [1, sz], F32))
                             for i, sz in enumerate((TOK, TOK, 512, 512)))
                emit_ln_stats(pbs[0:4], xb_s, xq_s, st_b)
                emit_ln_norm(pbs[0:4], xn, lambda dc, a: dc * TOK + a, 0, TOK, st_b)
                wb = [PL(nc.sbuf_tensor(f"wbb{i}", [128, DC * 128], BF16))
                      for i in range(2)]
                wvb = [PL(nc.sbuf_tensor(f"wvbb{i}", [128, 1024], BF16))
                       for i in range(2)]
                wsems = ["dw0", "dw1"]


                def evac_kb(ec, tb, bank):
                    P.wait("vector", "t", P.v("t"))
                    P.wait("vector", "dg")
                    P.op("vector", lambda E, b=bank: E.tensor_copy(stg[:], b[:, :]),
                         inc="v")
                    P.wait("gpsimd", "v", P.v("v"))
                    P.op("gpsimd", lambda E, ec=ec, tb=tb: E.dma_start(
                        out=kown_d.ap()[ec * 128:(ec + 1) * 128,
                                        tb * 512:(tb + 1) * 512],
                        in_=stg[:]), inc="dg")
                    return ("v", P.v("v"))
                emit_proj_fm(xn, TOK, Wk_b, wb, wsems, list(range(DC)), 0, TOK,
                             pbs[0:2], evac_kb)

                def evac_vb(tc, eb, bank):
                    P.wait("vector", "t", P.v("t"))
                    P.wait("vector", "dg")
                    P.op("vector", lambda E, b=bank: E.tensor_copy(stg[:], b[:, :]),
                         inc="v")
                    P.wait("gpsimd", "v", P.v("v"))
                    P.op("gpsimd", lambda E, tc=tc, eb=eb: E.dma_start(
                        out=vown_d.ap()[tc * 128:(tc + 1) * 128,
                                        eb * 512:(eb + 1) * 512],
                        in_=stg[:]), inc="dg")
                    return ("v", P.v("v"))
                emit_proj_tm(xn, TOK, Wv_b, wvb, 0, list(range(8)), pbs[0:8], None,
                             evac_vb)
                P.wait("gpsimd", "dg")
                P.op("gpsimd", lambda E: E.collective_compute(
                    "AllGather", ALU.bypass,
                    replica_groups=[[0, 1, 2, 3], [4, 5, 6, 7]],
                    ins=[kown_d.ap().opt()], outs=[kfull_d.ap().opt()]), inc="cc")
                ccK = P.v("cc")
                P.op("gpsimd", lambda E: E.collective_compute(
                    "AllGather", ALU.bypass,
                    replica_groups=[[0, 1, 2, 3], [4, 5, 6, 7]],
                    ins=[vown_d.ap().opt()], outs=[vfull_d.ap().opt()]), inc="cc")
                ccV = P.v("cc")

                def evac_qb(ec, tb, bank):
                    P.wait("scalar", "t", P.v("t"))
                    P.op("scalar", lambda E, ec=ec, tb=tb, b=bank: E.activation(
                        q_sb[:, ec * TOK + tb * 512: ec * TOK + (tb + 1) * 512],
                        b[:, :], AF.Identity, scale=QSCALE), inc="s")
                    return ("s", P.v("s"))
                emit_proj_fm(xn, TOK, Wq_b, wb, wsems, list(range(DC)), 0, TOK,
                             pbs[0:2], evac_qb)

            bphase = int(os.environ.get("MK_DEV_BPH", "4"))
            for half in range(2 if bphase >= 4 else 1):
                if bphase < 2:
                    break
                with contextlib.ExitStack() as sp:
                    SL = sp.enter_context
                    scsb = [SL(nc.sbuf_tensor(f"sc{half}{i}", [128, 4096], F32))
                            for i in range(2)]
                    psb = SL(nc.sbuf_tensor(f"ps{half}", [128, 4096], BF16))
                    rowst = SL(nc.sbuf_tensor(f"row{half}", [128, 8], F32))
                    kblk = [SL(nc.sbuf_tensor(f"kb{half}{i}", [128, DC * 512], BF16))
                            for i in range(2)]
                    pb2 = [SL(nc.psum_tensor(f"pbs{half}{i}", [128, 512], F32))
                           for i in range(4)]
                    ptrb2 = [SL(nc.psum_tensor(f"ptrb{half}{i}", [128, 512], BF16))
                             for i in range(2)]
                    kb_wait = [0, 0]
                    ptr_wait = [0] * 8
                    ev_g = [0, 0, 0, 0]
                    P.wait("tensor", "v")
                    P.wait("sync", "cc", ccK)
                    P.wait("sync", "t")
                    P.wait("sync", "v")
                    P.wait("sync", "s")
                    for pr in range(2):
                        qcs = [half * 4 + pr * 2, half * 4 + pr * 2 + 1]
                        for kb in range(8):
                            bi = kb % 2
                            if kb_wait[bi]:
                                P.wait("sync", "t", kb_wait[bi])
                            g, tl = kb // 2, (kb % 2) * 512
                            P.op("sync", lambda E, bi=bi, g=g, tl=tl: E.dma_start(
                                out=kblk[bi][:].rearrange("p (kc m) -> p kc m", kc=DC),
                                in_=kfull_d.ap()[g]
                                .rearrange("(kc p) t -> kc p t", p=128)
                                [:, :, tl:tl + 512].transpose([1, 0, 2])),
                                inc=("ky0", "ky1")[bi])
                            P.wait("tensor", ("ky0", "ky1")[bi])
                            for qi, qc in enumerate(qcs):
                                bk2 = qi * 2 + bi
                                if ev_g[bk2]:
                                    P.wait("tensor", "v", ev_g[bk2])
                                for kc in range(DC):
                                    P.op("tensor",
                                         lambda E, bk2=bk2, qc=qc, kc=kc, bi=bi:
                                         E.matmul(pb2[bk2][:, :],
                                                  q_sb[:, kc * TOK + qc * 128:
                                                       kc * TOK + (qc + 1) * 128],
                                                  kblk[bi][:,
                                                           kc * 512:(kc + 1) * 512],
                                                  start=(kc == 0),
                                                  stop=(kc == DC - 1)),
                                         inc=("t" if kc == DC - 1 else None))
                            kb_wait[bi] = P.v("t")
                            P.wait("vector", "t")
                            for qi in range(2):
                                bk2 = qi * 2 + bi
                                P.op("vector", lambda E, qi=qi, kb=kb, bk2=bk2:
                                     E.tensor_copy(
                                         scsb[qi][:, kb * 512:(kb + 1) * 512],
                                         pb2[bk2][:, :]), inc="v")
                                ev_g[bk2] = P.v("v")
                        vdrain()
                        for qi, qc in enumerate(qcs):
                            qloc = qc % 4
                            P.op("vector", lambda E, qi=qi, qloc=qloc: E.reduce_max(
                                rowst[:, qloc:qloc + 1], scsb[qi][:], axis=AX.X,
                                negate=True), inc="v")
                            vdrain()
                            P.wait("scalar", "v")
                            P.wait("scalar", "t")
                            if os.environ.get("MK_NOSM"):
                                P.op("scalar", lambda E, qi=qi: E.activation(
                                    psb[:], scsb[qi][:], AF.Identity), inc="s")
                            else:
                                P.op("scalar", lambda E, qi=qi, qloc=qloc: E.activation(
                                    psb[:], scsb[qi][:], AF.Exp,
                                    bias=rowst[:, qloc:qloc + 1],
                                    accum_out=rowst[:, 4 + qloc:5 + qloc]), inc="s")
                            P.wait("vector", "s")
                            P.op("vector", lambda E, qloc=qloc: E.reciprocal(
                                rowst[:, 4 + qloc:5 + qloc],
                                rowst[:, 4 + qloc:5 + qloc]), inc="v")
                            vdrain()
                            P.op("vector", lambda E, qloc=qloc: E.tensor_scalar_mul(
                                psb[:], psb[:], rowst[:, 4 + qloc:5 + qloc]),
                                inc="v")
                            vdrain()
                            P.wait("tensor", "v")
                            evacv = ptr_wait
                            for kc in range(0 if os.environ.get("MK_NOTR") else 32):
                                bk = kc % 2
                                sl = (kc // 2) % 4
                                if evacv[(kc - 2) % 8] and kc >= 2:
                                    P.wait("tensor", "v", evacv[(kc - 2) % 8])
                                elif evacv[(kc - 2) % 8]:
                                    P.wait("tensor", "v", evacv[(kc - 2) % 8])
                                P.op("tensor", lambda E, kc=kc, sl=sl, bk=bk:
                                     E.transpose(
                                         ptrb2[bk][:, sl * 128:(sl + 1) * 128],
                                         psb[:, kc * 128:(kc + 1) * 128], ident[:]),
                                     inc="t")
                                P.wait("vector", "t")
                                P.op("vector", lambda E, kc=kc, sl=sl, bk=bk,
                                     qloc=qloc: E.tensor_copy(
                                         PT[:, kc * 512 + qloc * 128:
                                            kc * 512 + (qloc + 1) * 128],
                                         ptrb2[bk][:, sl * 128:(sl + 1) * 128]),
                                     inc="v")
                                evacv[kc % 8] = P.v("v")
                if bphase < 3:
                    continue
                with contextlib.ExitStack() as ap_:
                    AL_ = ap_.enter_context
                    av = AL_(nc.sbuf_tensor(f"av{half}", [128, DC * 512], BF16))
                    vblk = [AL_(nc.sbuf_tensor(f"vb{half}{i}", [128, D], BF16))
                            for i in range(2)]
                    pb8 = [AL_(nc.psum_tensor(f"pav{half}{i}", [128, 512], F32))
                           for i in range(8)]
                    vb_wait = [0, 0]
                    P.wait("tensor", "v")
                    P.wait("sync", "cc", ccV)
                    P.wait("sync", "t")
                    P.wait("sync", "s")
                    P.wait("sync", "v")
                    P.wait("gpsimd", "t")
                    for p8 in range(2):
                        for kc in range(32):
                            bi = kc % 2
                            if vb_wait[bi]:
                                P.wait("sync", "t", vb_wait[bi])
                            g, r = kc // 8, (kc % 8) * 128
                            P.op("sync", lambda E, bi=bi, g=g, r=r: E.dma_start(
                                out=vblk[bi][:], in_=vfull_d.ap()[g, r:r + 128, :]),
                                inc=("vy0", "vy1")[bi])
                            P.wait("tensor", ("vy0", "vy1")[bi])
                            for ei in range(8):
                                ec = p8 * 8 + ei
                                P.op("tensor", lambda E, bi=bi, ec=ec, ei=ei, kc=kc:
                                     E.matmul(pb8[ei][:, :],
                                              vblk[bi][:, ec * 128:(ec + 1) * 128],
                                              PT[:, kc * 512:(kc + 1) * 512],
                                              start=(kc == 0), stop=(kc == 31)),
                                     inc=("t" if ei == 7 else None))
                            vb_wait[bi] = P.v("t")
                        P.wait("vector", "t")
                        for ei in range(8):
                            ec = p8 * 8 + ei
                            P.op("vector", lambda E, ec=ec, ei=ei: E.tensor_copy(
                                av[:, ec * 512:(ec + 1) * 512], pb8[ei][:, :]),
                                inc=("v" if ei == 7 else None))
                        P.wait("tensor", "v")
                    with contextlib.ExitStack() as ows:
                        OL = ows.enter_context
                        wo = [OL(nc.sbuf_tensor(f"wo{half}{i}", [128, DC * 128], BF16))
                              for i in range(2)]
                        for i in range(DC):
                            buf, sem = wo[i % 2], ("dw0", "dw1")[i % 2]
                            if i == 0:
                                load_wchunk(buf, sem, Wo_b, DC, 0, 128)
                            P.wait("tensor", sem)
                            if i + 1 < DC:
                                load_wchunk(wo[(i + 1) % 2],
                                            ("dw0", "dw1")[(i + 1) % 2],
                                            Wo_b, DC, (i + 1) * 128, 128)
                            for kc in range(DC):
                                P.op("tensor", lambda E, buf=buf, kc=kc: E.matmul(
                                    pb8[0][:, :], buf[:, kc * 128:(kc + 1) * 128],
                                    av[:, kc * 512:(kc + 1) * 512],
                                    start=(kc == 0), stop=(kc == DC - 1)),
                                    inc=("t" if kc == DC - 1 else None))
                            P.wait("vector", "t")
                            P.op("vector", lambda E, dc=i, half=half: E.tensor_add(
                                xs(dc, half * 512, (half + 1) * 512),
                                xs(dc, half * 512, (half + 1) * 512),
                                pb8[0][:, :]), inc="v")
                            P.wait("tensor", "v")
                            wchunk_done(buf, P.v("t"))
        barrier()

    sublayer_attn_a()
    if upto >= 2:
        sublayer_ffn("f1")
    if upto >= 3:
        sublayer_attn_b()
    if upto >= 4:
        sublayer_ffn("f2")

    P.wait("sync", "v")
    P.wait("sync", "t")
    P.op("sync", lambda E: E.dma_start(
        out=x_out.rearrange("(dc p) t -> p dc t", p=128),
        in_=x_sb[:].rearrange("p (dc t) -> p dc t", dc=DC)), inc="st")
    P.wait("sync", "st")
    P.wait("gpsimd", "st")

    with nc.Block() as block:
        @block.sync
        def _(e):
            for fn in P.streams["sync"]:
                fn(e)

        @block.gpsimd
        def _(e):
            for fn in P.streams["gpsimd"]:
                fn(e)

        @block.tensor
        def _(e):
            for fn in P.streams["tensor"]:
                fn(e)

        @block.vector
        def _(e):
            for fn in P.streams["vector"]:
                fn(e)

        @block.scalar
        def _(e):
            for fn in P.streams["scalar"]:
                fn(e)

    es.close()
    return nc


def kernel(**inp):
    x = np.asarray(inp["L0_attn_a_x"])
    WgT_h = np.ascontiguousarray(np.asarray(inp["L0_ffn_wq_gate"]).T)
    WuT_h = np.ascontiguousarray(np.asarray(inp["L0_ffn_wq_up"]).T)
    WdT_h = np.ascontiguousarray(np.asarray(inp["L0_ffn_wq_down"]).T)
    shared = {
        "Wq_a": np.asarray(inp["L0_attn_a_Wq"]), "bq_a": np.asarray(inp["L0_attn_a_bq"]),
        "Wk_a": np.asarray(inp["L0_attn_a_Wk"]), "bk_a": np.asarray(inp["L0_attn_a_bk"]),
        "Wv_a": np.asarray(inp["L0_attn_a_Wv"]), "bv_a": np.asarray(inp["L0_attn_a_bv"]),
        "Wal": np.asarray(inp["L0_attn_a_Walpha"]),
        "bal": np.asarray(inp["L0_attn_a_balpha"]),
        "Wbe": np.asarray(inp["L0_attn_a_Wbeta"]),
        "bbe": np.asarray(inp["L0_attn_a_bbeta"]),
        "Wq_b": np.asarray(inp["L0_attn_b_W_q"]), "Wk_b": np.asarray(inp["L0_attn_b_W_k"]),
        "Wv_b": np.asarray(inp["L0_attn_b_W_v"]), "Wo_b": np.asarray(inp["L0_attn_b_W_o"]),
        "WgT": WgT_h, "WuT": WuT_h, "WdT": WdT_h,
        "sg_in": np.asarray(inp["L0_ffn_s_gate"]),
        "su_in": np.asarray(inp["L0_ffn_s_up"]),
        "sd_in": np.asarray(inp["L0_ffn_s_down"]),
    }
    per_core = []
    for c in range(8):
        b, s = divmod(c, 4)
        d = dict(shared)
        d["x_in"] = np.ascontiguousarray(x[b, s * TOK:(s + 1) * TOK, :].T)
        gsel = np.zeros((NH, 4), dtype=np.float32)
        gsel[:, :s] = 1.0
        d["gsel_in"] = gsel
        per_core.append(d)

    upto = int(os.environ.get("MK_DEV_UPTO", "4"))
    nc = build_program(upto)
    res = run_bass_kernel_spmd(nc, per_core, core_ids=list(range(8)))
    out = np.empty((2, 4096, 2048), dtype=np.float32)
    for c in range(8):
        b, s = divmod(c, 4)
        out[b, s * TOK:(s + 1) * TOK, :] = np.asarray(res.results[c]["x_out"]).T
    return out
